# version 21
# baseline (speedup 1.0000x reference)
import numpy as np
import ml_dtypes
import concourse.bass as bass
import concourse.mybir as mybir
from concourse.bass_utils import run_bass_kernel_spmd
from concourse.ap import AP

F32 = mybir.dt.float32
BF16 = mybir.dt.bfloat16
AF = mybir.ActivationFunctionType
ALU = mybir.AluOpType
AXX = mybir.AxisListType.X

NCORE = 8
D = 1024
NF = 8
E = 8
EPS = 1e-5
STG = 256
CH = 128
BF = ml_dtypes.bfloat16

CONSTS = ("iden", "umask8", "idenf", "blko", "hdo", "ones1", "cvec")


class Prog:

    ENGS = ("sync", "gpsimd", "tensor", "vector", "scalar")
    NLANES = 12

    def __init__(self, nc, selfsync=False):
        self.nc = nc
        self.selfsync = selfsync
        if selfsync:
            self.ssem = {e: nc.alloc_semaphore(name="ss_" + e) for e in self.ENGS}
            self.sctr = {e: [0] for e in self.ENGS}
        self.q = {e: [] for e in self.ENGS}
        self.sem = {e: nc.alloc_semaphore(name="s_" + e) for e in self.ENGS}
        self.cnt = {e: 0 for e in self.ENGS}
        self.lane_pool = {"gpsimd": list(range(8)), "sync": list(range(8, 12))}
        self.lane_sem = [nc.alloc_semaphore(name="ln%d" % i)
                         for i in range(self.NLANES)]
        self.lane_cnt = [0] * self.NLANES
        self.lane_rr = {"gpsimd": 0, "sync": 0}
        self.vclock = {e: {} for e in self.ENGS}
        self.res = {}

    def _handle(self, sk):
        return self.sem[sk] if isinstance(sk, str) else self.lane_sem[sk]

    def _wrap(self, eng, engine):
        if not self.selfsync:
            return engine
        sem, ctr = self.ssem[eng], self.sctr[eng]

        class _Ser:
            def __getattr__(_s, nm):
                a = getattr(engine, nm)
                if not callable(a):
                    return a

                def call(*args, **kw):
                    r = a(*args, **kw)
                    if (hasattr(r, "then_inc") and nm != "wait_ge"
                            and not nm.startswith("dma_start")
                            and not nm.startswith("sem_")):
                        ctr[0] += 1
                        r.then_inc(sem, 1)
                        engine.wait_ge(sem, ctr[0])
                    return r
                return call
        return _Ser()

    def _dep_points(self, reads, writes):
        pts = []
        for r in reads:
            e = self.res.get(r)
            if e is None or e["w"] is None:
                raise KeyError(f"read of resource {r!r} with no prior writer")
            pts.append(e["w"])
        for w in writes:
            e = self.res.get(w)
            if e is not None:
                if e["w"] is not None:
                    pts.append(e["w"])
                pts.extend(e["r"])
        return pts

    def _plan(self, eng, pts):
        vc = self.vclock[eng]
        need = {}
        for sk, val, _ in pts:
            if need.get(sk, -1) < val:
                need[sk] = val
        waits = []
        for sk, val in sorted(need.items(), key=str):
            if sk == eng and val <= self.cnt[eng]:
                continue
            if vc.get(sk, 0) >= val:
                continue
            waits.append((self._handle(sk), val))
        for sk, val, pvc in pts:
            for k2, v2 in pvc.items():
                if vc.get(k2, 0) < v2:
                    vc[k2] = v2
            if vc.get(sk, 0) < val:
                vc[sk] = val
        return waits

    def _update_res(self, reads, writes, pt):
        for w in writes:
            self.res[w] = {"w": pt, "r": []}
        for r in reads:
            e = self.res.get(r)
            if e is None:
                e = self.res[r] = {"w": None, "r": []}
            e["r"].append(pt)

    def op(self, eng, fn, reads=(), writes=()):
        pts = self._dep_points(reads, writes)
        waits = self._plan(eng, pts)
        self.cnt[eng] += 1
        myval = self.cnt[eng]
        sem = self.sem[eng]

        def run(engine, waits=waits, fn=fn, eng=eng):
            for h, v in waits:
                engine.wait_ge(h, v)
            last = fn(self._wrap(eng, engine))
            assert last is not None, "op fn must return its last instruction"
            if self.selfsync:
                engine.sem_inc(sem, 1)
            else:
                last.then_inc(sem, 1)

        self.q[eng].append(run)
        pt = (eng, myval, dict(self.vclock[eng]))
        self._update_res(reads, writes, pt)

    def dma(self, eng, fn, n=1, reads=(), writes=()):
        pts = self._dep_points(reads, writes)
        pool = self.lane_pool[eng]
        lane = pool[self.lane_rr[eng]]
        self.lane_rr[eng] = (self.lane_rr[eng] + 1) % len(pool)
        if self.lane_cnt[lane] > 0:
            pts = pts + [(lane, self.lane_cnt[lane], {})]
        waits = self._plan(eng, pts)
        self.lane_cnt[lane] += 16 * n
        target = self.lane_cnt[lane]
        sem = self.lane_sem[lane]

        def run(engine, waits=waits, fn=fn, n=n, eng=eng):
            for h, v in waits:
                engine.wait_ge(h, v)
            insts = fn(self._wrap(eng, engine))
            assert len(insts) == n, f"expected {n} DMAs, got {len(insts)}"
            for i in insts:
                i.then_inc(sem, 16)

        self.q[eng].append(run)
        pt = (lane, target, dict(self.vclock[eng]))
        self._update_res(reads, writes, pt)

    def finalize(self):
        with self.nc.Block() as block:
            final = ([(self.sem[e], self.cnt[e]) for e in self.ENGS
                      if self.cnt[e] > 0] +
                     [(self.lane_sem[i], self.lane_cnt[i])
                      for i in range(self.NLANES) if self.lane_cnt[i] > 0])
            for name in self.ENGS:
                ops = self.q[name]

                def body(engine, ops=ops):
                    for op in ops:
                        op(engine)
                    for s, v in final:
                        engine.wait_ge(s, v)

                getattr(block, name)(body)


def build_nc(T=2048, nseq=2, half_tokens=1024, race=False, tap=False, tapset=None):
    NH = T // half_tokens
    NSTG = half_tokens // STG
    NTC = half_tokens // CH

    nc = bass.Bass("TRN2", target_bir_lowering=False,
                   detect_race_conditions=race)

    x_in = nc.dram_tensor("x", [nseq, T, D], F32, kind="ExternalInput").ap()
    y_out = nc.dram_tensor("y", [nseq, T, D], F32, kind="ExternalOutput").ap()
    d_wT = {nm: nc.dram_tensor("w%sT" % nm, [D, D], F32, kind="ExternalInput").ap()
            for nm in "rkvwo"}
    d_w1 = nc.dram_tensor("w1", [E, D, D], F32, kind="ExternalInput").ap()
    d_w2 = nc.dram_tensor("w2", [E, D, D], F32, kind="ExternalInput").ap()
    d_sgu = nc.dram_tensor("sgu", [D, 2 * D], F32, kind="ExternalInput").ap()
    d_sdn = nc.dram_tensor("sdn", [D, D], F32, kind="ExternalInput").ap()
    d_rtr = nc.dram_tensor("rtr", [D, E], F32, kind="ExternalInput").ap()
    d_sgw = nc.dram_tensor("sgw", [D, 1], F32, kind="ExternalInput").ap()
    d_iden = nc.dram_tensor("idn", [128, 128], BF16, kind="ExternalInput").ap()
    d_um8 = nc.dram_tensor("um8", [128, 128], mybir.dt.uint8, kind="ExternalInput").ap()
    d_idenf = nc.dram_tensor("idf", [128, 128], F32, kind="ExternalInput").ap()
    d_blko = nc.dram_tensor("blk", [128, 128], BF16, kind="ExternalInput").ap()
    d_hdo = nc.dram_tensor("hdo", [128, 128], BF16, kind="ExternalInput").ap()
    d_ones = nc.dram_tensor("on1", [1, 128], BF16, kind="ExternalInput").ap()
    d_cvec = nc.dram_tensor("cvc", [128, 52], F32, kind="ExternalInput").ap()

    A = lambda n, c, dt: nc.alloc_sbuf_tensor(n, [128, c], dt)
    resid = A("resid", NTC * 1024, F32)
    WPC = max(5 * 8192, 8 * 2048 + 3 * 8192 + 64 + 8)
    wpool = A("wpool", WPC, BF16)
    wps = wpool.ap().ap[0][0]

    def wview(off, cols):
        return AP(wpool, off, [[wps, 128], [1, cols]])

    wT = {nm: wview(i * 8192, 8192) for i, nm in enumerate("rkvwo")}
    OFF_SGU, OFF_W1, OFF_W2, OFF_SDN = 0, 8 * 2048, 8 * 2048 + 8192, 8 * 2048 + 2 * 8192
    OFF_RTR, OFF_SGW = 8 * 2048 + 3 * 8192, 8 * 2048 + 3 * 8192 + 64
    sgusb = wview(OFF_SGU, 8 * 2048)
    sdnsb = wview(OFF_SDN, 8192)
    rtrsb = wview(OFF_RTR, 64)
    sgwsb = wview(OFF_SGW, 8)
    R_SGU = ("wt_r", "wt_k")
    R_W1, R_W2, R_SDN = "wt_v", "wt_w", "wt_o"

    iden = A("iden", 128, BF16)
    umask8 = A("umask8", 128, mybir.dt.uint8)
    idenf = A("idenf", 128, F32)
    blko = A("blko", 128, BF16)
    hdo = A("hdo_sb", 128, BF16)
    ones1 = nc.alloc_sbuf_tensor("ones1", [1, 128], BF16)
    cvec = A("cvec", 52, F32)
    lnw64 = A("lnw64", 1, F32)

    xn_tok = A("xn_tok", 2048, BF16)
    xn_fm = A("xn_fm", 2048, BF16)
    xsh_fm = A("xsh_fm", 2048, BF16)
    mixb = A("mixb", 2048, BF16)
    tm1m = A("tm1m", 32, F32)
    r_bf = A("r_bf", 2048, BF16)
    k_bf = A("k_bf", 2048, BF16)
    eu_bf = A("eu_bf", 2048, BF16)
    sc_f32 = A("sc_f32", 2048, F32)
    sc2_f32 = A("sc2_f32", 2048, F32)
    chunk_f32 = A("chunk_f32", 2048, F32)
    expr_bf = A("expr_bf", 1024, BF16)
    rt_bf = A("rt_bf", 1024, BF16)
    kt_bf = A("kt_bf", 1024, BF16)
    kh_bf = A("kh_bf", 1024, BF16)
    khtok = A("khtok", 1024, BF16)
    t2_bf = A("t2_bf", 1024, BF16)
    vtok = A("vtok", 2048, BF16)
    mixtmp = A("mixtmp", 2048, BF16)
    vtil = A("vtil", 1024, BF16)
    mA_bf = A("mA_bf", 1024, BF16)
    mB_bf = A("mB_bf", 1024, BF16)
    S_bf = A("S_bf", 1024, BF16)
    eA_f32 = A("eA_f32", 48, F32)
    mcol_f32 = A("mcol_f32", 96, F32)
    D_sb = A("D_sb", 16, F32)
    o_fm = A("o_fm", 2048, BF16)
    og_fm = A("og_fm", 2048, BF16)
    prevc = A("prevc", 8, BF16)
    stat6 = A("stat6", 24, F32)
    mv = A("mv", 8, F32)
    o2_bf = expr_bf
    r1_bf = rt_bf
    b2_bf = kt_bf
    on_bf = kh_bf
    cmb_all = A("cmb_all", 64, BF16)
    gcol = A("gcol", 8, F32)
    lg_tok = A("lg_tok", 16, F32)
    esm = A("esm", 16, F32)
    top8 = A("top8", 16, F32)
    ssum = A("ssum", 4, F32)
    lgfm = nc.alloc_sbuf_tensor("lgfm", [8, 256], F32)

    pb = [nc.alloc_psum_tensor("pb%d" % i, [128, 512], F32) for i in range(8)]
    pbs = [p.ap().ap[0][0] for p in pb]
    PB = ["pb%d" % i for i in range(8)]

    def pbk(i, c0, n, parts=128):
        return AP(pb[i], c0, [[pbs[i], parts], [1, n]])

    def pbk_bf(i, c0, n, parts=128):
        return AP(pb[i], 0, [[pbs[i], parts], [1, 512]]).bitcast(BF16)[:, c0:c0 + n]

    tapd = {}
    tapn = [0]
    if tap:
        NCHUNK = nseq * T // CH
        _all = (("t_ofm", 1024, BF16), ("t_S", 1024, BF16),
                ("t_kt", 1024, BF16), ("t_mA", 1024, BF16),
                ("t_mB", 1024, BF16), ("t_og", 2048, BF16),
                ("t_var", 1024, F32))
        for nm, cols, dt in _all:
            if tapset is None or nm in tapset:
                tapd[nm] = nc.dram_tensor(nm, [NCHUNK, 128, cols], dt,
                                          kind="ExternalOutput").ap()

    P = Prog(nc, selfsync=race)
    pipe_sem = nc.alloc_semaphore(name="pipe")
    pipe_n = [0]

    def vguard(v, inst):
        if race:
            return
        pipe_n[0] += 1
        inst.then_inc(pipe_sem, 1)
        v.wait_ge(pipe_sem, pipe_n[0])

    def sl(t, c0, n):
        return t.ap()[:, c0:c0 + n]

    def ap3(t, blocks, bstep, inner, off=0):
        base = t.ap()
        return AP(t, off, [[base.ap[0][0], 128], [bstep, blocks], [1, inner]])

    def strided(t, off, step, count):
        base = t.ap()
        return AP(t, off, [[base.ap[0][0], 128], [step, count]])

    def fconsts(g):
        return [g.dma_start(out=iden.ap(), in_=d_iden),
                g.dma_start(out=umask8.ap(), in_=d_um8),
                g.dma_start(out=idenf.ap(), in_=d_idenf),
                g.dma_start(out=blko.ap(), in_=d_blko),
                g.dma_start(out=hdo.ap(), in_=d_hdo),
                g.dma_start(out=ones1.ap(), in_=d_ones),
                g.dma_start(out=cvec.ap(), in_=d_cvec)]
    P.dma("gpsimd", fconsts, n=7, writes=CONSTS)

    P.op("vector", lambda v: v.tensor_scalar_mul(lnw64.ap(), cvec.ap()[:, 40:41], 64.0),
         reads=["cvec"], writes=["lnw64"])
    P.op("vector", lambda v: v.tensor_scalar(tm1m.ap(), cvec.ap()[:, 0:32],
                                             -1.0, 1.0, ALU.mult, ALU.add),
         reads=["cvec"], writes=["tm1m"])

    def load_mat_1024(dst_off, src_ap, src_elem_off=0):
        def f(g):
            ia = AP(src_ap.tensor, src_elem_off, [[1024, 128], [131072, 8], [1, 1024]])
            oa = AP(wpool, dst_off, [[wps, 128], [1024, 8], [1, 1024]])
            return [g.dma_start(out=oa, in_=ia)]
        return f

    REGN = {0: "wt_r", 8192: "wt_k", 16384: "wt_v", 24576: "wt_w",
            32768: "wt_o"}

    def attn_offs(p):
        return {"r": 0 if p == 0 else 32768, "k": 8192, "v": 16384,
                "w": 24576, "o": 32768 if p == 0 else 0}

    def emit_attn_weight_loads(p):
        offs = attn_offs(p)
        for nm in "rkvwo":
            P.dma("gpsimd", load_mat_1024(offs[nm], d_wT[nm]),
                  writes=[REGN[offs[nm]]])

    def emit_ffn_shared_loads():
        def fsgu(g):
            ia = AP(d_sgu.tensor, 0, [[2048, 128], [262144, 8], [1, 2048]])
            oa = AP(wpool, OFF_SGU, [[wps, 128], [2048, 8], [1, 2048]])
            return [g.dma_start(out=oa, in_=ia)]
        P.dma("gpsimd", fsgu, writes=list(R_SGU))
        P.dma("gpsimd", load_mat_1024(OFF_SDN, d_sdn), writes=[R_SDN])

        def frtr(g):
            ia = AP(d_rtr.tensor, 0, [[8, 128], [1024, 8], [1, 8]])
            oa = AP(wpool, OFF_RTR, [[wps, 128], [8, 8], [1, 8]])
            return [g.dma_start(out=oa, in_=ia)]
        P.dma("gpsimd", frtr, writes=["rtr"])

        def fsgw(g):
            ia = AP(d_sgw.tensor, 0, [[1, 128], [128, 8], [1, 1]])
            oa = AP(wpool, OFF_SGW, [[wps, 128], [1, 8]])
            return [g.dma_start(out=oa, in_=ia, allow_slow_non_contiguous=True)]
        P.dma("gpsimd", fsgw, writes=["sgw"])

    def emit_ln_stats(src_ap, src_res, tag):
        s6 = stat6.ap()[:, tag * 12:tag * 12 + 12]
        m_ = lambda i: mv.ap()[:, tag * 4 + i:tag * 4 + i + 1]

        def fv(v):
            i1 = v.bn_stats(s6[:, 0:6], src_ap[:, 0:512])
            i2 = v.bn_stats(s6[:, 6:12], src_ap[:, 512:1024])
            if not race:
                i1.then_inc(pipe_sem, 1)
                i2.then_inc(pipe_sem, 1)
                pipe_n[0] += 2
            v.tensor_copy(m_(2), cvec.ap()[:, 40:41])
            if not race:
                v.wait_ge(pipe_sem, pipe_n[0])
            return v.bn_aggr(mv.ap()[:, tag * 4:tag * 4 + 2], s6)
        P.op("vector", fv, reads=[src_res, "cvec"],
             writes=["stat6:%d" % tag, "mv:%d" % tag])

        def fs(s):
            s.activation(m_(2), m_(1), AF.Ln, bias=cvec.ap()[:, 42:43], scale=1.0)
            return s.activation(m_(2), m_(2), AF.Exp, bias=0.0, scale=cvec.ap()[:, 46:47])
        P.op("scalar", fs, reads=["cvec"], writes=["mv:%d" % tag])

    def emit_ln_norm(src_ap, src_res, dst_ap, tag):
        m_ = lambda i: mv.ap()[:, tag * 4 + i:tag * 4 + i + 1]

        def fv(v):
            return v.tensor_scalar(dst_ap, src_ap, m_(0), m_(2),
                                   ALU.subtract, ALU.mult)
        P.op("vector", fv, reads=[src_res, "mv:%d" % tag], writes=["xn_tok"])

    def emit_ln(src_aps, src_ress):
        for tag in range(2):
            emit_ln_stats(src_aps[tag], src_ress[tag], tag)
        for tag in range(2):
            emit_ln_norm(src_aps[tag], src_ress[tag],
                         sl(xn_tok, tag * 1024, 1024), tag)

    def emit_xn_transposes():
        xst = xn_fm.ap().ap[0][0]
        tst = xn_tok.ap().ap[0][0]
        for tc in range(2):
            def ftr(sy, tc=tc):
                oa = AP(xn_fm, tc * CH, [[xst, 128], [STG, NF], [1, 128]])
                ia = AP(xn_tok, tc * 1024, [[tst, 128], [128, NF], [1, 128]])
                return [sy.dma_start(out=oa, in_=ia, transpose=True)]
            P.dma("sync", ftr, n=1, reads=["xn_tok"], writes=["xn_fm"])

    def emit_proj_fm(w_view, w_res, src_tile, src_res, banks, evac, n=STG):
        def f(te):
            last = None
            for fo in range(NF):
                outap = pbk(banks[fo // 2], (fo % 2) * n, n)
                for dc in range(NF):
                    last = te.matmul(
                        outap,
                        w_view[:, dc * 1024 + fo * 128: dc * 1024 + fo * 128 + 128],
                        sl(src_tile, dc * STG, n),
                        start=(dc == 0), stop=(dc == NF - 1))
            return last
        P.op("tensor", f, reads=[w_res, src_res],
             writes=[PB[b] for b in banks])
        evac(banks, n)

    def scan_banks(tc):
        pA = [0, 1, 2, 3] if tc % 2 == 0 else [4, 5, 6, 7]
        pB = [4, 5, 6, 7] if tc % 2 == 0 else [0, 1, 2, 3]
        return pA, pB, [PB[b_] for b_ in pA], [PB[b_] for b_ in pB]

    def scan_front(tc):
        mo = tc * 48
        eo = tc * 24
        RM, RE = "mcol:%d" % tc, "eA:%d" % tc
        mc = lambda a, b: mcol_f32.ap()[:, mo + a:mo + b]
        eA = eA_f32.ap()[:, eo:eo + 8]

        def fscan(v):
            last = None
            for f in range(NF):
                src = sl(sc_f32, f * STG + tc * CH, CH)
                last = v.tensor_tensor_scan(sl(chunk_f32, f * CH, CH), src, src,
                                            0.0, ALU.add, ALU.bypass)
            return last
        P.op("vector", fscan, reads=["sc_f32"], writes=["chunk_f32"])

        def fmid(v):
            v.tensor_copy(mc(0, 8), strided(chunk_f32, 63, CH, NF))
            v.tensor_scalar(mc(8, 16), strided(chunk_f32, 63, CH, NF),
                            -1.0, -30.0, ALU.mult, ALU.add)
            v.tensor_scalar_add(mc(32, 40), strided(chunk_f32, 63, CH, NF),
                                -30.0)
            return v.tensor_sub(eA, strided(chunk_f32, CH - 1, CH, NF),
                                strided(chunk_f32, 63, CH, NF))
        P.op("vector", fmid, reads=["chunk_f32"], writes=[RM, RE])

        def fmid2(s):
            eAk = eA_f32.ap()[:, eo + 8:eo + 16]
            eAm = eA_f32.ap()[:, eo + 16:eo + 24]
            s.activation(mc(16, 24), mc(0, 8), AF.Exp,
                         bias=cvec.ap()[:, 47:48], scale=1.0)
            s.activation(mc(24, 32), mc(0, 8), AF.Exp,
                         bias=cvec.ap()[:, 48:49], scale=cvec.ap()[:, 45:46])
            s.activation(eAk, eA, AF.Exp,
                         bias=cvec.ap()[:, 48:49], scale=cvec.ap()[:, 45:46])
            s.activation(eAm, eA, AF.Exp,
                         bias=cvec.ap()[:, 47:48], scale=cvec.ap()[:, 45:46])
            return s.activation(eA, eA, AF.Exp, bias=0.0,
                                scale=cvec.ap()[:, 45:46])
        P.op("scalar", fmid2, reads=["cvec"], writes=[RM, RE])

        def fexpr(s):
            last = None
            for f in range(NF):
                last = s.activation(sl(expr_bf, f * CH + 1, CH - 1),
                                    sl(chunk_f32, f * CH, CH - 1),
                                    AF.Exp, bias=mc(32 + f, 33 + f),
                                    scale=cvec.ap()[:, 45:46])
            return last
        P.op("scalar", fexpr, reads=["chunk_f32", RM, "cvec"], writes=["expr_bf"])
        P.op("vector", lambda v: v.tensor_copy(strided(expr_bf, 0, CH, NF),
                                               mc(16, 24)),
             reads=[RM], writes=["expr_bf"])

        def fek(s):
            last = None
            for f in range(NF):
                last = s.activation(sl(chunk_f32, 1024 + f * CH, CH),
                                    sl(chunk_f32, f * CH, CH),
                                    AF.Exp, bias=mc(8 + f, 9 + f), scale=1.0)
            return last
        P.op("scalar", fek, reads=[RM], writes=["chunk_f32"])

    def scan_main(tc):
        pA, pB, PA, PBk = scan_banks(tc)
        mo = tc * 48
        eo = tc * 24
        RM, RE = "mcol:%d" % tc, "eA:%d" % tc
        mc = lambda a, b: mcol_f32.ap()[:, mo + a:mo + b]
        eA = eA_f32.ap()[:, eo:eo + 8]

        def frt(v):
            rsl = ap3(r_bf, NF, STG, CH, off=tc * CH)
            v.tensor_mul(ap3(rt_bf, NF, CH, CH), rsl, ap3(expr_bf, NF, CH, CH))
            v.tensor_mul(ap3(kt_bf, NF, CH, CH), ap3(k_bf, NF, STG, CH, off=tc * CH),
                         ap3(chunk_f32, NF, CH, CH, off=1024))
            return v.tensor_mul(ap3(t2_bf, NF, CH, CH), rsl,
                                ap3(eu_bf, NF, STG, CH, off=tc * CH))
        P.op("vector", frt, reads=["r_bf", "expr_bf", "k_bf", "chunk_f32", "eu_bf"],
             writes=["rt_bf", "kt_bf", "t2_bf"])

        def fkh(v):
            last = None
            for f in range(NF):
                last = v.tensor_scalar_mul(sl(kh_bf, f * CH, CH),
                                           sl(kt_bf, f * CH, CH),
                                           eA_f32.ap()[:, eo + 8 + f:eo + 9 + f])
            return last
        P.op("vector", fkh, reads=["kt_bf", RE], writes=["kh_bf"])

        for rnd in range(2):
            def fkhT(te, rnd=rnd):
                last = None
                for j in range(4):
                    last = te.transpose(
                        AP(pb[pB[j]], 0, [[pbs[pB[j]], 128], [1, 512]])
                        .bitcast(BF16)[:, 0:128],
                        sl(kh_bf, (rnd * 4 + j) * CH, CH), iden.ap())
                return last
            P.op("tensor", fkhT, reads=["kh_bf", "iden"], writes=PBk)

            def fkhE(v, rnd=rnd):
                last = None
                for j in range(4):
                    last = v.tensor_copy(
                        sl(khtok, (rnd * 4 + j) * 128, 128),
                        AP(pb[pB[j]], 0, [[pbs[pB[j]], 128], [1, 512]])
                        .bitcast(BF16)[:, 0:128])
                return last
            P.op("vector", fkhE, reads=PBk, writes=["khtok"])

        def fd(te):
            last = None
            for f in range(NF):
                last = te.matmul(pbk(pB[0], 0, 16), sl(t2_bf, f * CH, CH),
                                 hdo.ap()[:, f * 16:f * 16 + 16],
                                 start=(f == 0), stop=(f == NF - 1))
            return last
        P.op("tensor", fd, reads=["t2_bf", "hdo"], writes=[PBk[0]])
        P.op("vector", lambda v: v.tensor_copy(D_sb.ap(), pbk(pB[0], 0, 16)),
             reads=[PBk[0]], writes=["D_sb"])

        sprime = kh_bf
        sprime2 = t2_bf

        def fsprime(v):
            last = None
            for f in range(NF):
                last = v.tensor_scalar_mul(
                    sl(sprime, f * 128, 128), sl(S_bf, f * 128, 128),
                    mc(24 + f, 25 + f))
            return last
        P.op("vector", fsprime, reads=["S_bf", RM], writes=["kh_bf"])

        def fsprime2(v):
            last = None
            for f in range(NF):
                last = v.tensor_scalar_mul(
                    sl(sprime2, f * 128, 128), sl(sprime, f * 128, 128),
                    eA_f32.ap()[:, eo + 16 + f:eo + 17 + f])
            return last
        P.op("vector", fsprime2, reads=["kh_bf", RE], writes=["t2_bf"])

        def fpall(te):
            last = None
            for f in range(NF):
                te.matmul(pbk(pA[0] + f // 4, (f % 4) * 128, 128),
                          kt_bf.ap()[0:64, f * CH:f * CH + CH],
                          rt_bf.ap()[0:64, f * CH:f * CH + CH],
                          start=True, stop=True)
                last = te.matmul(pbk(pA[2] + f // 4, (f % 4) * 128, 128),
                                 kt_bf.ap()[64:128, f * CH:f * CH + CH],
                                 rt_bf.ap()[64:128, f * CH:f * CH + CH],
                                 start=True, stop=True)
            return last
        P.op("tensor", fpall, reads=["kt_bf", "rt_bf"], writes=PA)

        def fmaskall(v):
            last = None
            for f in range(NF):
                v.select(sl(mA_bf, f * 128, 128), umask8.ap(),
                         pbk(pA[0] + f // 4, (f % 4) * 128, 128), umask8.ap())
                last = v.select(sl(mB_bf, f * 128, 128), umask8.ap(),
                                pbk(pA[2] + f // 4, (f % 4) * 128, 128), umask8.ap())
            E60 = 1.1420073898156842e+26
            vguard(v, last)
            v.tensor_scalar_mul(mA_bf.ap(), mA_bf.ap(), E60)
            return v.tensor_scalar_mul(mB_bf.ap(), mB_bf.ap(), E60)
        P.op("vector", fmaskall, reads=["umask8"] + PA,
             writes=["mA", "mB"])

        def fvtil(s):
            last = None
            for f in range(NF):
                s.activation(sl(vtil, f * 128, 64),
                             sl(vtok, tc * 1024 + f * 128, 64),
                             AF.Copy, bias=0.0,
                             scale=D_sb.ap()[:, 2 * f:2 * f + 1])
                last = s.activation(sl(vtil, f * 128 + 64, 64),
                                    sl(vtok, tc * 1024 + f * 128 + 64, 64),
                                    AF.Copy, bias=0.0,
                                    scale=D_sb.ap()[:, 2 * f + 1:2 * f + 2])
            return last
        P.op("scalar", fvtil, reads=["vtok", "D_sb"], writes=["vtil"])

        def foall(te):
            last = None
            for f in range(NF):
                b_ = pB[0] + f // 4
                c0 = (f % 4) * 128
                oo = pbk(b_, c0, 128)
                ooA = AP(pb[b_], c0, [[pbs[b_], 64], [1, 128]])
                ooB = AP(pb[b_], 64 * pbs[b_] + c0, [[pbs[b_], 64], [1, 128]])
                te.matmul(oo, sl(sprime, f * 128, 128),
                          rt_bf.ap()[:, f * CH:f * CH + CH], start=True, stop=False)
                te.matmul(ooA, sl(vtok, tc * 1024 + f * 128, 64),
                          sl(mA_bf, f * 128, 128), start=False, stop=False)
                te.matmul(ooB, sl(vtok, tc * 1024 + f * 128 + 64, 64),
                          sl(mB_bf, f * 128, 128), start=False, stop=False)
                te.matmul(oo, sl(vtil, f * 128, 128), iden.ap(),
                          start=False, stop=True)
                last = te.matmul(pbk(pB[2] + f // 4, (f % 4) * 128, 128),
                                 sl(khtok, f * 128, 128),
                                 sl(vtok, tc * 1024 + f * 128, 128),
                                 start=True, stop=True)
            return last
        P.op("tensor", foall,
             reads=["kh_bf", "rt_bf", "vtil", "iden", "mA",
                    "mB", "khtok", "vtok"],
             writes=PBk)

        def fseC(s):
            last = None
            for f in range(NF):
                last = s.activation(sl(o_fm, f * STG + tc * CH, CH),
                                    pbk(pB[0] + f // 4, (f % 4) * 128, 128),
                                    AF.Copy, bias=0.0, scale=1.0)
            return last
        P.op("scalar", fseC, reads=PBk[0:2], writes=["o_fm:%d" % tc])

        def fseall(v):
            sst = S_bf.ap().ap[0][0]
            s2t = t2_bf.ap().ap[0][0]
            last = None
            for half in range(2):
                for bi in range(2):
                    b_ = pB[2] + bi
                    f0 = bi * 4
                    co = half * 64
                    po = half * 64 * pbs[b_] + co
                    last = v.tensor_add(
                        AP(S_bf, half * 64 * sst + f0 * 128 + co,
                           [[sst, 64], [128, 4], [1, 64]]),
                        AP(t2_bf, half * 64 * s2t + f0 * 128 + co,
                           [[s2t, 64], [128, 4], [1, 64]]),
                        AP(pb[b_], po, [[pbs[b_], 64], [128, 4], [1, 64]]))
            return last
        P.op("vector", fseall, reads=PBk[2:4] + ["t2_bf"],
             writes=["S_bf"])

    def scan_gn(tc, alt=False):
        pA, pB, PA, PBk = scan_banks(tc)
        gns = mixb.ap().bitcast(F32)[:, 0:1024]
        if alt:
            r1v, b2v = rt_bf.ap(), kt_bf.ap()
            RR = ["rt_bf", "kt_bf"]
        else:
            scb = sc_f32.ap().bitcast(BF16)
            r1v, b2v = scb[:, 0:1024], scb[:, 1024:2048]
            RR = ["sc_f32"]
        RO = "o_fm:%d" % tc

        def fo2(s):
            return s.activation(ap3(mB_bf, NF, CH, CH),
                                ap3(o_fm, NF, STG, CH, off=tc * CH),
                                AF.Square, bias=0.0, scale=1.0)
        P.op("scalar", fo2, reads=[RO], writes=["mB"])

        def fgst(te):
            last = None
            for f in range(NF):
                bank = pA[0] + f // 4
                last = te.matmul(pbk(bank, (f % 4) * 128, 128), blko.ap(),
                                 sl(o_fm, f * STG + tc * CH, CH),
                                 start=(f % 4 == 0), stop=(f % 4 == 3))
            for f in range(NF):
                bank = pA[2] + f // 4
                last = te.matmul(pbk(bank, (f % 4) * 128, 128), blko.ap(),
                                 sl(mB_bf, f * CH, CH),
                                 start=(f % 4 == 0), stop=(f % 4 == 3))
            return last
        P.op("tensor", fgst, reads=["blko", RO, "mB"], writes=PA)

        def fgev(v):
            v.tensor_copy(sl(sc2_f32, 0, 512), pbk(pA[0], 0, 512))
            v.tensor_copy(sl(sc2_f32, 512, 512), pbk(pA[1], 0, 512))
            v.tensor_copy(sl(sc2_f32, 1024, 512), pbk(pA[2], 0, 512))
            return v.tensor_copy(sl(sc2_f32, 1536, 512), pbk(pA[3], 0, 512))
        P.op("vector", fgev, reads=PA, writes=["sc2_f32"])

        def fgsq(v):
            v.tensor_mul(gns, sl(sc2_f32, 0, 1024), sl(sc2_f32, 0, 1024))
            v.scalar_tensor_tensor(sl(sc2_f32, 1024, 1024),
                                   sl(sc2_f32, 1024, 1024), 64.0, gns,
                                   ALU.mult, ALU.subtract)
            return v.tensor_single_scalar(sl(sc2_f32, 1024, 1024),
                                          sl(sc2_f32, 1024, 1024), 0.0, ALU.max)
        P.op("vector", fgsq, reads=["mixb"], writes=["sc2_f32", "mixb"])

        def fgln(s):
            s.activation(sl(sc2_f32, 1024, 1024), sl(sc2_f32, 1024, 1024), AF.Ln,
                         bias=cvec.ap()[:, 43:44], scale=1.0)
            return s.activation(sl(sc2_f32, 1024, 1024), sl(sc2_f32, 1024, 1024),
                                AF.Exp, bias=0.0, scale=cvec.ap()[:, 46:47])
        P.op("scalar", fgln, reads=["cvec"], writes=["sc2_f32"])

        def fr1(v):
            v.tensor_scalar_mul(r1v, sl(sc2_f32, 1024, 1024),
                                lnw64.ap()[:, 0:1])
            v.tensor_mul(gns, sl(sc2_f32, 0, 1024), r1v)
            return v.tensor_scalar(b2v, gns, -1.0 / 64.0,
                                   cvec.ap()[:, 41:42], ALU.mult, ALU.add)
        P.op("vector", fr1, reads=["lnw64", "sc2_f32", "cvec"],
             writes=RR + ["mixb"])

        def fon(v):
            og = ap3(og_fm, NF, STG, CH, off=tc * CH)
            v.tensor_mul(og, ap3(o_fm, NF, STG, CH, off=tc * CH), r1v)
            v.tensor_add(og, og, b2v)
            return v.tensor_mul(og, og,
                                ap3(r_bf, NF, STG, CH, off=tc * CH))
        P.op("vector", fon, reads=[RO, "r_bf"] + RR,
             writes=["og_fm"])

        if tap:
            ci = tapn[0]
            tapn[0] += 1

            _tt = [(nm, t_, cols, rr) for nm, t_, cols, rr in
                   (("t_ofm", o_fm, 1024, "o_fm:%d" % tc),
                    ("t_S", S_bf, 1024, "S_bf"),
                    ("t_kt", kt_bf, 1024, "kt_bf"),
                    ("t_mA", mA_bf, 1024, "mA"),
                    ("t_mB", mB_bf, 1024, "mB"),
                    ("t_og", og_fm, 2048, "og_fm"),
                    ("t_var", sc2_f32, 1024, "sc2_f32")) if nm in tapd]

            def ftap(g, tc=tc, ci=ci, _tt=_tt):
                outs = []
                for nm, t_, cols, _ in _tt:
                    nel = 128 * cols
                    ia = (t_.ap()[:, 0:cols] if nm != "t_var"
                          else t_.ap()[:, 1024:2048])
                    if nm == "t_ofm":
                        ia = ap3(t_, NF, STG, CH, off=tc * CH)
                    oa = AP(tapd[nm].tensor, ci * nel,
                            [[cols, 128], [1, cols]])
                    outs.append(g.dma_start(out=oa, in_=ia))
                return outs
            P.dma("gpsimd", ftap, n=len(_tt),
                  reads=[rr for _, _, _, rr in _tt],
                  writes=["tap%d" % ci])
    def emit_front(b, h, st):
        t0 = h * half_tokens
        st_t0 = t0 + st * STG
        base_tc = st * 2
        RC = ["resid:%d" % (base_tc + tc) for tc in range(2)]

        def fx(g, st_t0=st_t0, base_tc=base_tc, b=b):
            ia = AP(x_in.tensor, b * T * D + st_t0 * D,
                    [[1024, 128], [131072, 2], [1, 1024]])
            oa = AP(resid, base_tc * 1024,
                    [[resid.ap().ap[0][0], 128], [1024, 2], [1, 1024]])
            return [g.dma_start(out=oa, in_=ia)]
        P.dma("gpsimd", fx, writes=RC)

        emit_ln([sl(resid, (base_tc + tc) * 1024, 1024) for tc in range(2)], RC)
        emit_xn_transposes()
        if h == 0 and st == 0:
            P.op("vector", lambda v: v.memset(prevc.ap(), 0.0), writes=["prevc"])

        def fsh2(v):
            v.tensor_copy(ap3(xsh_fm, NF, STG, STG - 1, off=1),
                          ap3(xn_fm, NF, STG, STG - 1, off=0))
            v.tensor_copy(strided(xsh_fm, 0, STG, NF), prevc.ap())
            v.tensor_copy(prevc.ap(), strided(xn_fm, STG - 1, STG, NF))
            return v.tensor_sub(mixtmp.ap(), xn_fm.ap(), xsh_fm.ap())
        P.op("vector", fsh2, reads=["xn_fm"],
             writes=["xsh_fm", "prevc", "mixtmp"])

    def phase_A(b, h, first=False, p=0):
        t0 = h * half_tokens
        offs = attn_offs(p)
        wtv = {nm: wview(offs[nm], 8192) for nm in "rkvwo"}
        rres = {nm: REGN[offs[nm]] for nm in "rkvwo"}
        emit_attn_weight_loads(p)
        if h == 0:
            P.op("vector", lambda v: v.memset(S_bf.ap(), 0.0), writes=["S_bf"])

        if first:
            emit_front(b, h, 0)
        for st in range(NSTG):
            base_tc = st * 2
            RC = ["resid:%d" % (base_tc + tc) for tc in range(2)]

            cvst = cvec.ap().ap[0][0]

            def emit_mix(coff, dst, dstres):
                def fmix(v):
                    last = None
                    for f in range(NF):
                        last = v.scalar_tensor_tensor(
                            sl(dst, f * STG, STG), sl(mixtmp, f * STG, STG),
                            cvec.ap()[:, coff + f:coff + f + 1],
                            sl(xsh_fm, f * STG, STG), ALU.mult, ALU.add)
                    return last
                P.op("vector", fmix, reads=["mixtmp", "cvec", "xsh_fm"],
                     writes=[dstres])

            def evac_r(banks, n):
                def fs(s):
                    last = None
                    for fo in range(NF):
                        last = s.activation(sl(r_bf, fo * STG, n),
                                            pbk(banks[fo // 2], (fo % 2) * n, n),
                                            AF.Sigmoid, bias=0.0, scale=1.0)
                    return last
                P.op("scalar", fs, reads=[PB[b_] for b_ in banks], writes=["r_bf"])
            emit_mix(0, mixb, "mixb")
            emit_proj_fm(wtv["r"], rres["r"], mixb, "mixb", [0, 1, 2, 3], evac_r)

            def evac_k(banks, n):
                def fv(v):
                    last = None
                    for fo in range(NF):
                        last = v.tensor_copy(sl(k_bf, fo * STG, n),
                                             pbk(banks[fo // 2], (fo % 2) * n, n))
                    return last
                P.op("vector", fv, reads=[PB[b_] for b_ in banks], writes=["k_bf"])

                def fs(s):
                    last = None
                    for fo in range(NF):
                        last = s.activation(sl(eu_bf, fo * STG, n),
                                            pbk(banks[fo // 2], (fo % 2) * n, n),
                                            AF.Exp,
                                            bias=cvec.ap()[:, 32 + fo:32 + fo + 1],
                                            scale=1.0)
                    return last
                P.op("scalar", fs, reads=[PB[b_] for b_ in banks] + ["cvec"],
                     writes=["eu_bf"])
            emit_mix(8, og_fm, "og_fm")
            emit_proj_fm(wtv["k"], rres["k"], og_fm, "og_fm", [4, 5, 6, 7], evac_k)

            def evac_w(banks, n):
                def fs(s):
                    last = None
                    for fo in range(NF):
                        last = s.activation(sl(sc_f32, fo * n, n),
                                            pbk(banks[fo // 2], (fo % 2) * n, n),
                                            AF.Exp, bias=0.0, scale=cvec.ap()[:, 45:46])
                    return last
                P.op("scalar", fs, reads=[PB[b_] for b_ in banks] + ["cvec"],
                     writes=["sc_f32"])
                P.op("scalar", lambda s: s.activation(
                    sc_f32.ap(), sc_f32.ap(), AF.Ln, bias=cvec.ap()[:, 44:45],
                    scale=1.0), reads=["cvec"], writes=["sc_f32"])
                P.op("vector", lambda v: v.tensor_scalar_add(
                    sc_f32.ap(), sc_f32.ap(), 1e-4), writes=["sc_f32"])
            emit_mix(24, mixb, "mixb")
            emit_proj_fm(wtv["w"], rres["w"], mixb, "mixb", [0, 1, 2, 3], evac_w)

            emit_mix(16, og_fm, "og_fm")

            def fvproj(te):
                last = None
                for tc in range(2):
                    for hlf in range(2):
                        outap = pbk(4 + tc * 2 + hlf, 0, 512)
                        for dc in range(NF):
                            last = te.matmul(
                                outap, sl(og_fm, dc * STG + tc * CH, 128),
                                wtv["v"][:, dc * 1024 + hlf * 512: dc * 1024 + hlf * 512 + 512],
                                start=(dc == 0), stop=(dc == NF - 1))
                return last
            P.op("tensor", fvproj, reads=["og_fm", rres["v"]],
                 writes=[PB[4], PB[5], PB[6], PB[7]])

            def fvevac(s):
                last = None
                for tc in range(2):
                    for hlf in range(2):
                        last = s.activation(sl(vtok, tc * 1024 + hlf * 512, 512),
                                            pbk(4 + tc * 2 + hlf, 0, 512),
                                            AF.Copy, bias=0.0, scale=1.0)
                return last
            P.op("scalar", fvevac, reads=[PB[4], PB[5], PB[6], PB[7]],
                 writes=["vtok"])

            scan_front(0)
            scan_main(0)
            scan_front(1)
            scan_gn(0)
            scan_main(1)
            if st + 1 < NSTG:
                emit_front(b, h, st + 1)
            scan_gn(1, alt=(st == NSTG - 1))

            def foproj(te):
                last = None
                for tc in range(2):
                    for hlf in range(2):
                        outap = pbk(tc * 2 + hlf, 0, 512)
                        for dc in range(NF):
                            last = te.matmul(
                                outap,
                                sl(og_fm, dc * STG + tc * CH, 128),
                                wtv["o"][:, dc * 1024 + hlf * 512: dc * 1024 + hlf * 512 + 512],
                                start=(dc == 0), stop=(dc == NF - 1))
                return last
            P.op("tensor", foproj, reads=[rres["o"], "og_fm"], writes=PB[0:4])

            def fradd(v, base_tc=base_tc):
                last = None
                for tc in range(2):
                    for hlf in range(2):
                        c0 = (base_tc + tc) * 1024 + hlf * 512
                        last = v.tensor_add(sl(resid, c0, 512), sl(resid, c0, 512),
                                            pbk(tc * 2 + hlf, 0, 512))
                return last
            P.op("vector", fradd, reads=PB[0:4] + RC, writes=RC)

    XN2H = [(sc_f32, 0), (sc_f32, 2048), (chunk_f32, 0), (chunk_f32, 2048)]
    XN2R = ["sc_f32", "sc_f32", "chunk_f32", "chunk_f32"]

    def xn2v(st):
        t, c0 = XN2H[st]
        a = t.ap() if t.ap().dtype == BF16 else t.ap().bitcast(BF16)
        return a[:, c0:c0 + 2048]

    def expert_slots(e):
        if e % 2 == 0:
            return OFF_W1, OFF_W2, R_W1, R_W2
        return 0, 8192, "wt_r", "wt_k"

    def emit_expert_load(e):
        off1, off2, r1, r2 = expert_slots(e)
        P.dma("gpsimd", load_mat_1024(off1, d_w1, src_elem_off=e * D * D),
              writes=[r1])
        P.dma("gpsimd", load_mat_1024(off2, d_w2, src_elem_off=e * D * D),
              writes=[r2])

    def phase_B_router(st):
        xn2s = xn2v(st)

        def frt(te):
            last = None
            for tc2 in range(2):
                for dc in range(NF):
                    last = te.matmul(
                        pbk(1 + tc2, 0, 8),
                        xn2s[:, dc * STG + tc2 * CH:dc * STG + tc2 * CH + 128],
                        rtrsb[:, dc * 8:dc * 8 + 8],
                        start=(dc == 0), stop=(dc == NF - 1))
            return last
        P.op("tensor", frt, reads=["rtr", XN2R[st]], writes=[PB[1], PB[2]])

        def frte(v):
            v.tensor_copy(lg_tok.ap()[:, 0:8], pbk(1, 0, 8))
            return v.tensor_copy(lg_tok.ap()[:, 8:16], pbk(2, 0, 8))
        P.op("vector", frte, reads=[PB[1], PB[2]], writes=["lg_tok"])

        def fmax(v):
            v.reduce_max(ssum.ap()[:, 0:1], lg_tok.ap()[:, 0:8], AXX, negate=True)
            return v.reduce_max(ssum.ap()[:, 2:3], lg_tok.ap()[:, 8:16], AXX,
                                negate=True)
        P.op("vector", fmax, reads=["lg_tok"], writes=["ssum"])

        def fex(s):
            s.activation(esm.ap()[:, 0:8], lg_tok.ap()[:, 0:8], AF.Exp,
                         bias=ssum.ap()[:, 0:1], scale=1.0,
                         accum_out=ssum.ap()[:, 1:2])
            return s.activation(esm.ap()[:, 8:16], lg_tok.ap()[:, 8:16], AF.Exp,
                                bias=ssum.ap()[:, 2:3], scale=1.0,
                                accum_out=ssum.ap()[:, 3:4])
        P.op("scalar", fex, reads=["lg_tok"], writes=["esm", "ssum"])

        def ftp(v, st=st):
            ea = esm.ap()[:, 0:8]
            eb = esm.ap()[:, 8:16]
            ta = top8.ap()[:, 0:8]
            tb = top8.ap()[:, 8:16]
            ga = lg_tok.ap()[:, 0:8]
            gb = lg_tok.ap()[:, 8:16]
            v.max(ta, ea)
            vguard(v, v.max(tb, eb))
            v.reciprocal(ssum.ap()[:, 0:1], ssum.ap()[:, 1:2])
            v.reciprocal(ssum.ap()[:, 2:3], ssum.ap()[:, 3:4])
            v.tensor_single_scalar(ga, ea, top8.ap()[:, 1:2], ALU.is_ge)
            vguard(v, v.tensor_single_scalar(gb, eb, top8.ap()[:, 9:10],
                                             ALU.is_ge))
            v.tensor_mul(ga, ga, ea)
            vguard(v, v.tensor_mul(gb, gb, eb))
            v.tensor_scalar_mul(cmb_all.ap()[:, st * 16:st * 16 + 8], ga,
                                ssum.ap()[:, 0:1])
            return v.tensor_scalar_mul(cmb_all.ap()[:, st * 16 + 8:st * 16 + 16],
                                       gb, ssum.ap()[:, 2:3])
        P.op("vector", ftp, reads=["esm"],
             writes=["lg_tok", "top8", "ssum", "cmb_all"])

    def phase_B_front(b, h, st):
        base_tc = st * 2
        RC = ["resid:%d" % (base_tc + tc) for tc in range(2)]
        emit_ln([sl(resid, (base_tc + tc) * 1024, 1024) for tc in range(2)], RC)
        host, c0 = XN2H[st]
        hst = host.ap().ap[0][0]
        tst = xn_tok.ap().ap[0][0]
        for tc in range(2):
            def ftr(sy, tc=tc):
                oa = AP(host, c0 // 2 + tc * 64,
                        [[hst, 128], [128, NF], [1, 64]]).bitcast(BF16)
                ia = AP(xn_tok, tc * 1024, [[tst, 128], [128, NF], [1, 128]])
                return [sy.dma_start(out=oa, in_=ia, transpose=True)]
            P.dma("sync", ftr, n=1, reads=["xn_tok"], writes=[XN2R[st]])

    def phase_B_stage(b, h, st):
        base_tc = st * 2
        RC = ["resid:%d" % (base_tc + tc) for tc in range(2)]
        xn2s = xn2v(st)

        def fgc(te):
            last = None
            for tc2 in range(2):
                for dc in range(NF):
                    last = te.matmul(
                        pbk(5, tc2 * 256, 1),
                        xn2s[:, dc * STG + tc2 * CH:dc * STG + tc2 * CH + 128],
                        sgwsb[:, dc:dc + 1],
                        start=(dc == 0), stop=(dc == NF - 1))
            return last
        P.op("tensor", fgc, reads=["sgw", XN2R[st]], writes=[PB[5]])

        def fgsig(s, st=st):
            s.activation(gcol.ap()[:, st * 2:st * 2 + 1], pbk(5, 0, 1),
                         AF.Sigmoid, bias=0.0, scale=1.0)
            return s.activation(gcol.ap()[:, st * 2 + 1:st * 2 + 2],
                                pbk(5, 256, 1), AF.Sigmoid, bias=0.0, scale=1.0)
        P.op("scalar", fgsig, reads=[PB[5]], writes=["gcol"])

        hb, hbr = (eu_bf, "eu_bf") if st % 2 == 0 else (xn_fm, "xn_fm")
        tb, tbr = (mixb, "mixb") if st % 2 == 0 else (og_fm, "og_fm")

        def fgu(te):
            last = None
            for f2 in range(16):
                outap = pbk(f2 // 2, (f2 % 2) * 256, 256)
                for dc in range(NF):
                    last = te.matmul(
                        outap,
                        sgusb[:, dc * 2048 + f2 * 128: dc * 2048 + f2 * 128 + 128],
                        xn2s[:, dc * STG:dc * STG + STG],
                        start=(f2 % 2 == 0 and dc == 0),
                        stop=(f2 % 2 == 1 and dc == NF - 1))
            return last
        P.op("tensor", fgu, reads=list(R_SGU) + [XN2R[st]], writes=PB[:])

        def fsg(s):
            last = None
            for f in range(NF):
                last = s.activation(sl(hb, f * STG, STG),
                                    pbk(f // 2, (f % 2) * 256, 256),
                                    AF.Silu, bias=0.0, scale=1.0)
            return last
        P.op("scalar", fsg, reads=PB[0:4], writes=[hbr])

        def fup(v):
            last = None
            for f in range(NF):
                last = v.tensor_mul(sl(tb, f * STG, STG), sl(hb, f * STG, STG),
                                    pbk(4 + f // 2, (f % 2) * 256, 256))
            return last
        P.op("vector", fup, reads=[hbr] + PB[4:8], writes=[tbr])

        def fsdn(te):
            last = None
            for tc2 in range(2):
                for hlf in range(2):
                    outap = pbk(tc2 * 2 + hlf, 0, 512)
                    for fc in range(NF):
                        last = te.matmul(
                            outap, sl(tb, fc * STG + tc2 * CH, 128),
                            sdnsb[:, fc * 1024 + hlf * 512: fc * 1024 + hlf * 512 + 512],
                            start=(fc == 0), stop=(fc == NF - 1))
            return last
        P.op("tensor", fsdn, reads=[R_SDN, tbr], writes=PB[0:4])

        def fsadd(v, st=st):
            last = None
            for tc2 in range(2):
                for hlf in range(2):
                    c0 = (base_tc + tc2) * 1024 + hlf * 512
                    last = v.scalar_tensor_tensor(
                        sl(resid, c0, 512), pbk(tc2 * 2 + hlf, 0, 512),
                        gcol.ap()[:, st * 2 + tc2:st * 2 + tc2 + 1],
                        sl(resid, c0, 512), ALU.mult, ALU.add)
            return last
        P.op("vector", fsadd, reads=PB[0:4] + RC + ["gcol"], writes=RC)

    def phase_B_expert(e, b=0, h=0, nxt=None):
        off1, off2, r1, r2 = expert_slots(e)
        final = (e == E - 1)
        w1v = wview(off1, 8192)
        w2v = wview(off2, 8192)
        for st in range(NSTG):
            xn2s = xn2v(st)
            base_tc = st * 2
            RC = ["resid:%d" % (base_tc + tc) for tc in range(2)]
            hb, hbr = (eu_bf, "eu_bf") if st % 2 == 0 else (xn_fm, "xn_fm")

            def fw1(te, st=st, xn2s=xn2s, w1v=w1v):
                last = None
                for fc in range(NF):
                    outap = pbk(4 + fc // 2, (fc % 2) * 256, 256)
                    for dc in range(NF):
                        last = te.matmul(
                            outap,
                            w1v[:, dc * 1024 + fc * 128: dc * 1024 + fc * 128 + 128],
                            xn2s[:, dc * STG:dc * STG + STG],
                            start=(fc % 2 == 0 and dc == 0),
                            stop=(fc % 2 == 1 and dc == NF - 1))
                return last
            P.op("tensor", fw1, reads=[r1, XN2R[st]], writes=PB[4:8])

            def fsilu(s, hb=hb):
                last = None
                for fc in range(NF):
                    last = s.activation(sl(hb, fc * STG, STG),
                                        pbk(4 + fc // 2, (fc % 2) * 256, 256),
                                        AF.Silu, bias=0.0, scale=1.0)
                return last
            P.op("scalar", fsilu, reads=PB[4:8], writes=[hbr])

            def fw2(te, w2v=w2v, hb=hb):
                last = None
                for tc2 in range(2):
                    for hlf in range(2):
                        outap = pbk(tc2 * 2 + hlf, 0, 512)
                        for fc in range(NF):
                            last = te.matmul(
                                outap, sl(hb, fc * STG + tc2 * CH, 128),
                                w2v[:, fc * 1024 + hlf * 512: fc * 1024 + hlf * 512 + 512],
                                start=(fc == 0), stop=(fc == NF - 1))
                return last
            P.op("tensor", fw2, reads=[r2, hbr], writes=PB[0:4])

            def fadd(v, base_tc=base_tc, e=e, st=st):
                last = None
                for tc2 in range(2):
                    for hlf in range(2):
                        c0 = (base_tc + tc2) * 1024 + hlf * 512
                        last = v.scalar_tensor_tensor(
                            sl(resid, c0, 512), pbk(tc2 * 2 + hlf, 0, 512),
                            cmb_all.ap()[:, st * 16 + tc2 * 8 + e:
                                         st * 16 + tc2 * 8 + e + 1],
                            sl(resid, c0, 512), ALU.mult, ALU.add)
                return last
            P.op("vector", fadd, reads=PB[0:4] + RC + ["cmb_all"], writes=RC)

            if final:
                t0 = h * half_tokens

                def fst(g, st=st, t0=t0, b=b):
                    st_t0 = t0 + st * STG
                    oa = AP(y_out.tensor, b * T * D + st_t0 * D,
                            [[1024, 128], [131072, 2], [1, 1024]])
                    ia = AP(resid, st * 2 * 1024,
                            [[resid.ap().ap[0][0], 128], [1024, 2], [1, 1024]])
                    return [g.dma_start(out=oa, in_=ia)]
                P.dma("gpsimd", fst, reads=RC,
                      writes=["y:%d:%d:%d" % (b, h, st)])
                if st == 0 and nxt is not None:
                    emit_front(nxt[0], nxt[1], 0)

    def phase_B(b, h, nxt=None):
        emit_ffn_shared_loads()
        emit_expert_load(0)
        for st in range(NSTG):
            phase_B_front(b, h, st)
        for st in range(NSTG):
            phase_B_stage(b, h, st)
        for st in range(NSTG):
            phase_B_router(st)
        for e in range(E):
            if e + 1 < E:
                emit_expert_load(e + 1)
            phase_B_expert(e, b=b, h=h, nxt=nxt)

    bh = [(b, h) for b in range(nseq) for h in range(NH)]
    for i, (b, h) in enumerate(bh):
        phase_A(b, h, first=(i == 0), p=i % 2)
        phase_B(b, h, nxt=(bh[i + 1] if i + 1 < len(bh) else None))

    P.finalize()
    return nc


def make_host_inputs(inp):
    tT = lambda a: np.ascontiguousarray(np.asarray(a).T)
    cN = np.ascontiguousarray
    iden = np.eye(128, dtype=BF)
    umask = np.triu(np.ones((128, 128), dtype=np.float32), 1)
    blko = np.zeros((128, 128), dtype=np.float32)
    blko[0:64, 0:64] = 1.0
    blko[64:128, 64:128] = 1.0
    hdo = np.zeros((128, 128), dtype=np.float32)
    for f in range(8):
        hdo[0:64, f * 16 + 2 * f] = 1.0
        hdo[64:128, f * 16 + 2 * f + 1] = 1.0
    cvec = np.zeros((128, 52), dtype=np.float32)
    cvec[:, 47] = -30.0
    cvec[:, 48] = 30.0
    for i, nm in enumerate(("tm_r", "tm_k", "tm_v", "tm_w")):
        cvec[:, i * 8:(i + 1) * 8] = np.asarray(inp["L0_attn_" + nm]).reshape(8, 128).T
    cvec[:, 32:40] = np.asarray(inp["L0_attn_u"]).reshape(-1).reshape(8, 128).T
    cvec[:, 40] = np.tile(np.asarray(inp["L0_attn_ln_w"]), 2)
    cvec[:, 41] = np.tile(np.asarray(inp["L0_attn_ln_b"]), 2)
    cvec[:, 42] = 1e-5
    cvec[:, 43] = 4096.0 * 1e-5
    cvec[:, 44] = 1.0
    cvec[:, 45] = -1.0
    cvec[:, 46] = -0.5
    shared = {
        "wrT": tT(inp["L0_attn_W_r"]), "wkT": tT(inp["L0_attn_W_k"]),
        "wvT": tT(inp["L0_attn_W_v"]), "wwT": tT(inp["L0_attn_W_w"]),
        "woT": tT(inp["L0_attn_W_o"]),
        "w1": cN(np.asarray(inp["L0_ffn_w1"])), "w2": cN(np.asarray(inp["L0_ffn_w2"])),
        "sgu": cN(np.asarray(inp["L0_ffn_shared_gate_up"])),
        "sdn": cN(np.asarray(inp["L0_ffn_shared_down"])),
        "rtr": cN(np.asarray(inp["L0_ffn_router_w"])),
        "sgw": cN(np.asarray(inp["L0_ffn_shared_gate_w"])),
        "idn": iden, "um8": umask.astype(np.uint8),
        "idf": np.eye(128, dtype=np.float32), "blk": blko.astype(BF), "hdo": hdo.astype(BF),
        "on1": np.ones((1, 128), dtype=BF), "cvc": cvec,
    }
    x = np.asarray(inp["L0_attn_x"])
    xs = [cN(x[c * 2:(c + 1) * 2]) for c in range(NCORE)]
    return shared, xs


def kernel(**inputs):
    shared, xs = make_host_inputs(inputs)
    nc = build_nc()
    in_maps = [dict(shared, x=xs[c]) for c in range(NCORE)]
    res = run_bass_kernel_spmd(nc, in_maps, core_ids=list(range(NCORE)))
    out = np.concatenate([res.results[c]["y"] for c in range(NCORE)], axis=0)
    return np.ascontiguousarray(out.astype(np.float32))


# version 22
# speedup vs baseline: 1.0215x; 1.0215x over previous
import numpy as np
import ml_dtypes
import concourse.bass as bass
import concourse.mybir as mybir
from concourse.bass_utils import run_bass_kernel_spmd
from concourse.ap import AP

F32 = mybir.dt.float32
BF16 = mybir.dt.bfloat16
AF = mybir.ActivationFunctionType
ALU = mybir.AluOpType
AXX = mybir.AxisListType.X

NCORE = 8
D = 1024
NF = 8
E = 8
EPS = 1e-5
STG = 256
CH = 128
BF = ml_dtypes.bfloat16

CONSTS = ("iden", "umask8", "idenf", "blko", "hdo", "ones1", "cvec")


class Prog:

    ENGS = ("sync", "gpsimd", "tensor", "vector", "scalar")
    NLANES = 12

    def __init__(self, nc, selfsync=False):
        self.nc = nc
        self.selfsync = selfsync
        if selfsync:
            self.ssem = {e: nc.alloc_semaphore(name="ss_" + e) for e in self.ENGS}
            self.sctr = {e: [0] for e in self.ENGS}
        self.q = {e: [] for e in self.ENGS}
        self.sem = {e: nc.alloc_semaphore(name="s_" + e) for e in self.ENGS}
        self.cnt = {e: 0 for e in self.ENGS}
        self.lane_pool = {"gpsimd": list(range(8)), "sync": list(range(8, 12))}
        self.lane_sem = [nc.alloc_semaphore(name="ln%d" % i)
                         for i in range(self.NLANES)]
        self.lane_cnt = [0] * self.NLANES
        self.lane_rr = {"gpsimd": 0, "sync": 0}
        self.vclock = {e: {} for e in self.ENGS}
        self.res = {}

    def _handle(self, sk):
        return self.sem[sk] if isinstance(sk, str) else self.lane_sem[sk]

    def _wrap(self, eng, engine):
        if not self.selfsync:
            return engine
        sem, ctr = self.ssem[eng], self.sctr[eng]

        class _Ser:
            def __getattr__(_s, nm):
                a = getattr(engine, nm)
                if not callable(a):
                    return a

                def call(*args, **kw):
                    r = a(*args, **kw)
                    if (hasattr(r, "then_inc") and nm != "wait_ge"
                            and not nm.startswith("dma_start")
                            and not nm.startswith("sem_")):
                        ctr[0] += 1
                        r.then_inc(sem, 1)
                        engine.wait_ge(sem, ctr[0])
                    return r
                return call
        return _Ser()

    def _dep_points(self, reads, writes):
        pts = []
        for r in reads:
            e = self.res.get(r)
            if e is None or e["w"] is None:
                raise KeyError(f"read of resource {r!r} with no prior writer")
            pts.append(e["w"])
        for w in writes:
            e = self.res.get(w)
            if e is not None:
                if e["w"] is not None:
                    pts.append(e["w"])
                pts.extend(e["r"])
        return pts

    def _plan(self, eng, pts):
        vc = self.vclock[eng]
        need = {}
        for sk, val, _ in pts:
            if need.get(sk, -1) < val:
                need[sk] = val
        waits = []
        for sk, val in sorted(need.items(), key=str):
            if sk == eng and val <= self.cnt[eng]:
                continue
            if vc.get(sk, 0) >= val:
                continue
            waits.append((self._handle(sk), val))
        for sk, val, pvc in pts:
            for k2, v2 in pvc.items():
                if vc.get(k2, 0) < v2:
                    vc[k2] = v2
            if vc.get(sk, 0) < val:
                vc[sk] = val
        return waits

    def _update_res(self, reads, writes, pt):
        for w in writes:
            self.res[w] = {"w": pt, "r": []}
        for r in reads:
            e = self.res.get(r)
            if e is None:
                e = self.res[r] = {"w": None, "r": []}
            e["r"].append(pt)

    def op(self, eng, fn, reads=(), writes=()):
        pts = self._dep_points(reads, writes)
        waits = self._plan(eng, pts)
        self.cnt[eng] += 1
        myval = self.cnt[eng]
        sem = self.sem[eng]

        def run(engine, waits=waits, fn=fn, eng=eng):
            for h, v in waits:
                engine.wait_ge(h, v)
            last = fn(self._wrap(eng, engine))
            assert last is not None, "op fn must return its last instruction"
            if self.selfsync:
                engine.sem_inc(sem, 1)
            else:
                last.then_inc(sem, 1)

        self.q[eng].append(run)
        pt = (eng, myval, dict(self.vclock[eng]))
        self._update_res(reads, writes, pt)

    def dma(self, eng, fn, n=1, reads=(), writes=()):
        pts = self._dep_points(reads, writes)
        pool = self.lane_pool[eng]
        lane = pool[self.lane_rr[eng]]
        self.lane_rr[eng] = (self.lane_rr[eng] + 1) % len(pool)
        if self.lane_cnt[lane] > 0:
            pts = pts + [(lane, self.lane_cnt[lane], {})]
        waits = self._plan(eng, pts)
        self.lane_cnt[lane] += 16 * n
        target = self.lane_cnt[lane]
        sem = self.lane_sem[lane]

        def run(engine, waits=waits, fn=fn, n=n, eng=eng):
            for h, v in waits:
                engine.wait_ge(h, v)
            insts = fn(self._wrap(eng, engine))
            assert len(insts) == n, f"expected {n} DMAs, got {len(insts)}"
            for i in insts:
                i.then_inc(sem, 16)

        self.q[eng].append(run)
        pt = (lane, target, dict(self.vclock[eng]))
        self._update_res(reads, writes, pt)

    def finalize(self):
        with self.nc.Block() as block:
            final = ([(self.sem[e], self.cnt[e]) for e in self.ENGS
                      if self.cnt[e] > 0] +
                     [(self.lane_sem[i], self.lane_cnt[i])
                      for i in range(self.NLANES) if self.lane_cnt[i] > 0])
            for name in self.ENGS:
                ops = self.q[name]

                def body(engine, ops=ops):
                    for op in ops:
                        op(engine)
                    for s, v in final:
                        engine.wait_ge(s, v)

                getattr(block, name)(body)


def build_nc(T=2048, nseq=2, half_tokens=1024, race=False, tap=False, tapset=None):
    NH = T // half_tokens
    NSTG = half_tokens // STG
    NTC = half_tokens // CH

    nc = bass.Bass("TRN2", target_bir_lowering=False,
                   detect_race_conditions=race)

    x_in = nc.dram_tensor("x", [nseq, T, D], F32, kind="ExternalInput").ap()
    y_out = nc.dram_tensor("y", [nseq, T, D], F32, kind="ExternalOutput").ap()
    d_wT = {nm: nc.dram_tensor("w%sT" % nm, [D, D], F32, kind="ExternalInput").ap()
            for nm in "rkvwo"}
    d_w1 = nc.dram_tensor("w1", [E, D, D], F32, kind="ExternalInput").ap()
    d_w2 = nc.dram_tensor("w2", [E, D, D], F32, kind="ExternalInput").ap()
    d_sgu = nc.dram_tensor("sgu", [D, 2 * D], F32, kind="ExternalInput").ap()
    d_sdn = nc.dram_tensor("sdn", [D, D], F32, kind="ExternalInput").ap()
    d_rtr = nc.dram_tensor("rtr", [D, E], F32, kind="ExternalInput").ap()
    d_sgw = nc.dram_tensor("sgw", [D, 1], F32, kind="ExternalInput").ap()
    d_iden = nc.dram_tensor("idn", [128, 128], BF16, kind="ExternalInput").ap()
    d_um8 = nc.dram_tensor("um8", [128, 128], mybir.dt.uint8, kind="ExternalInput").ap()
    d_idenf = nc.dram_tensor("idf", [128, 128], F32, kind="ExternalInput").ap()
    d_blko = nc.dram_tensor("blk", [128, 128], BF16, kind="ExternalInput").ap()
    d_hdo = nc.dram_tensor("hdo", [128, 128], BF16, kind="ExternalInput").ap()
    d_ones = nc.dram_tensor("on1", [1, 128], BF16, kind="ExternalInput").ap()
    d_cvec = nc.dram_tensor("cvc", [128, 52], F32, kind="ExternalInput").ap()

    A = lambda n, c, dt: nc.alloc_sbuf_tensor(n, [128, c], dt)
    resid = A("resid", NTC * 1024, F32)
    WPC = max(5 * 8192, 8 * 2048 + 3 * 8192 + 64 + 8)
    wpool = A("wpool", WPC, BF16)
    wps = wpool.ap().ap[0][0]

    def wview(off, cols):
        return AP(wpool, off, [[wps, 128], [1, cols]])

    wT = {nm: wview(i * 8192, 8192) for i, nm in enumerate("rkvwo")}
    OFF_SGU, OFF_W1, OFF_W2, OFF_SDN = 0, 8 * 2048, 8 * 2048 + 8192, 8 * 2048 + 2 * 8192
    OFF_RTR, OFF_SGW = 8 * 2048 + 3 * 8192, 8 * 2048 + 3 * 8192 + 64
    sgusb = wview(OFF_SGU, 8 * 2048)
    sdnsb = wview(OFF_SDN, 8192)
    rtrsb = wview(OFF_RTR, 64)
    sgwsb = wview(OFF_SGW, 8)
    R_SGU = ("wt_r", "wt_k")
    R_W1, R_W2, R_SDN = "wt_v", "wt_w", "wt_o"

    iden = A("iden", 128, BF16)
    umask8 = A("umask8", 128, mybir.dt.uint8)
    idenf = A("idenf", 128, F32)
    blko = A("blko", 128, BF16)
    hdo = A("hdo_sb", 128, BF16)
    ones1 = nc.alloc_sbuf_tensor("ones1", [1, 128], BF16)
    cvec = A("cvec", 52, F32)
    lnw64 = A("lnw64", 1, F32)

    xn_tok = A("xn_tok", 2048, BF16)
    xn_fm = A("xn_fm", 2048, BF16)
    xsh_fm = A("xsh_fm", 2048, BF16)
    mixb = A("mixb", 2048, BF16)
    tm1m = A("tm1m", 32, F32)
    r_bf = A("r_bf", 2048, BF16)
    k_bf = A("k_bf", 2048, BF16)
    eu_bf = A("eu_bf", 2048, BF16)
    sc_f32 = A("sc_f32", 2048, F32)
    sc2_f32 = A("sc2_f32", 2048, F32)
    chunk_f32 = A("chunk_f32", 2048, F32)
    expr_bf = A("expr_bf", 1024, BF16)
    rt_bf = A("rt_bf", 1024, BF16)
    kt_bf = A("kt_bf", 1024, BF16)
    kh_bf = A("kh_bf", 1024, BF16)
    khtok = A("khtok", 1024, BF16)
    t2_bf = A("t2_bf", 1024, BF16)
    vtok = A("vtok", 2048, BF16)
    mixtmp = A("mixtmp", 2048, BF16)
    vtil = A("vtil", 1024, BF16)
    mA_bf = A("mA_bf", 1024, BF16)
    mB_bf = A("mB_bf", 1024, BF16)
    S_bf = A("S_bf", 1024, BF16)
    eA_f32 = A("eA_f32", 48, F32)
    mcol_f32 = A("mcol_f32", 96, F32)
    D_sb = A("D_sb", 16, F32)
    o_fm = A("o_fm", 2048, BF16)
    og_fm = A("og_fm", 2048, BF16)
    prevc = A("prevc", 8, BF16)
    stat6 = A("stat6", 96, F32)
    mv = A("mv", 32, F32)
    o2_bf = expr_bf
    r1_bf = rt_bf
    b2_bf = kt_bf
    on_bf = kh_bf
    cmb_all = A("cmb_all", 64, BF16)
    gcol = A("gcol", 8, F32)
    lg_tok = A("lg_tok", 16, F32)
    esm = A("esm", 16, F32)
    top8 = A("top8", 16, F32)
    ssum = A("ssum", 4, F32)
    lgfm = nc.alloc_sbuf_tensor("lgfm", [8, 256], F32)

    pb = [nc.alloc_psum_tensor("pb%d" % i, [128, 512], F32) for i in range(8)]
    pbs = [p.ap().ap[0][0] for p in pb]
    PB = ["pb%d" % i for i in range(8)]

    def pbk(i, c0, n, parts=128):
        return AP(pb[i], c0, [[pbs[i], parts], [1, n]])

    def pbk_bf(i, c0, n, parts=128):
        return AP(pb[i], 0, [[pbs[i], parts], [1, 512]]).bitcast(BF16)[:, c0:c0 + n]

    tapd = {}
    tapn = [0]
    if tap:
        NCHUNK = nseq * T // CH
        _all = (("t_ofm", 1024, BF16), ("t_S", 1024, BF16),
                ("t_kt", 1024, BF16), ("t_mA", 1024, BF16),
                ("t_mB", 1024, BF16), ("t_og", 2048, BF16),
                ("t_var", 1024, F32))
        for nm, cols, dt in _all:
            if tapset is None or nm in tapset:
                tapd[nm] = nc.dram_tensor(nm, [NCHUNK, 128, cols], dt,
                                          kind="ExternalOutput").ap()

    P = Prog(nc, selfsync=race)
    pipe_sem = nc.alloc_semaphore(name="pipe")
    pipe_n = [0]

    def vguard(v, inst):
        if race:
            return
        pipe_n[0] += 1
        inst.then_inc(pipe_sem, 1)
        v.wait_ge(pipe_sem, pipe_n[0])

    def sl(t, c0, n):
        return t.ap()[:, c0:c0 + n]

    def ap3(t, blocks, bstep, inner, off=0):
        base = t.ap()
        return AP(t, off, [[base.ap[0][0], 128], [bstep, blocks], [1, inner]])

    def strided(t, off, step, count):
        base = t.ap()
        return AP(t, off, [[base.ap[0][0], 128], [step, count]])

    def fconsts(g):
        return [g.dma_start(out=iden.ap(), in_=d_iden),
                g.dma_start(out=umask8.ap(), in_=d_um8),
                g.dma_start(out=idenf.ap(), in_=d_idenf),
                g.dma_start(out=blko.ap(), in_=d_blko),
                g.dma_start(out=hdo.ap(), in_=d_hdo),
                g.dma_start(out=ones1.ap(), in_=d_ones),
                g.dma_start(out=cvec.ap(), in_=d_cvec)]
    P.dma("gpsimd", fconsts, n=7, writes=CONSTS)

    P.op("vector", lambda v: v.tensor_scalar_mul(lnw64.ap(), cvec.ap()[:, 40:41], 64.0),
         reads=["cvec"], writes=["lnw64"])
    P.op("vector", lambda v: v.tensor_scalar(tm1m.ap(), cvec.ap()[:, 0:32],
                                             -1.0, 1.0, ALU.mult, ALU.add),
         reads=["cvec"], writes=["tm1m"])

    def load_mat_1024(dst_off, src_ap, src_elem_off=0):
        def f(g):
            ia = AP(src_ap.tensor, src_elem_off, [[1024, 128], [131072, 8], [1, 1024]])
            oa = AP(wpool, dst_off, [[wps, 128], [1024, 8], [1, 1024]])
            return [g.dma_start(out=oa, in_=ia)]
        return f

    REGN = {0: "wt_r", 8192: "wt_k", 16384: "wt_v", 24576: "wt_w",
            32768: "wt_o"}

    def attn_offs(p):
        return {"r": 0 if p == 0 else 32768, "k": 8192, "v": 16384,
                "w": 24576, "o": 32768 if p == 0 else 0}

    def emit_attn_weight_loads(p):
        offs = attn_offs(p)
        for nm in "rkvwo":
            P.dma("gpsimd", load_mat_1024(offs[nm], d_wT[nm]),
                  writes=[REGN[offs[nm]]])

    def emit_ffn_shared_loads():
        def fsgu(g):
            ia = AP(d_sgu.tensor, 0, [[2048, 128], [262144, 8], [1, 2048]])
            oa = AP(wpool, OFF_SGU, [[wps, 128], [2048, 8], [1, 2048]])
            return [g.dma_start(out=oa, in_=ia)]
        P.dma("gpsimd", fsgu, writes=list(R_SGU))
        P.dma("gpsimd", load_mat_1024(OFF_SDN, d_sdn), writes=[R_SDN])

        def frtr(g):
            ia = AP(d_rtr.tensor, 0, [[8, 128], [1024, 8], [1, 8]])
            oa = AP(wpool, OFF_RTR, [[wps, 128], [8, 8], [1, 8]])
            return [g.dma_start(out=oa, in_=ia)]
        P.dma("gpsimd", frtr, writes=["rtr"])

        def fsgw(g):
            ia = AP(d_sgw.tensor, 0, [[1, 128], [128, 8], [1, 1]])
            oa = AP(wpool, OFF_SGW, [[wps, 128], [1, 8]])
            return [g.dma_start(out=oa, in_=ia, allow_slow_non_contiguous=True)]
        P.dma("gpsimd", fsgw, writes=["sgw"])

    def emit_ln_stats(src_ap, src_res, tag):
        s6 = stat6.ap()[:, tag * 12:tag * 12 + 12]
        m_ = lambda i: mv.ap()[:, tag * 4 + i:tag * 4 + i + 1]

        def fv(v):
            i1 = v.bn_stats(s6[:, 0:6], src_ap[:, 0:512])
            i2 = v.bn_stats(s6[:, 6:12], src_ap[:, 512:1024])
            if not race:
                i1.then_inc(pipe_sem, 1)
                i2.then_inc(pipe_sem, 1)
                pipe_n[0] += 2
            v.tensor_copy(m_(2), cvec.ap()[:, 40:41])
            if not race:
                v.wait_ge(pipe_sem, pipe_n[0])
            return v.bn_aggr(mv.ap()[:, tag * 4:tag * 4 + 2], s6)
        P.op("vector", fv, reads=[src_res, "cvec"],
             writes=["stat6:%d" % tag, "mv:%d" % tag])

        def fs(s):
            s.activation(m_(2), m_(1), AF.Ln, bias=cvec.ap()[:, 42:43], scale=1.0)
            return s.activation(m_(2), m_(2), AF.Exp, bias=0.0, scale=cvec.ap()[:, 46:47])
        P.op("scalar", fs, reads=["cvec"], writes=["mv:%d" % tag])

    def emit_ln_norm(src_ap, src_res, dst_ap, tag):
        m_ = lambda i: mv.ap()[:, tag * 4 + i:tag * 4 + i + 1]

        def fv(v):
            return v.tensor_scalar(dst_ap, src_ap, m_(0), m_(2),
                                   ALU.subtract, ALU.mult)
        P.op("vector", fv, reads=[src_res, "mv:%d" % tag], writes=["xn_tok"])

    def emit_ln(src_aps, src_ress, tb=0):
        for i in range(2):
            emit_ln_stats(src_aps[i], src_ress[i], tb + i)
        for i in range(2):
            emit_ln_norm(src_aps[i], src_ress[i],
                         sl(xn_tok, i * 1024, 1024), tb + i)

    def emit_xn_transposes():
        xst = xn_fm.ap().ap[0][0]
        tst = xn_tok.ap().ap[0][0]
        for tc in range(2):
            def ftr(sy, tc=tc):
                oa = AP(xn_fm, tc * CH, [[xst, 128], [STG, NF], [1, 128]])
                ia = AP(xn_tok, tc * 1024, [[tst, 128], [128, NF], [1, 128]])
                return [sy.dma_start(out=oa, in_=ia, transpose=True)]
            P.dma("sync", ftr, n=1, reads=["xn_tok"], writes=["xn_fm"])

    def emit_proj_fm(w_view, w_res, src_tile, src_res, banks, evac, n=STG):
        def f(te):
            last = None
            for fo in range(NF):
                outap = pbk(banks[fo // 2], (fo % 2) * n, n)
                for dc in range(NF):
                    last = te.matmul(
                        outap,
                        w_view[:, dc * 1024 + fo * 128: dc * 1024 + fo * 128 + 128],
                        sl(src_tile, dc * STG, n),
                        start=(dc == 0), stop=(dc == NF - 1))
            return last
        P.op("tensor", f, reads=[w_res, src_res],
             writes=[PB[b] for b in banks])
        evac(banks, n)

    def scan_banks(tc):
        pA = [0, 1, 2, 3] if tc % 2 == 0 else [4, 5, 6, 7]
        pB = [4, 5, 6, 7] if tc % 2 == 0 else [0, 1, 2, 3]
        return pA, pB, [PB[b_] for b_ in pA], [PB[b_] for b_ in pB]

    def scan_front(tc):
        mo = tc * 48
        eo = tc * 24
        RM, RE = "mcol:%d" % tc, "eA:%d" % tc
        mc = lambda a, b: mcol_f32.ap()[:, mo + a:mo + b]
        eA = eA_f32.ap()[:, eo:eo + 8]

        def fscan(v):
            last = None
            for f in range(NF):
                src = sl(sc_f32, f * STG + tc * CH, CH)
                last = v.tensor_tensor_scan(sl(chunk_f32, f * CH, CH), src, src,
                                            0.0, ALU.add, ALU.bypass)
            return last
        P.op("vector", fscan, reads=["sc_f32"], writes=["chunk_f32"])

        def fmid(v):
            v.tensor_copy(mc(0, 8), strided(chunk_f32, 63, CH, NF))
            v.tensor_scalar(mc(8, 16), strided(chunk_f32, 63, CH, NF),
                            -1.0, -30.0, ALU.mult, ALU.add)
            v.tensor_scalar_add(mc(32, 40), strided(chunk_f32, 63, CH, NF),
                                -30.0)
            return v.tensor_sub(eA, strided(chunk_f32, CH - 1, CH, NF),
                                strided(chunk_f32, 63, CH, NF))
        P.op("vector", fmid, reads=["chunk_f32"], writes=[RM, RE])

        def fmid2(s):
            eAk = eA_f32.ap()[:, eo + 8:eo + 16]
            eAm = eA_f32.ap()[:, eo + 16:eo + 24]
            s.activation(mc(16, 24), mc(0, 8), AF.Exp,
                         bias=cvec.ap()[:, 47:48], scale=1.0)
            s.activation(mc(24, 32), mc(0, 8), AF.Exp,
                         bias=cvec.ap()[:, 48:49], scale=cvec.ap()[:, 45:46])
            s.activation(eAk, eA, AF.Exp,
                         bias=cvec.ap()[:, 48:49], scale=cvec.ap()[:, 45:46])
            s.activation(eAm, eA, AF.Exp,
                         bias=cvec.ap()[:, 47:48], scale=cvec.ap()[:, 45:46])
            return s.activation(eA, eA, AF.Exp, bias=0.0,
                                scale=cvec.ap()[:, 45:46])
        P.op("scalar", fmid2, reads=["cvec"], writes=[RM, RE])

        def fexpr(s):
            last = None
            for f in range(NF):
                last = s.activation(sl(expr_bf, f * CH + 1, CH - 1),
                                    sl(chunk_f32, f * CH, CH - 1),
                                    AF.Exp, bias=mc(32 + f, 33 + f),
                                    scale=cvec.ap()[:, 45:46])
            return last
        P.op("scalar", fexpr, reads=["chunk_f32", RM, "cvec"], writes=["expr_bf"])
        P.op("vector", lambda v: v.tensor_copy(strided(expr_bf, 0, CH, NF),
                                               mc(16, 24)),
             reads=[RM], writes=["expr_bf"])

        def fek(s):
            last = None
            for f in range(NF):
                last = s.activation(sl(chunk_f32, 1024 + f * CH, CH),
                                    sl(chunk_f32, f * CH, CH),
                                    AF.Exp, bias=mc(8 + f, 9 + f), scale=1.0)
            return last
        P.op("scalar", fek, reads=[RM], writes=["chunk_f32"])

    def scan_main(tc):
        pA, pB, PA, PBk = scan_banks(tc)
        mo = tc * 48
        eo = tc * 24
        RM, RE = "mcol:%d" % tc, "eA:%d" % tc
        mc = lambda a, b: mcol_f32.ap()[:, mo + a:mo + b]
        eA = eA_f32.ap()[:, eo:eo + 8]

        def frt(v):
            rsl = ap3(r_bf, NF, STG, CH, off=tc * CH)
            v.tensor_mul(ap3(rt_bf, NF, CH, CH), rsl, ap3(expr_bf, NF, CH, CH))
            v.tensor_mul(ap3(kt_bf, NF, CH, CH), ap3(k_bf, NF, STG, CH, off=tc * CH),
                         ap3(chunk_f32, NF, CH, CH, off=1024))
            return v.tensor_mul(ap3(t2_bf, NF, CH, CH), rsl,
                                ap3(eu_bf, NF, STG, CH, off=tc * CH))
        P.op("vector", frt, reads=["r_bf", "expr_bf", "k_bf", "chunk_f32", "eu_bf"],
             writes=["rt_bf", "kt_bf", "t2_bf"])

        def fkh(v):
            last = None
            for f in range(NF):
                last = v.tensor_scalar_mul(sl(kh_bf, f * CH, CH),
                                           sl(kt_bf, f * CH, CH),
                                           eA_f32.ap()[:, eo + 8 + f:eo + 9 + f])
            return last
        P.op("vector", fkh, reads=["kt_bf", RE], writes=["kh_bf"])

        for rnd in range(2):
            def fkhT(te, rnd=rnd):
                last = None
                for j in range(4):
                    last = te.transpose(
                        AP(pb[pB[j]], 0, [[pbs[pB[j]], 128], [1, 512]])
                        .bitcast(BF16)[:, 0:128],
                        sl(kh_bf, (rnd * 4 + j) * CH, CH), iden.ap())
                return last
            P.op("tensor", fkhT, reads=["kh_bf", "iden"], writes=PBk)

            def fkhE(v, rnd=rnd):
                last = None
                for j in range(4):
                    last = v.tensor_copy(
                        sl(khtok, (rnd * 4 + j) * 128, 128),
                        AP(pb[pB[j]], 0, [[pbs[pB[j]], 128], [1, 512]])
                        .bitcast(BF16)[:, 0:128])
                return last
            P.op("vector", fkhE, reads=PBk, writes=["khtok"])

        def fd(te):
            last = None
            for f in range(NF):
                last = te.matmul(pbk(pB[0], 0, 16), sl(t2_bf, f * CH, CH),
                                 hdo.ap()[:, f * 16:f * 16 + 16],
                                 start=(f == 0), stop=(f == NF - 1))
            return last
        P.op("tensor", fd, reads=["t2_bf", "hdo"], writes=[PBk[0]])
        P.op("vector", lambda v: v.tensor_copy(D_sb.ap(), pbk(pB[0], 0, 16)),
             reads=[PBk[0]], writes=["D_sb"])

        sprime = kh_bf
        sprime2 = t2_bf

        def fsprime(v):
            last = None
            for f in range(NF):
                last = v.tensor_scalar_mul(
                    sl(sprime, f * 128, 128), sl(S_bf, f * 128, 128),
                    mc(24 + f, 25 + f))
            return last
        P.op("vector", fsprime, reads=["S_bf", RM], writes=["kh_bf"])

        def fsprime2(v):
            last = None
            for f in range(NF):
                last = v.tensor_scalar_mul(
                    sl(sprime2, f * 128, 128), sl(sprime, f * 128, 128),
                    eA_f32.ap()[:, eo + 16 + f:eo + 17 + f])
            return last
        P.op("vector", fsprime2, reads=["kh_bf", RE], writes=["t2_bf"])

        def fpall(te):
            last = None
            for f in range(NF):
                te.matmul(pbk(pA[0] + f // 4, (f % 4) * 128, 128),
                          kt_bf.ap()[0:64, f * CH:f * CH + CH],
                          rt_bf.ap()[0:64, f * CH:f * CH + CH],
                          start=True, stop=True)
                last = te.matmul(pbk(pA[2] + f // 4, (f % 4) * 128, 128),
                                 kt_bf.ap()[64:128, f * CH:f * CH + CH],
                                 rt_bf.ap()[64:128, f * CH:f * CH + CH],
                                 start=True, stop=True)
            return last
        P.op("tensor", fpall, reads=["kt_bf", "rt_bf"], writes=PA)

        def fmaskall(v):
            last = None
            for f in range(NF):
                v.select(sl(mA_bf, f * 128, 128), umask8.ap(),
                         pbk(pA[0] + f // 4, (f % 4) * 128, 128), umask8.ap())
                last = v.select(sl(mB_bf, f * 128, 128), umask8.ap(),
                                pbk(pA[2] + f // 4, (f % 4) * 128, 128), umask8.ap())
            E60 = 1.1420073898156842e+26
            vguard(v, last)
            v.tensor_scalar_mul(mA_bf.ap(), mA_bf.ap(), E60)
            return v.tensor_scalar_mul(mB_bf.ap(), mB_bf.ap(), E60)
        P.op("vector", fmaskall, reads=["umask8"] + PA,
             writes=["mA", "mB"])

        def fvtil(s):
            last = None
            for f in range(NF):
                s.activation(sl(vtil, f * 128, 64),
                             sl(vtok, tc * 1024 + f * 128, 64),
                             AF.Copy, bias=0.0,
                             scale=D_sb.ap()[:, 2 * f:2 * f + 1])
                last = s.activation(sl(vtil, f * 128 + 64, 64),
                                    sl(vtok, tc * 1024 + f * 128 + 64, 64),
                                    AF.Copy, bias=0.0,
                                    scale=D_sb.ap()[:, 2 * f + 1:2 * f + 2])
            return last
        P.op("scalar", fvtil, reads=["vtok", "D_sb"], writes=["vtil"])

        def foall(te):
            last = None
            for f in range(NF):
                b_ = pB[0] + f // 4
                c0 = (f % 4) * 128
                oo = pbk(b_, c0, 128)
                ooA = AP(pb[b_], c0, [[pbs[b_], 64], [1, 128]])
                ooB = AP(pb[b_], 64 * pbs[b_] + c0, [[pbs[b_], 64], [1, 128]])
                te.matmul(oo, sl(sprime, f * 128, 128),
                          rt_bf.ap()[:, f * CH:f * CH + CH], start=True, stop=False)
                te.matmul(ooA, sl(vtok, tc * 1024 + f * 128, 64),
                          sl(mA_bf, f * 128, 128), start=False, stop=False)
                te.matmul(ooB, sl(vtok, tc * 1024 + f * 128 + 64, 64),
                          sl(mB_bf, f * 128, 128), start=False, stop=False)
                te.matmul(oo, sl(vtil, f * 128, 128), iden.ap(),
                          start=False, stop=True)
                last = te.matmul(pbk(pB[2] + f // 4, (f % 4) * 128, 128),
                                 sl(khtok, f * 128, 128),
                                 sl(vtok, tc * 1024 + f * 128, 128),
                                 start=True, stop=True)
            return last
        P.op("tensor", foall,
             reads=["kh_bf", "rt_bf", "vtil", "iden", "mA",
                    "mB", "khtok", "vtok"],
             writes=PBk)

        def fseC(s):
            last = None
            for f in range(NF):
                last = s.activation(sl(o_fm, f * STG + tc * CH, CH),
                                    pbk(pB[0] + f // 4, (f % 4) * 128, 128),
                                    AF.Copy, bias=0.0, scale=1.0)
            return last
        P.op("scalar", fseC, reads=PBk[0:2], writes=["o_fm:%d" % tc])

        def fseall(v):
            sst = S_bf.ap().ap[0][0]
            s2t = t2_bf.ap().ap[0][0]
            last = None
            for half in range(2):
                for bi in range(2):
                    b_ = pB[2] + bi
                    f0 = bi * 4
                    co = half * 64
                    po = half * 64 * pbs[b_] + co
                    last = v.tensor_add(
                        AP(S_bf, half * 64 * sst + f0 * 128 + co,
                           [[sst, 64], [128, 4], [1, 64]]),
                        AP(t2_bf, half * 64 * s2t + f0 * 128 + co,
                           [[s2t, 64], [128, 4], [1, 64]]),
                        AP(pb[b_], po, [[pbs[b_], 64], [128, 4], [1, 64]]))
            return last
        P.op("vector", fseall, reads=PBk[2:4] + ["t2_bf"],
             writes=["S_bf"])

    def scan_gn(tc, alt=False):
        pA, pB, PA, PBk = scan_banks(tc)
        gns = mixb.ap().bitcast(F32)[:, 0:1024]
        if alt:
            r1v, b2v = rt_bf.ap(), kt_bf.ap()
            RR = ["rt_bf", "kt_bf"]
        else:
            scb = sc_f32.ap().bitcast(BF16)
            r1v, b2v = scb[:, 0:1024], scb[:, 1024:2048]
            RR = ["sc_f32"]
        RO = "o_fm:%d" % tc

        def fo2(s):
            return s.activation(ap3(mB_bf, NF, CH, CH),
                                ap3(o_fm, NF, STG, CH, off=tc * CH),
                                AF.Square, bias=0.0, scale=1.0)
        P.op("scalar", fo2, reads=[RO], writes=["mB"])

        def fgst(te):
            last = None
            for f in range(NF):
                bank = pA[0] + f // 4
                last = te.matmul(pbk(bank, (f % 4) * 128, 128), blko.ap(),
                                 sl(o_fm, f * STG + tc * CH, CH),
                                 start=(f % 4 == 0), stop=(f % 4 == 3))
            for f in range(NF):
                bank = pA[2] + f // 4
                last = te.matmul(pbk(bank, (f % 4) * 128, 128), blko.ap(),
                                 sl(mB_bf, f * CH, CH),
                                 start=(f % 4 == 0), stop=(f % 4 == 3))
            return last
        P.op("tensor", fgst, reads=["blko", RO, "mB"], writes=PA)

        def fgev(v):
            v.tensor_copy(sl(sc2_f32, 0, 512), pbk(pA[0], 0, 512))
            v.tensor_copy(sl(sc2_f32, 512, 512), pbk(pA[1], 0, 512))
            v.tensor_copy(sl(sc2_f32, 1024, 512), pbk(pA[2], 0, 512))
            return v.tensor_copy(sl(sc2_f32, 1536, 512), pbk(pA[3], 0, 512))
        P.op("vector", fgev, reads=PA, writes=["sc2_f32"])

        def fgsq(v):
            v.tensor_mul(gns, sl(sc2_f32, 0, 1024), sl(sc2_f32, 0, 1024))
            v.scalar_tensor_tensor(sl(sc2_f32, 1024, 1024),
                                   sl(sc2_f32, 1024, 1024), 64.0, gns,
                                   ALU.mult, ALU.subtract)
            return v.tensor_single_scalar(sl(sc2_f32, 1024, 1024),
                                          sl(sc2_f32, 1024, 1024), 0.0, ALU.max)
        P.op("vector", fgsq, reads=["mixb"], writes=["sc2_f32", "mixb"])

        def fgln(s):
            s.activation(sl(sc2_f32, 1024, 1024), sl(sc2_f32, 1024, 1024), AF.Ln,
                         bias=cvec.ap()[:, 43:44], scale=1.0)
            return s.activation(sl(sc2_f32, 1024, 1024), sl(sc2_f32, 1024, 1024),
                                AF.Exp, bias=0.0, scale=cvec.ap()[:, 46:47])
        P.op("scalar", fgln, reads=["cvec"], writes=["sc2_f32"])

        def fr1(v):
            v.tensor_scalar_mul(r1v, sl(sc2_f32, 1024, 1024),
                                lnw64.ap()[:, 0:1])
            v.tensor_mul(gns, sl(sc2_f32, 0, 1024), r1v)
            return v.tensor_scalar(b2v, gns, -1.0 / 64.0,
                                   cvec.ap()[:, 41:42], ALU.mult, ALU.add)
        P.op("vector", fr1, reads=["lnw64", "sc2_f32", "cvec"],
             writes=RR + ["mixb"])

        def fon(v):
            og = ap3(og_fm, NF, STG, CH, off=tc * CH)
            v.tensor_mul(og, ap3(o_fm, NF, STG, CH, off=tc * CH), r1v)
            v.tensor_add(og, og, b2v)
            return v.tensor_mul(og, og,
                                ap3(r_bf, NF, STG, CH, off=tc * CH))
        P.op("vector", fon, reads=[RO, "r_bf"] + RR,
             writes=["og_fm"])

        if tap:
            ci = tapn[0]
            tapn[0] += 1

            _tt = [(nm, t_, cols, rr) for nm, t_, cols, rr in
                   (("t_ofm", o_fm, 1024, "o_fm:%d" % tc),
                    ("t_S", S_bf, 1024, "S_bf"),
                    ("t_kt", kt_bf, 1024, "kt_bf"),
                    ("t_mA", mA_bf, 1024, "mA"),
                    ("t_mB", mB_bf, 1024, "mB"),
                    ("t_og", og_fm, 2048, "og_fm"),
                    ("t_var", sc2_f32, 1024, "sc2_f32")) if nm in tapd]

            def ftap(g, tc=tc, ci=ci, _tt=_tt):
                outs = []
                for nm, t_, cols, _ in _tt:
                    nel = 128 * cols
                    ia = (t_.ap()[:, 0:cols] if nm != "t_var"
                          else t_.ap()[:, 1024:2048])
                    if nm == "t_ofm":
                        ia = ap3(t_, NF, STG, CH, off=tc * CH)
                    oa = AP(tapd[nm].tensor, ci * nel,
                            [[cols, 128], [1, cols]])
                    outs.append(g.dma_start(out=oa, in_=ia))
                return outs
            P.dma("gpsimd", ftap, n=len(_tt),
                  reads=[rr for _, _, _, rr in _tt],
                  writes=["tap%d" % ci])
    def emit_front(b, h, st):
        t0 = h * half_tokens
        st_t0 = t0 + st * STG
        base_tc = st * 2
        RC = ["resid:%d" % (base_tc + tc) for tc in range(2)]

        def fx(g, st_t0=st_t0, base_tc=base_tc, b=b):
            ia = AP(x_in.tensor, b * T * D + st_t0 * D,
                    [[1024, 128], [131072, 2], [1, 1024]])
            oa = AP(resid, base_tc * 1024,
                    [[resid.ap().ap[0][0], 128], [1024, 2], [1, 1024]])
            return [g.dma_start(out=oa, in_=ia)]
        P.dma("gpsimd", fx, writes=RC)

        emit_ln([sl(resid, (base_tc + tc) * 1024, 1024) for tc in range(2)], RC)
        emit_xn_transposes()
        if h == 0 and st == 0:
            P.op("vector", lambda v: v.memset(prevc.ap(), 0.0), writes=["prevc"])

        def fsh2(v):
            v.tensor_copy(ap3(xsh_fm, NF, STG, STG - 1, off=1),
                          ap3(xn_fm, NF, STG, STG - 1, off=0))
            v.tensor_copy(strided(xsh_fm, 0, STG, NF), prevc.ap())
            v.tensor_copy(prevc.ap(), strided(xn_fm, STG - 1, STG, NF))
            return v.tensor_sub(mixtmp.ap(), xn_fm.ap(), xsh_fm.ap())
        P.op("vector", fsh2, reads=["xn_fm"],
             writes=["xsh_fm", "prevc", "mixtmp"])

    def phase_A(b, h, first=False, p=0):
        t0 = h * half_tokens
        offs = attn_offs(p)
        wtv = {nm: wview(offs[nm], 8192) for nm in "rkvwo"}
        rres = {nm: REGN[offs[nm]] for nm in "rkvwo"}
        emit_attn_weight_loads(p)
        if h == 0:
            P.op("vector", lambda v: v.memset(S_bf.ap(), 0.0), writes=["S_bf"])

        if first:
            emit_front(b, h, 0)
        for st in range(NSTG):
            base_tc = st * 2
            RC = ["resid:%d" % (base_tc + tc) for tc in range(2)]

            cvst = cvec.ap().ap[0][0]

            def emit_mix(coff, dst, dstres):
                def fmix(v):
                    last = None
                    for f in range(NF):
                        last = v.scalar_tensor_tensor(
                            sl(dst, f * STG, STG), sl(mixtmp, f * STG, STG),
                            cvec.ap()[:, coff + f:coff + f + 1],
                            sl(xsh_fm, f * STG, STG), ALU.mult, ALU.add)
                    return last
                P.op("vector", fmix, reads=["mixtmp", "cvec", "xsh_fm"],
                     writes=[dstres])

            def evac_r(banks, n):
                def fs(s):
                    last = None
                    for fo in range(NF):
                        last = s.activation(sl(r_bf, fo * STG, n),
                                            pbk(banks[fo // 2], (fo % 2) * n, n),
                                            AF.Sigmoid, bias=0.0, scale=1.0)
                    return last
                P.op("scalar", fs, reads=[PB[b_] for b_ in banks], writes=["r_bf"])
            emit_mix(0, mixb, "mixb")
            emit_proj_fm(wtv["r"], rres["r"], mixb, "mixb", [0, 1, 2, 3], evac_r)

            def evac_k(banks, n):
                def fv(v):
                    last = None
                    for fo in range(NF):
                        last = v.tensor_copy(sl(k_bf, fo * STG, n),
                                             pbk(banks[fo // 2], (fo % 2) * n, n))
                    return last
                P.op("vector", fv, reads=[PB[b_] for b_ in banks], writes=["k_bf"])

                def fs(s):
                    last = None
                    for fo in range(NF):
                        last = s.activation(sl(eu_bf, fo * STG, n),
                                            pbk(banks[fo // 2], (fo % 2) * n, n),
                                            AF.Exp,
                                            bias=cvec.ap()[:, 32 + fo:32 + fo + 1],
                                            scale=1.0)
                    return last
                P.op("scalar", fs, reads=[PB[b_] for b_ in banks] + ["cvec"],
                     writes=["eu_bf"])
            emit_mix(8, og_fm, "og_fm")
            emit_proj_fm(wtv["k"], rres["k"], og_fm, "og_fm", [4, 5, 6, 7], evac_k)

            def evac_w(banks, n):
                def fs(s):
                    last = None
                    for fo in range(NF):
                        last = s.activation(sl(sc_f32, fo * n, n),
                                            pbk(banks[fo // 2], (fo % 2) * n, n),
                                            AF.Exp, bias=0.0, scale=cvec.ap()[:, 45:46])
                    return last
                P.op("scalar", fs, reads=[PB[b_] for b_ in banks] + ["cvec"],
                     writes=["sc_f32"])
                P.op("scalar", lambda s: s.activation(
                    sc_f32.ap(), sc_f32.ap(), AF.Ln, bias=cvec.ap()[:, 44:45],
                    scale=1.0), reads=["cvec"], writes=["sc_f32"])
                P.op("vector", lambda v: v.tensor_scalar_add(
                    sc_f32.ap(), sc_f32.ap(), 1e-4), writes=["sc_f32"])
            emit_mix(24, mixb, "mixb")
            emit_proj_fm(wtv["w"], rres["w"], mixb, "mixb", [0, 1, 2, 3], evac_w)

            emit_mix(16, og_fm, "og_fm")

            def fvproj(te):
                last = None
                for tc in range(2):
                    for hlf in range(2):
                        outap = pbk(4 + tc * 2 + hlf, 0, 512)
                        for dc in range(NF):
                            last = te.matmul(
                                outap, sl(og_fm, dc * STG + tc * CH, 128),
                                wtv["v"][:, dc * 1024 + hlf * 512: dc * 1024 + hlf * 512 + 512],
                                start=(dc == 0), stop=(dc == NF - 1))
                return last
            P.op("tensor", fvproj, reads=["og_fm", rres["v"]],
                 writes=[PB[4], PB[5], PB[6], PB[7]])

            def fvevac(s):
                last = None
                for tc in range(2):
                    for hlf in range(2):
                        last = s.activation(sl(vtok, tc * 1024 + hlf * 512, 512),
                                            pbk(4 + tc * 2 + hlf, 0, 512),
                                            AF.Copy, bias=0.0, scale=1.0)
                return last
            P.op("scalar", fvevac, reads=[PB[4], PB[5], PB[6], PB[7]],
                 writes=["vtok"])

            scan_front(0)
            scan_main(0)
            scan_front(1)
            scan_gn(0)
            scan_main(1)
            if st + 1 < NSTG:
                emit_front(b, h, st + 1)
            scan_gn(1, alt=(st == NSTG - 1))

            def foproj(te):
                last = None
                for tc in range(2):
                    for hlf in range(2):
                        outap = pbk(tc * 2 + hlf, 0, 512)
                        for dc in range(NF):
                            last = te.matmul(
                                outap,
                                sl(og_fm, dc * STG + tc * CH, 128),
                                wtv["o"][:, dc * 1024 + hlf * 512: dc * 1024 + hlf * 512 + 512],
                                start=(dc == 0), stop=(dc == NF - 1))
                return last
            P.op("tensor", foproj, reads=[rres["o"], "og_fm"], writes=PB[0:4])

            def fradd(v, base_tc=base_tc):
                last = None
                for tc in range(2):
                    for hlf in range(2):
                        c0 = (base_tc + tc) * 1024 + hlf * 512
                        last = v.tensor_add(sl(resid, c0, 512), sl(resid, c0, 512),
                                            pbk(tc * 2 + hlf, 0, 512))
                return last
            P.op("vector", fradd, reads=PB[0:4] + RC, writes=RC)

    XN2H = [(sc_f32, 0), (sc_f32, 2048), (chunk_f32, 0), (chunk_f32, 2048)]
    XN2R = ["sc_f32", "sc_f32", "chunk_f32", "chunk_f32"]

    def xn2v(st):
        t, c0 = XN2H[st]
        a = t.ap() if t.ap().dtype == BF16 else t.ap().bitcast(BF16)
        return a[:, c0:c0 + 2048]

    def expert_slots(e):
        if e % 2 == 0:
            return OFF_W1, OFF_W2, R_W1, R_W2
        return 0, 8192, "wt_r", "wt_k"

    def emit_expert_load(e):
        off1, off2, r1, r2 = expert_slots(e)
        P.dma("gpsimd", load_mat_1024(off1, d_w1, src_elem_off=e * D * D),
              writes=[r1])
        P.dma("gpsimd", load_mat_1024(off2, d_w2, src_elem_off=e * D * D),
              writes=[r2])

    def phase_B_router(st):
        xn2s = xn2v(st)

        def frt(te):
            last = None
            for tc2 in range(2):
                for dc in range(NF):
                    last = te.matmul(
                        pbk(1 + tc2, 0, 8),
                        xn2s[:, dc * STG + tc2 * CH:dc * STG + tc2 * CH + 128],
                        rtrsb[:, dc * 8:dc * 8 + 8],
                        start=(dc == 0), stop=(dc == NF - 1))
            return last
        P.op("tensor", frt, reads=["rtr", XN2R[st]], writes=[PB[1], PB[2]])

        def frte(v):
            v.tensor_copy(lg_tok.ap()[:, 0:8], pbk(1, 0, 8))
            return v.tensor_copy(lg_tok.ap()[:, 8:16], pbk(2, 0, 8))
        P.op("vector", frte, reads=[PB[1], PB[2]], writes=["lg_tok"])

        def fmax(v):
            v.reduce_max(ssum.ap()[:, 0:1], lg_tok.ap()[:, 0:8], AXX, negate=True)
            return v.reduce_max(ssum.ap()[:, 2:3], lg_tok.ap()[:, 8:16], AXX,
                                negate=True)
        P.op("vector", fmax, reads=["lg_tok"], writes=["ssum"])

        def fex(s):
            s.activation(esm.ap()[:, 0:8], lg_tok.ap()[:, 0:8], AF.Exp,
                         bias=ssum.ap()[:, 0:1], scale=1.0,
                         accum_out=ssum.ap()[:, 1:2])
            return s.activation(esm.ap()[:, 8:16], lg_tok.ap()[:, 8:16], AF.Exp,
                                bias=ssum.ap()[:, 2:3], scale=1.0,
                                accum_out=ssum.ap()[:, 3:4])
        P.op("scalar", fex, reads=["lg_tok"], writes=["esm", "ssum"])

        def ftp(v, st=st):
            ea = esm.ap()[:, 0:8]
            eb = esm.ap()[:, 8:16]
            ta = top8.ap()[:, 0:8]
            tb = top8.ap()[:, 8:16]
            ga = lg_tok.ap()[:, 0:8]
            gb = lg_tok.ap()[:, 8:16]
            v.max(ta, ea)
            vguard(v, v.max(tb, eb))
            v.reciprocal(ssum.ap()[:, 0:1], ssum.ap()[:, 1:2])
            v.reciprocal(ssum.ap()[:, 2:3], ssum.ap()[:, 3:4])
            v.tensor_single_scalar(ga, ea, top8.ap()[:, 1:2], ALU.is_ge)
            vguard(v, v.tensor_single_scalar(gb, eb, top8.ap()[:, 9:10],
                                             ALU.is_ge))
            v.tensor_mul(ga, ga, ea)
            vguard(v, v.tensor_mul(gb, gb, eb))
            v.tensor_scalar_mul(cmb_all.ap()[:, st * 16:st * 16 + 8], ga,
                                ssum.ap()[:, 0:1])
            return v.tensor_scalar_mul(cmb_all.ap()[:, st * 16 + 8:st * 16 + 16],
                                       gb, ssum.ap()[:, 2:3])
        P.op("vector", ftp, reads=["esm"],
             writes=["lg_tok", "top8", "ssum", "cmb_all"])

    def phase_B_front(b, h, st):
        base_tc = st * 2
        RC = ["resid:%d" % (base_tc + tc) for tc in range(2)]
        emit_ln([sl(resid, (base_tc + tc) * 1024, 1024) for tc in range(2)], RC,
                tb=st * 2)
        host, c0 = XN2H[st]
        hst = host.ap().ap[0][0]
        tst = xn_tok.ap().ap[0][0]
        for tc in range(2):
            def ftr(sy, tc=tc):
                oa = AP(host, c0 // 2 + tc * 64,
                        [[hst, 128], [128, NF], [1, 64]]).bitcast(BF16)
                ia = AP(xn_tok, tc * 1024, [[tst, 128], [128, NF], [1, 128]])
                return [sy.dma_start(out=oa, in_=ia, transpose=True)]
            P.dma("sync", ftr, n=1, reads=["xn_tok"], writes=[XN2R[st]])

    def phase_B_stage(b, h, st):
        base_tc = st * 2
        RC = ["resid:%d" % (base_tc + tc) for tc in range(2)]
        xn2s = xn2v(st)

        def fgc(te):
            last = None
            for tc2 in range(2):
                for dc in range(NF):
                    last = te.matmul(
                        pbk(5, tc2 * 256, 1),
                        xn2s[:, dc * STG + tc2 * CH:dc * STG + tc2 * CH + 128],
                        sgwsb[:, dc:dc + 1],
                        start=(dc == 0), stop=(dc == NF - 1))
            return last
        P.op("tensor", fgc, reads=["sgw", XN2R[st]], writes=[PB[5]])

        def fgsig(s, st=st):
            s.activation(gcol.ap()[:, st * 2:st * 2 + 1], pbk(5, 0, 1),
                         AF.Sigmoid, bias=0.0, scale=1.0)
            return s.activation(gcol.ap()[:, st * 2 + 1:st * 2 + 2],
                                pbk(5, 256, 1), AF.Sigmoid, bias=0.0, scale=1.0)
        P.op("scalar", fgsig, reads=[PB[5]], writes=["gcol"])

        hb, hbr = (eu_bf, "eu_bf") if st % 2 == 0 else (xn_fm, "xn_fm")
        tb, tbr = (mixb, "mixb") if st % 2 == 0 else (og_fm, "og_fm")

        def fgu(te):
            last = None
            for f2 in range(16):
                outap = pbk(f2 // 2, (f2 % 2) * 256, 256)
                for dc in range(NF):
                    last = te.matmul(
                        outap,
                        sgusb[:, dc * 2048 + f2 * 128: dc * 2048 + f2 * 128 + 128],
                        xn2s[:, dc * STG:dc * STG + STG],
                        start=(f2 % 2 == 0 and dc == 0),
                        stop=(f2 % 2 == 1 and dc == NF - 1))
            return last
        P.op("tensor", fgu, reads=list(R_SGU) + [XN2R[st]], writes=PB[:])

        def fsg(s):
            last = None
            for f in range(NF):
                last = s.activation(sl(hb, f * STG, STG),
                                    pbk(f // 2, (f % 2) * 256, 256),
                                    AF.Silu, bias=0.0, scale=1.0)
            return last
        P.op("scalar", fsg, reads=PB[0:4], writes=[hbr])

        def fup(v):
            last = None
            for f in range(NF):
                last = v.tensor_mul(sl(tb, f * STG, STG), sl(hb, f * STG, STG),
                                    pbk(4 + f // 2, (f % 2) * 256, 256))
            return last
        P.op("vector", fup, reads=[hbr] + PB[4:8], writes=[tbr])

        def fsdn(te):
            last = None
            for tc2 in range(2):
                for hlf in range(2):
                    outap = pbk(tc2 * 2 + hlf, 0, 512)
                    for fc in range(NF):
                        last = te.matmul(
                            outap, sl(tb, fc * STG + tc2 * CH, 128),
                            sdnsb[:, fc * 1024 + hlf * 512: fc * 1024 + hlf * 512 + 512],
                            start=(fc == 0), stop=(fc == NF - 1))
            return last
        P.op("tensor", fsdn, reads=[R_SDN, tbr], writes=PB[0:4])

        def fsadd(v, st=st):
            last = None
            for tc2 in range(2):
                for hlf in range(2):
                    c0 = (base_tc + tc2) * 1024 + hlf * 512
                    last = v.scalar_tensor_tensor(
                        sl(resid, c0, 512), pbk(tc2 * 2 + hlf, 0, 512),
                        gcol.ap()[:, st * 2 + tc2:st * 2 + tc2 + 1],
                        sl(resid, c0, 512), ALU.mult, ALU.add)
            return last
        P.op("vector", fsadd, reads=PB[0:4] + RC + ["gcol"], writes=RC)

    def phase_B_expert(e, b=0, h=0, nxt=None):
        off1, off2, r1, r2 = expert_slots(e)
        final = (e == E - 1)
        w1v = wview(off1, 8192)
        w2v = wview(off2, 8192)
        for st in range(NSTG):
            xn2s = xn2v(st)
            base_tc = st * 2
            RC = ["resid:%d" % (base_tc + tc) for tc in range(2)]
            hb, hbr = (eu_bf, "eu_bf") if st % 2 == 0 else (xn_fm, "xn_fm")

            def fw1(te, st=st, xn2s=xn2s, w1v=w1v):
                last = None
                for fc in range(NF):
                    outap = pbk(4 + fc // 2, (fc % 2) * 256, 256)
                    for dc in range(NF):
                        last = te.matmul(
                            outap,
                            w1v[:, dc * 1024 + fc * 128: dc * 1024 + fc * 128 + 128],
                            xn2s[:, dc * STG:dc * STG + STG],
                            start=(fc % 2 == 0 and dc == 0),
                            stop=(fc % 2 == 1 and dc == NF - 1))
                return last
            P.op("tensor", fw1, reads=[r1, XN2R[st]], writes=PB[4:8])

            def fsilu(s, hb=hb):
                last = None
                for fc in range(NF):
                    last = s.activation(sl(hb, fc * STG, STG),
                                        pbk(4 + fc // 2, (fc % 2) * 256, 256),
                                        AF.Silu, bias=0.0, scale=1.0)
                return last
            P.op("scalar", fsilu, reads=PB[4:8], writes=[hbr])

            def fw2(te, w2v=w2v, hb=hb):
                last = None
                for tc2 in range(2):
                    for hlf in range(2):
                        outap = pbk(tc2 * 2 + hlf, 0, 512)
                        for fc in range(NF):
                            last = te.matmul(
                                outap, sl(hb, fc * STG + tc2 * CH, 128),
                                w2v[:, fc * 1024 + hlf * 512: fc * 1024 + hlf * 512 + 512],
                                start=(fc == 0), stop=(fc == NF - 1))
                return last
            P.op("tensor", fw2, reads=[r2, hbr], writes=PB[0:4])

            def fadd(v, base_tc=base_tc, e=e, st=st):
                last = None
                for tc2 in range(2):
                    for hlf in range(2):
                        c0 = (base_tc + tc2) * 1024 + hlf * 512
                        last = v.scalar_tensor_tensor(
                            sl(resid, c0, 512), pbk(tc2 * 2 + hlf, 0, 512),
                            cmb_all.ap()[:, st * 16 + tc2 * 8 + e:
                                         st * 16 + tc2 * 8 + e + 1],
                            sl(resid, c0, 512), ALU.mult, ALU.add)
                return last
            P.op("vector", fadd, reads=PB[0:4] + RC + ["cmb_all"], writes=RC)

            if final:
                t0 = h * half_tokens

                def fst(g, st=st, t0=t0, b=b):
                    st_t0 = t0 + st * STG
                    oa = AP(y_out.tensor, b * T * D + st_t0 * D,
                            [[1024, 128], [131072, 2], [1, 1024]])
                    ia = AP(resid, st * 2 * 1024,
                            [[resid.ap().ap[0][0], 128], [1024, 2], [1, 1024]])
                    return [g.dma_start(out=oa, in_=ia)]
                P.dma("gpsimd", fst, reads=RC,
                      writes=["y:%d:%d:%d" % (b, h, st)])
                if st == 0 and nxt is not None:
                    emit_front(nxt[0], nxt[1], 0)

    def phase_B(b, h, nxt=None):
        emit_ffn_shared_loads()
        emit_expert_load(0)
        for st in range(NSTG):
            phase_B_front(b, h, st)
        for st in range(NSTG):
            phase_B_stage(b, h, st)
        for st in range(NSTG):
            phase_B_router(st)
        for e in range(E):
            if e + 1 < E:
                emit_expert_load(e + 1)
            phase_B_expert(e, b=b, h=h, nxt=nxt)

    bh = [(b, h) for b in range(nseq) for h in range(NH)]
    for i, (b, h) in enumerate(bh):
        phase_A(b, h, first=(i == 0), p=i % 2)
        phase_B(b, h, nxt=(bh[i + 1] if i + 1 < len(bh) else None))

    P.finalize()
    return nc


def make_host_inputs(inp):
    tT = lambda a: np.ascontiguousarray(np.asarray(a).T)
    cN = np.ascontiguousarray
    iden = np.eye(128, dtype=BF)
    umask = np.triu(np.ones((128, 128), dtype=np.float32), 1)
    blko = np.zeros((128, 128), dtype=np.float32)
    blko[0:64, 0:64] = 1.0
    blko[64:128, 64:128] = 1.0
    hdo = np.zeros((128, 128), dtype=np.float32)
    for f in range(8):
        hdo[0:64, f * 16 + 2 * f] = 1.0
        hdo[64:128, f * 16 + 2 * f + 1] = 1.0
    cvec = np.zeros((128, 52), dtype=np.float32)
    cvec[:, 47] = -30.0
    cvec[:, 48] = 30.0
    for i, nm in enumerate(("tm_r", "tm_k", "tm_v", "tm_w")):
        cvec[:, i * 8:(i + 1) * 8] = np.asarray(inp["L0_attn_" + nm]).reshape(8, 128).T
    cvec[:, 32:40] = np.asarray(inp["L0_attn_u"]).reshape(-1).reshape(8, 128).T
    cvec[:, 40] = np.tile(np.asarray(inp["L0_attn_ln_w"]), 2)
    cvec[:, 41] = np.tile(np.asarray(inp["L0_attn_ln_b"]), 2)
    cvec[:, 42] = 1e-5
    cvec[:, 43] = 4096.0 * 1e-5
    cvec[:, 44] = 1.0
    cvec[:, 45] = -1.0
    cvec[:, 46] = -0.5
    shared = {
        "wrT": tT(inp["L0_attn_W_r"]), "wkT": tT(inp["L0_attn_W_k"]),
        "wvT": tT(inp["L0_attn_W_v"]), "wwT": tT(inp["L0_attn_W_w"]),
        "woT": tT(inp["L0_attn_W_o"]),
        "w1": cN(np.asarray(inp["L0_ffn_w1"])), "w2": cN(np.asarray(inp["L0_ffn_w2"])),
        "sgu": cN(np.asarray(inp["L0_ffn_shared_gate_up"])),
        "sdn": cN(np.asarray(inp["L0_ffn_shared_down"])),
        "rtr": cN(np.asarray(inp["L0_ffn_router_w"])),
        "sgw": cN(np.asarray(inp["L0_ffn_shared_gate_w"])),
        "idn": iden, "um8": umask.astype(np.uint8),
        "idf": np.eye(128, dtype=np.float32), "blk": blko.astype(BF), "hdo": hdo.astype(BF),
        "on1": np.ones((1, 128), dtype=BF), "cvc": cvec,
    }
    x = np.asarray(inp["L0_attn_x"])
    xs = [cN(x[c * 2:(c + 1) * 2]) for c in range(NCORE)]
    return shared, xs


def kernel(**inputs):
    shared, xs = make_host_inputs(inputs)
    nc = build_nc()
    in_maps = [dict(shared, x=xs[c]) for c in range(NCORE)]
    res = run_bass_kernel_spmd(nc, in_maps, core_ids=list(range(NCORE)))
    out = np.concatenate([res.results[c]["y"] for c in range(NCORE)], axis=0)
    return np.ascontiguousarray(out.astype(np.float32))


# version 23
# speedup vs baseline: 1.1456x; 1.1214x over previous
import numpy as np
import ml_dtypes
import concourse.bass as bass
import concourse.mybir as mybir
from concourse.bass_utils import run_bass_kernel_spmd
from concourse.ap import AP

F32 = mybir.dt.float32
BF16 = mybir.dt.bfloat16
AF = mybir.ActivationFunctionType
ALU = mybir.AluOpType
AXX = mybir.AxisListType.X

NCORE = 8
D = 1024
NF = 8
E = 8
EPS = 1e-5
STG = 256
CH = 128
BF = ml_dtypes.bfloat16

CONSTS = ("iden", "umask8", "idenf", "blko", "hdo", "ones1", "cvec")


class Prog:

    ENGS = ("sync", "gpsimd", "tensor", "vector", "scalar")
    NLANES = 12

    def __init__(self, nc, selfsync=False):
        self.nc = nc
        self.selfsync = selfsync
        if selfsync:
            self.ssem = {e: nc.alloc_semaphore(name="ss_" + e) for e in self.ENGS}
            self.sctr = {e: [0] for e in self.ENGS}
        self.q = {e: [] for e in self.ENGS}
        self.sem = {e: nc.alloc_semaphore(name="s_" + e) for e in self.ENGS}
        self.cnt = {e: 0 for e in self.ENGS}
        self.lane_pool = {"gpsimd": list(range(8)), "sync": list(range(8, 12))}
        self.lane_sem = [nc.alloc_semaphore(name="ln%d" % i)
                         for i in range(self.NLANES)]
        self.lane_cnt = [0] * self.NLANES
        self.lane_rr = {"gpsimd": 0, "sync": 0}
        self.vclock = {e: {} for e in self.ENGS}
        self.res = {}

    def _handle(self, sk):
        return self.sem[sk] if isinstance(sk, str) else self.lane_sem[sk]

    def _wrap(self, eng, engine):
        if not self.selfsync:
            return engine
        sem, ctr = self.ssem[eng], self.sctr[eng]

        class _Ser:
            def __getattr__(_s, nm):
                a = getattr(engine, nm)
                if not callable(a):
                    return a

                def call(*args, **kw):
                    r = a(*args, **kw)
                    if (hasattr(r, "then_inc") and nm != "wait_ge"
                            and not nm.startswith("dma_start")
                            and not nm.startswith("sem_")):
                        ctr[0] += 1
                        r.then_inc(sem, 1)
                        engine.wait_ge(sem, ctr[0])
                    return r
                return call
        return _Ser()

    def _dep_points(self, reads, writes):
        pts = []
        for r in reads:
            e = self.res.get(r)
            if e is None or e["w"] is None:
                raise KeyError(f"read of resource {r!r} with no prior writer")
            pts.append(e["w"])
        for w in writes:
            e = self.res.get(w)
            if e is not None:
                if e["w"] is not None:
                    pts.append(e["w"])
                pts.extend(e["r"])
        return pts

    def _plan(self, eng, pts):
        vc = self.vclock[eng]
        need = {}
        for sk, val, _ in pts:
            if need.get(sk, -1) < val:
                need[sk] = val
        waits = []
        for sk, val in sorted(need.items(), key=str):
            if sk == eng and val <= self.cnt[eng]:
                continue
            if vc.get(sk, 0) >= val:
                continue
            waits.append((self._handle(sk), val))
        for sk, val, pvc in pts:
            for k2, v2 in pvc.items():
                if vc.get(k2, 0) < v2:
                    vc[k2] = v2
            if vc.get(sk, 0) < val:
                vc[sk] = val
        return waits

    def _update_res(self, reads, writes, pt):
        for w in writes:
            self.res[w] = {"w": pt, "r": []}
        for r in reads:
            e = self.res.get(r)
            if e is None:
                e = self.res[r] = {"w": None, "r": []}
            e["r"].append(pt)

    def op(self, eng, fn, reads=(), writes=()):
        pts = self._dep_points(reads, writes)
        waits = self._plan(eng, pts)
        self.cnt[eng] += 1
        myval = self.cnt[eng]
        sem = self.sem[eng]

        def run(engine, waits=waits, fn=fn, eng=eng):
            for h, v in waits:
                engine.wait_ge(h, v)
            last = fn(self._wrap(eng, engine))
            assert last is not None, "op fn must return its last instruction"
            if self.selfsync:
                engine.sem_inc(sem, 1)
            else:
                last.then_inc(sem, 1)

        self.q[eng].append(run)
        pt = (eng, myval, dict(self.vclock[eng]))
        self._update_res(reads, writes, pt)

    def dma(self, eng, fn, n=1, reads=(), writes=()):
        pts = self._dep_points(reads, writes)
        pool = self.lane_pool[eng]
        lane = pool[self.lane_rr[eng]]
        self.lane_rr[eng] = (self.lane_rr[eng] + 1) % len(pool)
        if self.lane_cnt[lane] > 0:
            pts = pts + [(lane, self.lane_cnt[lane], {})]
        waits = self._plan(eng, pts)
        self.lane_cnt[lane] += 16 * n
        target = self.lane_cnt[lane]
        sem = self.lane_sem[lane]

        def run(engine, waits=waits, fn=fn, n=n, eng=eng):
            for h, v in waits:
                engine.wait_ge(h, v)
            insts = fn(self._wrap(eng, engine))
            assert len(insts) == n, f"expected {n} DMAs, got {len(insts)}"
            for i in insts:
                i.then_inc(sem, 16)

        self.q[eng].append(run)
        pt = (lane, target, dict(self.vclock[eng]))
        self._update_res(reads, writes, pt)

    def finalize(self):
        with self.nc.Block() as block:
            final = ([(self.sem[e], self.cnt[e]) for e in self.ENGS
                      if self.cnt[e] > 0] +
                     [(self.lane_sem[i], self.lane_cnt[i])
                      for i in range(self.NLANES) if self.lane_cnt[i] > 0])
            for name in self.ENGS:
                ops = self.q[name]

                def body(engine, ops=ops):
                    for op in ops:
                        op(engine)
                    for s, v in final:
                        engine.wait_ge(s, v)

                getattr(block, name)(body)


def build_nc(T=2048, nseq=2, half_tokens=1024, race=False, tap=False, tapset=None):
    NH = T // half_tokens
    NSTG = half_tokens // STG
    NTC = half_tokens // CH

    nc = bass.Bass("TRN2", target_bir_lowering=False,
                   detect_race_conditions=race)

    x_in = nc.dram_tensor("x", [nseq, T, D], F32, kind="ExternalInput").ap()
    y_out = nc.dram_tensor("y", [nseq, T, D], F32, kind="ExternalOutput").ap()
    d_wT = {nm: nc.dram_tensor("w%sT" % nm, [D, D], F32, kind="ExternalInput").ap()
            for nm in "rkvwo"}
    d_w1 = nc.dram_tensor("w1", [E, D, D], F32, kind="ExternalInput").ap()
    d_w2 = nc.dram_tensor("w2", [E, D, D], F32, kind="ExternalInput").ap()
    d_sgu = nc.dram_tensor("sgu", [D, 2 * D], F32, kind="ExternalInput").ap()
    d_sdn = nc.dram_tensor("sdn", [D, D], F32, kind="ExternalInput").ap()
    d_rtr = nc.dram_tensor("rtr", [D, E], F32, kind="ExternalInput").ap()
    d_sgw = nc.dram_tensor("sgw", [D, 1], F32, kind="ExternalInput").ap()
    d_iden = nc.dram_tensor("idn", [128, 128], BF16, kind="ExternalInput").ap()
    d_um8 = nc.dram_tensor("um8", [128, 128], mybir.dt.uint8, kind="ExternalInput").ap()
    d_idenf = nc.dram_tensor("idf", [128, 128], F32, kind="ExternalInput").ap()
    d_blko = nc.dram_tensor("blk", [128, 128], BF16, kind="ExternalInput").ap()
    d_hdo = nc.dram_tensor("hdo", [128, 128], BF16, kind="ExternalInput").ap()
    d_ones = nc.dram_tensor("on1", [1, 128], BF16, kind="ExternalInput").ap()
    d_cvec = nc.dram_tensor("cvc", [128, 52], F32, kind="ExternalInput").ap()

    A = lambda n, c, dt: nc.alloc_sbuf_tensor(n, [128, c], dt)
    resid = A("resid", NTC * 1024, F32)
    WPC = max(5 * 8192, 8 * 2048 + 3 * 8192 + 64 + 8)
    wpool = A("wpool", WPC, BF16)
    wps = wpool.ap().ap[0][0]

    def wview(off, cols):
        return AP(wpool, off, [[wps, 128], [1, cols]])

    wT = {nm: wview(i * 8192, 8192) for i, nm in enumerate("rkvwo")}
    OFF_SGU, OFF_W1, OFF_W2, OFF_SDN = 0, 8 * 2048, 8 * 2048 + 8192, 8 * 2048 + 2 * 8192
    OFF_RTR, OFF_SGW = 8 * 2048 + 3 * 8192, 8 * 2048 + 3 * 8192 + 64
    sgusb = wview(OFF_SGU, 8 * 2048)
    sdnsb = wview(OFF_SDN, 8192)
    rtrsb = wview(OFF_RTR, 64)
    sgwsb = wview(OFF_SGW, 8)
    R_SGU = ("wt_r", "wt_k")
    R_W1, R_W2, R_SDN = "wt_v", "wt_w", "wt_o"

    iden = A("iden", 128, BF16)
    umask8 = A("umask8", 128, mybir.dt.uint8)
    idenf = A("idenf", 128, F32)
    blko = A("blko", 128, BF16)
    hdo = A("hdo_sb", 128, BF16)
    ones1 = nc.alloc_sbuf_tensor("ones1", [1, 128], BF16)
    cvec = A("cvec", 52, F32)
    lnw64 = A("lnw64", 1, F32)

    xn_tok = A("xn_tok", 2048, BF16)
    xn_fm = A("xn_fm", 2048, BF16)
    xsh_fm = A("xsh_fm", 2048, BF16)
    mixb = A("mixb", 2048, BF16)
    tm1m = A("tm1m", 32, F32)
    r_bf = A("r_bf", 2048, BF16)
    k_bf = A("k_bf", 2048, BF16)
    eu_bf = A("eu_bf", 2048, BF16)
    sc_f32 = A("sc_f32", 2048, F32)
    sc2_f32 = A("sc2_f32", 2048, F32)
    chunk_f32 = A("chunk_f32", 2048, F32)
    expr_bf = A("expr_bf", 1024, BF16)
    rt_bf = A("rt_bf", 1024, BF16)
    kt_bf = A("kt_bf", 1024, BF16)
    kh_bf = A("kh_bf", 1024, BF16)
    khtok = A("khtok", 1024, BF16)
    t2_bf = A("t2_bf", 1024, BF16)
    vtok = A("vtok", 2048, BF16)
    mixtmp = A("mixtmp", 2048, BF16)
    vtil = A("vtil", 1024, BF16)
    mA_bf = A("mA_bf", 1024, BF16)
    mB_bf = A("mB_bf", 1024, BF16)
    S_bf = A("S_bf", 1024, BF16)
    eA_f32 = A("eA_f32", 48, F32)
    mcol_f32 = A("mcol_f32", 96, F32)
    D_sb = A("D_sb", 16, F32)
    o_fm = A("o_fm", 2048, BF16)
    og_fm = A("og_fm", 2048, BF16)
    prevc = A("prevc", 8, BF16)
    stat6 = A("stat6", 24, F32)
    mv = A("mv", 8, F32)
    o2_bf = expr_bf
    r1_bf = rt_bf
    b2_bf = kt_bf
    on_bf = kh_bf
    cmb_all = A("cmb_all", 64, BF16)
    gcol = A("gcol", 8, F32)
    lg_tok = A("lg_tok", 16, F32)
    esm = A("esm", 16, F32)
    top8 = A("top8", 16, F32)
    ssum = A("ssum", 4, F32)
    lgfm = nc.alloc_sbuf_tensor("lgfm", [8, 256], F32)

    pb = [nc.alloc_psum_tensor("pb%d" % i, [128, 512], F32) for i in range(8)]
    pbs = [p.ap().ap[0][0] for p in pb]
    PB = ["pb%d" % i for i in range(8)]

    def pbk(i, c0, n, parts=128):
        return AP(pb[i], c0, [[pbs[i], parts], [1, n]])

    def pbk_bf(i, c0, n, parts=128):
        return AP(pb[i], 0, [[pbs[i], parts], [1, 512]]).bitcast(BF16)[:, c0:c0 + n]

    tapd = {}
    tapn = [0]
    if tap:
        NCHUNK = nseq * T // CH
        _all = (("t_ofm", 1024, BF16), ("t_S", 1024, BF16),
                ("t_kt", 1024, BF16), ("t_mA", 1024, BF16),
                ("t_mB", 1024, BF16), ("t_og", 2048, BF16),
                ("t_var", 1024, F32))
        for nm, cols, dt in _all:
            if tapset is None or nm in tapset:
                tapd[nm] = nc.dram_tensor(nm, [NCHUNK, 128, cols], dt,
                                          kind="ExternalOutput").ap()

    P = Prog(nc, selfsync=race)
    pipe_sem = nc.alloc_semaphore(name="pipe")
    pipe_n = [0]

    def vguard(v, inst):
        if race:
            return
        pipe_n[0] += 1
        inst.then_inc(pipe_sem, 1)
        v.wait_ge(pipe_sem, pipe_n[0])

    def sl(t, c0, n):
        return t.ap()[:, c0:c0 + n]

    def ap3(t, blocks, bstep, inner, off=0):
        base = t.ap()
        return AP(t, off, [[base.ap[0][0], 128], [bstep, blocks], [1, inner]])

    def strided(t, off, step, count):
        base = t.ap()
        return AP(t, off, [[base.ap[0][0], 128], [step, count]])

    def fconsts(g):
        return [g.dma_start(out=iden.ap(), in_=d_iden),
                g.dma_start(out=umask8.ap(), in_=d_um8),
                g.dma_start(out=idenf.ap(), in_=d_idenf),
                g.dma_start(out=blko.ap(), in_=d_blko),
                g.dma_start(out=hdo.ap(), in_=d_hdo),
                g.dma_start(out=ones1.ap(), in_=d_ones),
                g.dma_start(out=cvec.ap(), in_=d_cvec)]
    P.dma("gpsimd", fconsts, n=7, writes=CONSTS)

    P.op("vector", lambda v: v.tensor_scalar_mul(lnw64.ap(), cvec.ap()[:, 40:41], 64.0),
         reads=["cvec"], writes=["lnw64"])
    P.op("vector", lambda v: v.tensor_scalar(tm1m.ap(), cvec.ap()[:, 0:32],
                                             -1.0, 1.0, ALU.mult, ALU.add),
         reads=["cvec"], writes=["tm1m"])

    def load_mat_1024(dst_off, src_ap, src_elem_off=0):
        def f(g):
            ia = AP(src_ap.tensor, src_elem_off, [[1024, 128], [131072, 8], [1, 1024]])
            oa = AP(wpool, dst_off, [[wps, 128], [1024, 8], [1, 1024]])
            return [g.dma_start(out=oa, in_=ia)]
        return f

    REGN = {0: "wt_r", 8192: "wt_k", 16384: "wt_v", 24576: "wt_w",
            32768: "wt_o"}

    def attn_offs(p):
        return {"r": 0 if p == 0 else 32768, "k": 8192, "v": 16384,
                "w": 24576, "o": 32768 if p == 0 else 0}

    def emit_attn_weight_loads(p):
        offs = attn_offs(p)
        for nm in "rkvwo":
            P.dma("gpsimd", load_mat_1024(offs[nm], d_wT[nm]),
                  writes=[REGN[offs[nm]]])

    def emit_ffn_shared_loads():
        def fsgu(g):
            ia = AP(d_sgu.tensor, 0, [[2048, 128], [262144, 8], [1, 2048]])
            oa = AP(wpool, OFF_SGU, [[wps, 128], [2048, 8], [1, 2048]])
            return [g.dma_start(out=oa, in_=ia)]
        P.dma("gpsimd", fsgu, writes=list(R_SGU))
        P.dma("gpsimd", load_mat_1024(OFF_SDN, d_sdn), writes=[R_SDN])

        def frtr(g):
            ia = AP(d_rtr.tensor, 0, [[8, 128], [1024, 8], [1, 8]])
            oa = AP(wpool, OFF_RTR, [[wps, 128], [8, 8], [1, 8]])
            return [g.dma_start(out=oa, in_=ia)]
        P.dma("gpsimd", frtr, writes=["rtr"])

        def fsgw(g):
            ia = AP(d_sgw.tensor, 0, [[1, 128], [128, 8], [1, 1]])
            oa = AP(wpool, OFF_SGW, [[wps, 128], [1, 8]])
            return [g.dma_start(out=oa, in_=ia, allow_slow_non_contiguous=True)]
        P.dma("gpsimd", fsgw, writes=["sgw"])

    def emit_ln_stats(src_ap, src_res, tag):
        s6 = stat6.ap()[:, tag * 12:tag * 12 + 12]
        m_ = lambda i: mv.ap()[:, tag * 4 + i:tag * 4 + i + 1]

        def fv(v):
            i1 = v.bn_stats(s6[:, 0:6], src_ap[:, 0:512])
            i2 = v.bn_stats(s6[:, 6:12], src_ap[:, 512:1024])
            if not race:
                i1.then_inc(pipe_sem, 1)
                i2.then_inc(pipe_sem, 1)
                pipe_n[0] += 2
            v.tensor_copy(m_(2), cvec.ap()[:, 40:41])
            if not race:
                v.wait_ge(pipe_sem, pipe_n[0])
            return v.bn_aggr(mv.ap()[:, tag * 4:tag * 4 + 2], s6)
        P.op("vector", fv, reads=[src_res, "cvec"],
             writes=["stat6:%d" % tag, "mv:%d" % tag])

        def fs(s):
            s.activation(m_(2), m_(1), AF.Ln, bias=cvec.ap()[:, 42:43], scale=1.0)
            return s.activation(m_(2), m_(2), AF.Exp, bias=0.0, scale=cvec.ap()[:, 46:47])
        P.op("scalar", fs, reads=["cvec"], writes=["mv:%d" % tag])

    def emit_ln_norm(src_ap, src_res, dst_ap, tag):
        m_ = lambda i: mv.ap()[:, tag * 4 + i:tag * 4 + i + 1]

        def fv(v):
            return v.tensor_scalar(dst_ap, src_ap, m_(0), m_(2),
                                   ALU.subtract, ALU.mult)
        P.op("vector", fv, reads=[src_res, "mv:%d" % tag], writes=["xn_tok"])

    def emit_ln(src_aps, src_ress):
        for tag in range(2):
            emit_ln_stats(src_aps[tag], src_ress[tag], tag)
        for tag in range(2):
            emit_ln_norm(src_aps[tag], src_ress[tag],
                         sl(xn_tok, tag * 1024, 1024), tag)

    def emit_xn_transposes():
        xst = xn_fm.ap().ap[0][0]
        tst = xn_tok.ap().ap[0][0]
        for tc in range(2):
            def ftr(sy, tc=tc):
                oa = AP(xn_fm, tc * CH, [[xst, 128], [STG, NF], [1, 128]])
                ia = AP(xn_tok, tc * 1024, [[tst, 128], [128, NF], [1, 128]])
                return [sy.dma_start(out=oa, in_=ia, transpose=True)]
            P.dma("sync", ftr, n=1, reads=["xn_tok"],
                  writes=["xfA", "xfB"])

    def emit_proj_fm(w_view, w_res, src_tile, src_res, banks, evac, n=STG):
        def f(te):
            last = None
            for fo in range(NF):
                outap = pbk(banks[fo // 2], (fo % 2) * n, n)
                for dc in range(NF):
                    last = te.matmul(
                        outap,
                        w_view[:, dc * 1024 + fo * 128: dc * 1024 + fo * 128 + 128],
                        sl(src_tile, dc * STG, n),
                        start=(dc == 0), stop=(dc == NF - 1))
            return last
        P.op("tensor", f, reads=[w_res, src_res],
             writes=[PB[b] for b in banks])
        evac(banks, n)

    def scan_banks(tc):
        pA = [0, 1, 2, 3] if tc % 2 == 0 else [4, 5, 6, 7]
        pB = [4, 5, 6, 7] if tc % 2 == 0 else [0, 1, 2, 3]
        return pA, pB, [PB[b_] for b_ in pA], [PB[b_] for b_ in pB]

    def scan_front(tc):
        mo = tc * 48
        eo = tc * 24
        RM, RE = "mcol:%d" % tc, "eA:%d" % tc
        mc = lambda a, b: mcol_f32.ap()[:, mo + a:mo + b]
        eA = eA_f32.ap()[:, eo:eo + 8]

        def fscan(v):
            last = None
            for f in range(NF):
                src = sl(sc_f32, f * STG + tc * CH, CH)
                last = v.tensor_tensor_scan(sl(chunk_f32, f * CH, CH), src, src,
                                            0.0, ALU.add, ALU.bypass)
            return last
        P.op("vector", fscan, reads=["sc_f32"], writes=["chunk_f32"])

        def fmid(v):
            v.tensor_copy(mc(0, 8), strided(chunk_f32, 63, CH, NF))
            v.tensor_scalar(mc(8, 16), strided(chunk_f32, 63, CH, NF),
                            -1.0, -30.0, ALU.mult, ALU.add)
            v.tensor_scalar_add(mc(32, 40), strided(chunk_f32, 63, CH, NF),
                                -30.0)
            return v.tensor_sub(eA, strided(chunk_f32, CH - 1, CH, NF),
                                strided(chunk_f32, 63, CH, NF))
        P.op("vector", fmid, reads=["chunk_f32"], writes=[RM, RE])

        def fmid2(s):
            eAk = eA_f32.ap()[:, eo + 8:eo + 16]
            eAm = eA_f32.ap()[:, eo + 16:eo + 24]
            s.activation(mc(16, 24), mc(0, 8), AF.Exp,
                         bias=cvec.ap()[:, 47:48], scale=1.0)
            s.activation(mc(24, 32), mc(0, 8), AF.Exp,
                         bias=cvec.ap()[:, 48:49], scale=cvec.ap()[:, 45:46])
            s.activation(eAk, eA, AF.Exp,
                         bias=cvec.ap()[:, 48:49], scale=cvec.ap()[:, 45:46])
            s.activation(eAm, eA, AF.Exp,
                         bias=cvec.ap()[:, 47:48], scale=cvec.ap()[:, 45:46])
            return s.activation(eA, eA, AF.Exp, bias=0.0,
                                scale=cvec.ap()[:, 45:46])
        P.op("scalar", fmid2, reads=["cvec"], writes=[RM, RE])

        def fexpr(s):
            last = None
            for f in range(NF):
                last = s.activation(sl(expr_bf, f * CH + 1, CH - 1),
                                    sl(chunk_f32, f * CH, CH - 1),
                                    AF.Exp, bias=mc(32 + f, 33 + f),
                                    scale=cvec.ap()[:, 45:46])
            return last
        P.op("scalar", fexpr, reads=["chunk_f32", RM, "cvec"], writes=["expr_bf"])
        P.op("vector", lambda v: v.tensor_copy(strided(expr_bf, 0, CH, NF),
                                               mc(16, 24)),
             reads=[RM], writes=["expr_bf"])

        def fek(s):
            last = None
            for f in range(NF):
                last = s.activation(sl(chunk_f32, 1024 + f * CH, CH),
                                    sl(chunk_f32, f * CH, CH),
                                    AF.Exp, bias=mc(8 + f, 9 + f), scale=1.0)
            return last
        P.op("scalar", fek, reads=[RM], writes=["chunk_f32"])

    def scan_main(tc):
        pA, pB, PA, PBk = scan_banks(tc)
        mo = tc * 48
        eo = tc * 24
        RM, RE = "mcol:%d" % tc, "eA:%d" % tc
        mc = lambda a, b: mcol_f32.ap()[:, mo + a:mo + b]
        eA = eA_f32.ap()[:, eo:eo + 8]

        def frt(v):
            rsl = ap3(r_bf, NF, STG, CH, off=tc * CH)
            v.tensor_mul(ap3(rt_bf, NF, CH, CH), rsl, ap3(expr_bf, NF, CH, CH))
            v.tensor_mul(ap3(kt_bf, NF, CH, CH), ap3(k_bf, NF, STG, CH, off=tc * CH),
                         ap3(chunk_f32, NF, CH, CH, off=1024))
            return v.tensor_mul(ap3(t2_bf, NF, CH, CH), rsl,
                                ap3(eu_bf, NF, STG, CH, off=tc * CH))
        P.op("vector", frt,
             reads=["r_bf", "expr_bf", "k_bf", "chunk_f32", "euA", "euB"],
             writes=["rt_bf", "kt_bf", "t2_bf"])

        def fkh(v):
            last = None
            for f in range(NF):
                last = v.tensor_scalar_mul(sl(kh_bf, f * CH, CH),
                                           sl(kt_bf, f * CH, CH),
                                           eA_f32.ap()[:, eo + 8 + f:eo + 9 + f])
            return last
        P.op("vector", fkh, reads=["kt_bf", RE], writes=["kh_bf"])

        for rnd in range(2):
            def fkhT(te, rnd=rnd):
                last = None
                for j in range(4):
                    last = te.transpose(
                        AP(pb[pB[j]], 0, [[pbs[pB[j]], 128], [1, 512]])
                        .bitcast(BF16)[:, 0:128],
                        sl(kh_bf, (rnd * 4 + j) * CH, CH), iden.ap())
                return last
            P.op("tensor", fkhT, reads=["kh_bf", "iden"], writes=PBk)

            def fkhE(v, rnd=rnd):
                last = None
                for j in range(4):
                    last = v.tensor_copy(
                        sl(khtok, (rnd * 4 + j) * 128, 128),
                        AP(pb[pB[j]], 0, [[pbs[pB[j]], 128], [1, 512]])
                        .bitcast(BF16)[:, 0:128])
                return last
            P.op("vector", fkhE, reads=PBk, writes=["khtok"])

        def fd(te):
            last = None
            for f in range(NF):
                last = te.matmul(pbk(pB[0], 0, 16), sl(t2_bf, f * CH, CH),
                                 hdo.ap()[:, f * 16:f * 16 + 16],
                                 start=(f == 0), stop=(f == NF - 1))
            return last
        P.op("tensor", fd, reads=["t2_bf", "hdo"], writes=[PBk[0]])
        P.op("vector", lambda v: v.tensor_copy(D_sb.ap(), pbk(pB[0], 0, 16)),
             reads=[PBk[0]], writes=["D_sb"])

        sprime = kh_bf
        sprime2 = t2_bf

        def fsprime(v):
            last = None
            for f in range(NF):
                last = v.tensor_scalar_mul(
                    sl(sprime, f * 128, 128), sl(S_bf, f * 128, 128),
                    mc(24 + f, 25 + f))
            return last
        P.op("vector", fsprime, reads=["S_bf", RM], writes=["kh_bf"])

        def fsprime2(v):
            last = None
            for f in range(NF):
                last = v.tensor_scalar_mul(
                    sl(sprime2, f * 128, 128), sl(sprime, f * 128, 128),
                    eA_f32.ap()[:, eo + 16 + f:eo + 17 + f])
            return last
        P.op("vector", fsprime2, reads=["kh_bf", RE], writes=["t2_bf"])

        def fpall(te):
            last = None
            for f in range(NF):
                te.matmul(pbk(pA[0] + f // 4, (f % 4) * 128, 128),
                          kt_bf.ap()[0:64, f * CH:f * CH + CH],
                          rt_bf.ap()[0:64, f * CH:f * CH + CH],
                          start=True, stop=True)
                last = te.matmul(pbk(pA[2] + f // 4, (f % 4) * 128, 128),
                                 kt_bf.ap()[64:128, f * CH:f * CH + CH],
                                 rt_bf.ap()[64:128, f * CH:f * CH + CH],
                                 start=True, stop=True)
            return last
        P.op("tensor", fpall, reads=["kt_bf", "rt_bf"], writes=PA)

        def fmaskall(v):
            last = None
            for f in range(NF):
                v.select(sl(mA_bf, f * 128, 128), umask8.ap(),
                         pbk(pA[0] + f // 4, (f % 4) * 128, 128), umask8.ap())
                last = v.select(sl(mB_bf, f * 128, 128), umask8.ap(),
                                pbk(pA[2] + f // 4, (f % 4) * 128, 128), umask8.ap())
            E60 = 1.1420073898156842e+26
            vguard(v, last)
            v.tensor_scalar_mul(mA_bf.ap(), mA_bf.ap(), E60)
            return v.tensor_scalar_mul(mB_bf.ap(), mB_bf.ap(), E60)
        P.op("vector", fmaskall, reads=["umask8"] + PA,
             writes=["mA", "mB"])

        def fvtil(s):
            last = None
            for f in range(NF):
                s.activation(sl(vtil, f * 128, 64),
                             sl(vtok, tc * 1024 + f * 128, 64),
                             AF.Copy, bias=0.0,
                             scale=D_sb.ap()[:, 2 * f:2 * f + 1])
                last = s.activation(sl(vtil, f * 128 + 64, 64),
                                    sl(vtok, tc * 1024 + f * 128 + 64, 64),
                                    AF.Copy, bias=0.0,
                                    scale=D_sb.ap()[:, 2 * f + 1:2 * f + 2])
            return last
        P.op("scalar", fvtil, reads=["vtok", "D_sb"], writes=["vtil"])

        def foall(te):
            last = None
            for f in range(NF):
                b_ = pB[0] + f // 4
                c0 = (f % 4) * 128
                oo = pbk(b_, c0, 128)
                ooA = AP(pb[b_], c0, [[pbs[b_], 64], [1, 128]])
                ooB = AP(pb[b_], 64 * pbs[b_] + c0, [[pbs[b_], 64], [1, 128]])
                te.matmul(oo, sl(sprime, f * 128, 128),
                          rt_bf.ap()[:, f * CH:f * CH + CH], start=True, stop=False)
                te.matmul(ooA, sl(vtok, tc * 1024 + f * 128, 64),
                          sl(mA_bf, f * 128, 128), start=False, stop=False)
                te.matmul(ooB, sl(vtok, tc * 1024 + f * 128 + 64, 64),
                          sl(mB_bf, f * 128, 128), start=False, stop=False)
                te.matmul(oo, sl(vtil, f * 128, 128), iden.ap(),
                          start=False, stop=True)
                last = te.matmul(pbk(pB[2] + f // 4, (f % 4) * 128, 128),
                                 sl(khtok, f * 128, 128),
                                 sl(vtok, tc * 1024 + f * 128, 128),
                                 start=True, stop=True)
            return last
        P.op("tensor", foall,
             reads=["kh_bf", "rt_bf", "vtil", "iden", "mA",
                    "mB", "khtok", "vtok"],
             writes=PBk)

        def fseC(s):
            last = None
            for f in range(NF):
                last = s.activation(sl(o_fm, f * STG + tc * CH, CH),
                                    pbk(pB[0] + f // 4, (f % 4) * 128, 128),
                                    AF.Copy, bias=0.0, scale=1.0)
            return last
        P.op("scalar", fseC, reads=PBk[0:2], writes=["o_fm:%d" % tc])

        def fseall(v):
            sst = S_bf.ap().ap[0][0]
            s2t = t2_bf.ap().ap[0][0]
            last = None
            for half in range(2):
                for bi in range(2):
                    b_ = pB[2] + bi
                    f0 = bi * 4
                    co = half * 64
                    po = half * 64 * pbs[b_] + co
                    last = v.tensor_add(
                        AP(S_bf, half * 64 * sst + f0 * 128 + co,
                           [[sst, 64], [128, 4], [1, 64]]),
                        AP(t2_bf, half * 64 * s2t + f0 * 128 + co,
                           [[s2t, 64], [128, 4], [1, 64]]),
                        AP(pb[b_], po, [[pbs[b_], 64], [128, 4], [1, 64]]))
            return last
        P.op("vector", fseall, reads=PBk[2:4] + ["t2_bf"],
             writes=["S_bf"])

    def scan_gn(tc, alt=False):
        pA, pB, PA, PBk = scan_banks(tc)
        gns = mixb.ap().bitcast(F32)[:, 0:1024]
        if alt:
            r1v, b2v = rt_bf.ap(), kt_bf.ap()
            RR = ["rt_bf", "kt_bf"]
        else:
            scb = sc_f32.ap().bitcast(BF16)
            r1v, b2v = scb[:, 0:1024], scb[:, 1024:2048]
            RR = ["sc_f32"]
        RO = "o_fm:%d" % tc

        def fo2(s):
            return s.activation(ap3(mB_bf, NF, CH, CH),
                                ap3(o_fm, NF, STG, CH, off=tc * CH),
                                AF.Square, bias=0.0, scale=1.0)
        P.op("scalar", fo2, reads=[RO], writes=["mB"])

        def fgst(te):
            last = None
            for f in range(NF):
                bank = pA[0] + f // 4
                last = te.matmul(pbk(bank, (f % 4) * 128, 128), blko.ap(),
                                 sl(o_fm, f * STG + tc * CH, CH),
                                 start=(f % 4 == 0), stop=(f % 4 == 3))
            for f in range(NF):
                bank = pA[2] + f // 4
                last = te.matmul(pbk(bank, (f % 4) * 128, 128), blko.ap(),
                                 sl(mB_bf, f * CH, CH),
                                 start=(f % 4 == 0), stop=(f % 4 == 3))
            return last
        P.op("tensor", fgst, reads=["blko", RO, "mB"], writes=PA)

        def fgev(v):
            v.tensor_copy(sl(sc2_f32, 0, 512), pbk(pA[0], 0, 512))
            v.tensor_copy(sl(sc2_f32, 512, 512), pbk(pA[1], 0, 512))
            v.tensor_copy(sl(sc2_f32, 1024, 512), pbk(pA[2], 0, 512))
            return v.tensor_copy(sl(sc2_f32, 1536, 512), pbk(pA[3], 0, 512))
        P.op("vector", fgev, reads=PA, writes=["sc2_f32"])

        def fgsq(v):
            v.tensor_mul(gns, sl(sc2_f32, 0, 1024), sl(sc2_f32, 0, 1024))
            v.scalar_tensor_tensor(sl(sc2_f32, 1024, 1024),
                                   sl(sc2_f32, 1024, 1024), 64.0, gns,
                                   ALU.mult, ALU.subtract)
            return v.tensor_single_scalar(sl(sc2_f32, 1024, 1024),
                                          sl(sc2_f32, 1024, 1024), 0.0, ALU.max)
        P.op("vector", fgsq, reads=["mixb"], writes=["sc2_f32", "mixb"])

        def fgln(s):
            s.activation(sl(sc2_f32, 1024, 1024), sl(sc2_f32, 1024, 1024), AF.Ln,
                         bias=cvec.ap()[:, 43:44], scale=1.0)
            return s.activation(sl(sc2_f32, 1024, 1024), sl(sc2_f32, 1024, 1024),
                                AF.Exp, bias=0.0, scale=cvec.ap()[:, 46:47])
        P.op("scalar", fgln, reads=["cvec"], writes=["sc2_f32"])

        def fr1(v):
            v.tensor_scalar_mul(r1v, sl(sc2_f32, 1024, 1024),
                                lnw64.ap()[:, 0:1])
            v.tensor_mul(gns, sl(sc2_f32, 0, 1024), r1v)
            return v.tensor_scalar(b2v, gns, -1.0 / 64.0,
                                   cvec.ap()[:, 41:42], ALU.mult, ALU.add)
        P.op("vector", fr1, reads=["lnw64", "sc2_f32", "cvec"],
             writes=RR + ["mixb"])

        def fon(v):
            og = ap3(og_fm, NF, STG, CH, off=tc * CH)
            v.tensor_mul(og, ap3(o_fm, NF, STG, CH, off=tc * CH), r1v)
            v.tensor_add(og, og, b2v)
            return v.tensor_mul(og, og,
                                ap3(r_bf, NF, STG, CH, off=tc * CH))
        P.op("vector", fon, reads=[RO, "r_bf"] + RR,
             writes=["og_fm"])

        if tap:
            ci = tapn[0]
            tapn[0] += 1

            _tt = [(nm, t_, cols, rr) for nm, t_, cols, rr in
                   (("t_ofm", o_fm, 1024, "o_fm:%d" % tc),
                    ("t_S", S_bf, 1024, "S_bf"),
                    ("t_kt", kt_bf, 1024, "kt_bf"),
                    ("t_mA", mA_bf, 1024, "mA"),
                    ("t_mB", mB_bf, 1024, "mB"),
                    ("t_og", og_fm, 2048, "og_fm"),
                    ("t_var", sc2_f32, 1024, "sc2_f32")) if nm in tapd]

            def ftap(g, tc=tc, ci=ci, _tt=_tt):
                outs = []
                for nm, t_, cols, _ in _tt:
                    nel = 128 * cols
                    ia = (t_.ap()[:, 0:cols] if nm != "t_var"
                          else t_.ap()[:, 1024:2048])
                    if nm == "t_ofm":
                        ia = ap3(t_, NF, STG, CH, off=tc * CH)
                    oa = AP(tapd[nm].tensor, ci * nel,
                            [[cols, 128], [1, cols]])
                    outs.append(g.dma_start(out=oa, in_=ia))
                return outs
            P.dma("gpsimd", ftap, n=len(_tt),
                  reads=[rr for _, _, _, rr in _tt],
                  writes=["tap%d" % ci])
    def emit_front(b, h, st):
        t0 = h * half_tokens
        st_t0 = t0 + st * STG
        base_tc = st * 2
        RC = ["resid:%d" % (base_tc + tc) for tc in range(2)]

        def fx(g, st_t0=st_t0, base_tc=base_tc, b=b):
            ia = AP(x_in.tensor, b * T * D + st_t0 * D,
                    [[1024, 128], [131072, 2], [1, 1024]])
            oa = AP(resid, base_tc * 1024,
                    [[resid.ap().ap[0][0], 128], [1024, 2], [1, 1024]])
            return [g.dma_start(out=oa, in_=ia)]
        P.dma("gpsimd", fx, writes=RC)

        emit_ln([sl(resid, (base_tc + tc) * 1024, 1024) for tc in range(2)], RC)
        emit_xn_transposes()
        if h == 0 and st == 0:
            P.op("vector", lambda v: v.memset(prevc.ap(), 0.0), writes=["prevc"])

        def fsh2(v):
            v.tensor_copy(ap3(xsh_fm, NF, STG, STG - 1, off=1),
                          ap3(xn_fm, NF, STG, STG - 1, off=0))
            v.tensor_copy(strided(xsh_fm, 0, STG, NF), prevc.ap())
            v.tensor_copy(prevc.ap(), strided(xn_fm, STG - 1, STG, NF))
            return v.tensor_sub(mixtmp.ap(), xn_fm.ap(), xsh_fm.ap())
        P.op("vector", fsh2, reads=["xfA", "xfB"],
             writes=["xsh_fm", "prevc", "mixtmp"])

    def phase_A(b, h, first=False, p=0):
        t0 = h * half_tokens
        offs = attn_offs(p)
        wtv = {nm: wview(offs[nm], 8192) for nm in "rkvwo"}
        rres = {nm: REGN[offs[nm]] for nm in "rkvwo"}
        emit_attn_weight_loads(p)
        if h == 0:
            P.op("vector", lambda v: v.memset(S_bf.ap(), 0.0), writes=["S_bf"])

        if first:
            emit_front(b, h, 0)
        for st in range(NSTG):
            base_tc = st * 2
            RC = ["resid:%d" % (base_tc + tc) for tc in range(2)]

            cvst = cvec.ap().ap[0][0]

            def emit_mix(coff, dst, dstres):
                def fmix(v):
                    last = None
                    for f in range(NF):
                        last = v.scalar_tensor_tensor(
                            sl(dst, f * STG, STG), sl(mixtmp, f * STG, STG),
                            cvec.ap()[:, coff + f:coff + f + 1],
                            sl(xsh_fm, f * STG, STG), ALU.mult, ALU.add)
                    return last
                P.op("vector", fmix, reads=["mixtmp", "cvec", "xsh_fm"],
                     writes=[dstres])

            def evac_r(banks, n):
                def fs(s):
                    last = None
                    for fo in range(NF):
                        last = s.activation(sl(r_bf, fo * STG, n),
                                            pbk(banks[fo // 2], (fo % 2) * n, n),
                                            AF.Sigmoid, bias=0.0, scale=1.0)
                    return last
                P.op("scalar", fs, reads=[PB[b_] for b_ in banks], writes=["r_bf"])
            emit_mix(0, mixb, "mixb")
            emit_proj_fm(wtv["r"], rres["r"], mixb, "mixb", [0, 1, 2, 3], evac_r)

            def evac_k(banks, n):
                def fv(v):
                    last = None
                    for fo in range(NF):
                        last = v.tensor_copy(sl(k_bf, fo * STG, n),
                                             pbk(banks[fo // 2], (fo % 2) * n, n))
                    return last
                P.op("vector", fv, reads=[PB[b_] for b_ in banks], writes=["k_bf"])

                def fs(s):
                    last = None
                    for fo in range(NF):
                        last = s.activation(sl(eu_bf, fo * STG, n),
                                            pbk(banks[fo // 2], (fo % 2) * n, n),
                                            AF.Exp,
                                            bias=cvec.ap()[:, 32 + fo:32 + fo + 1],
                                            scale=1.0)
                    return last
                P.op("scalar", fs, reads=[PB[b_] for b_ in banks] + ["cvec"],
                     writes=["euA", "euB"])
            emit_mix(8, og_fm, "og_fm")
            emit_proj_fm(wtv["k"], rres["k"], og_fm, "og_fm", [4, 5, 6, 7], evac_k)

            def evac_w(banks, n):
                def fs(s):
                    last = None
                    for fo in range(NF):
                        last = s.activation(sl(sc_f32, fo * n, n),
                                            pbk(banks[fo // 2], (fo % 2) * n, n),
                                            AF.Exp, bias=0.0, scale=cvec.ap()[:, 45:46])
                    return last
                P.op("scalar", fs, reads=[PB[b_] for b_ in banks] + ["cvec"],
                     writes=["sc_f32"])
                P.op("scalar", lambda s: s.activation(
                    sc_f32.ap(), sc_f32.ap(), AF.Ln, bias=cvec.ap()[:, 44:45],
                    scale=1.0), reads=["cvec"], writes=["sc_f32"])
                P.op("vector", lambda v: v.tensor_scalar_add(
                    sc_f32.ap(), sc_f32.ap(), 1e-4), writes=["sc_f32"])
            emit_mix(24, mixb, "mixb")
            emit_proj_fm(wtv["w"], rres["w"], mixb, "mixb", [0, 1, 2, 3], evac_w)

            emit_mix(16, og_fm, "og_fm")

            def fvproj(te):
                last = None
                for tc in range(2):
                    for hlf in range(2):
                        outap = pbk(4 + tc * 2 + hlf, 0, 512)
                        for dc in range(NF):
                            last = te.matmul(
                                outap, sl(og_fm, dc * STG + tc * CH, 128),
                                wtv["v"][:, dc * 1024 + hlf * 512: dc * 1024 + hlf * 512 + 512],
                                start=(dc == 0), stop=(dc == NF - 1))
                return last
            P.op("tensor", fvproj, reads=["og_fm", rres["v"]],
                 writes=[PB[4], PB[5], PB[6], PB[7]])

            def fvevac(s):
                last = None
                for tc in range(2):
                    for hlf in range(2):
                        last = s.activation(sl(vtok, tc * 1024 + hlf * 512, 512),
                                            pbk(4 + tc * 2 + hlf, 0, 512),
                                            AF.Copy, bias=0.0, scale=1.0)
                return last
            P.op("scalar", fvevac, reads=[PB[4], PB[5], PB[6], PB[7]],
                 writes=["vtok"])

            scan_front(0)
            scan_main(0)
            scan_front(1)
            scan_gn(0)
            scan_main(1)
            if st + 1 < NSTG:
                emit_front(b, h, st + 1)
            scan_gn(1, alt=(st == NSTG - 1))

            def foproj(te):
                last = None
                for tc in range(2):
                    for hlf in range(2):
                        outap = pbk(tc * 2 + hlf, 0, 512)
                        for dc in range(NF):
                            last = te.matmul(
                                outap,
                                sl(og_fm, dc * STG + tc * CH, 128),
                                wtv["o"][:, dc * 1024 + hlf * 512: dc * 1024 + hlf * 512 + 512],
                                start=(dc == 0), stop=(dc == NF - 1))
                return last
            P.op("tensor", foproj, reads=[rres["o"], "og_fm"], writes=PB[0:4])

            def fradd(v, base_tc=base_tc):
                last = None
                for tc in range(2):
                    for hlf in range(2):
                        c0 = (base_tc + tc) * 1024 + hlf * 512
                        last = v.tensor_add(sl(resid, c0, 512), sl(resid, c0, 512),
                                            pbk(tc * 2 + hlf, 0, 512))
                return last
            P.op("vector", fradd, reads=PB[0:4] + RC, writes=RC)

    XN2H = [(sc_f32, 0), (sc_f32, 2048), (chunk_f32, 0), (chunk_f32, 2048)]
    XN2R = ["sc_f32", "sc_f32", "chunk_f32", "chunk_f32"]

    def xn2v(st):
        t, c0 = XN2H[st]
        a = t.ap() if t.ap().dtype == BF16 else t.ap().bitcast(BF16)
        return a[:, c0:c0 + 2048]

    def expert_slots(e):
        if e % 2 == 0:
            return OFF_W1, OFF_W2, R_W1, R_W2
        return 0, 8192, "wt_r", "wt_k"

    def emit_expert_load(e):
        off1, off2, r1, r2 = expert_slots(e)
        P.dma("gpsimd", load_mat_1024(off1, d_w1, src_elem_off=e * D * D),
              writes=[r1])
        P.dma("gpsimd", load_mat_1024(off2, d_w2, src_elem_off=e * D * D),
              writes=[r2])

    def phase_B_router(st):
        xn2s = xn2v(st)

        def frt(te):
            last = None
            for tc2 in range(2):
                for dc in range(NF):
                    last = te.matmul(
                        pbk(1 + tc2, 0, 8),
                        xn2s[:, dc * STG + tc2 * CH:dc * STG + tc2 * CH + 128],
                        rtrsb[:, dc * 8:dc * 8 + 8],
                        start=(dc == 0), stop=(dc == NF - 1))
            return last
        P.op("tensor", frt, reads=["rtr", XN2R[st]], writes=[PB[1], PB[2]])

        def frte(v):
            v.tensor_copy(lg_tok.ap()[:, 0:8], pbk(1, 0, 8))
            return v.tensor_copy(lg_tok.ap()[:, 8:16], pbk(2, 0, 8))
        P.op("vector", frte, reads=[PB[1], PB[2]], writes=["lg_tok"])

        def fmax(v):
            v.reduce_max(ssum.ap()[:, 0:1], lg_tok.ap()[:, 0:8], AXX, negate=True)
            return v.reduce_max(ssum.ap()[:, 2:3], lg_tok.ap()[:, 8:16], AXX,
                                negate=True)
        P.op("vector", fmax, reads=["lg_tok"], writes=["ssum"])

        def fex(s):
            s.activation(esm.ap()[:, 0:8], lg_tok.ap()[:, 0:8], AF.Exp,
                         bias=ssum.ap()[:, 0:1], scale=1.0,
                         accum_out=ssum.ap()[:, 1:2])
            return s.activation(esm.ap()[:, 8:16], lg_tok.ap()[:, 8:16], AF.Exp,
                                bias=ssum.ap()[:, 2:3], scale=1.0,
                                accum_out=ssum.ap()[:, 3:4])
        P.op("scalar", fex, reads=["lg_tok"], writes=["esm", "ssum"])

        def ftp(v, st=st):
            ea = esm.ap()[:, 0:8]
            eb = esm.ap()[:, 8:16]
            ta = top8.ap()[:, 0:8]
            tb = top8.ap()[:, 8:16]
            ga = lg_tok.ap()[:, 0:8]
            gb = lg_tok.ap()[:, 8:16]
            v.max(ta, ea)
            vguard(v, v.max(tb, eb))
            v.reciprocal(ssum.ap()[:, 0:1], ssum.ap()[:, 1:2])
            v.reciprocal(ssum.ap()[:, 2:3], ssum.ap()[:, 3:4])
            v.tensor_single_scalar(ga, ea, top8.ap()[:, 1:2], ALU.is_ge)
            vguard(v, v.tensor_single_scalar(gb, eb, top8.ap()[:, 9:10],
                                             ALU.is_ge))
            v.tensor_mul(ga, ga, ea)
            vguard(v, v.tensor_mul(gb, gb, eb))
            v.tensor_scalar_mul(cmb_all.ap()[:, st * 16:st * 16 + 8], ga,
                                ssum.ap()[:, 0:1])
            return v.tensor_scalar_mul(cmb_all.ap()[:, st * 16 + 8:st * 16 + 16],
                                       gb, ssum.ap()[:, 2:3])
        P.op("vector", ftp, reads=["esm"],
             writes=["lg_tok", "top8", "ssum", "cmb_all"])

    def phase_B_front(b, h, st):
        base_tc = st * 2
        RC = ["resid:%d" % (base_tc + tc) for tc in range(2)]
        emit_ln([sl(resid, (base_tc + tc) * 1024, 1024) for tc in range(2)], RC)
        host, c0 = XN2H[st]
        hst = host.ap().ap[0][0]
        tst = xn_tok.ap().ap[0][0]
        for tc in range(2):
            def ftr(sy, tc=tc):
                oa = AP(host, c0 // 2 + tc * 64,
                        [[hst, 128], [128, NF], [1, 64]]).bitcast(BF16)
                ia = AP(xn_tok, tc * 1024, [[tst, 128], [128, NF], [1, 128]])
                return [sy.dma_start(out=oa, in_=ia, transpose=True)]
            P.dma("sync", ftr, n=1, reads=["xn_tok"], writes=[XN2R[st]])

    def phase_B_stage(b, h, st):
        base_tc = st * 2
        RC = ["resid:%d" % (base_tc + tc) for tc in range(2)]
        xn2s = xn2v(st)

        def fgc(te):
            last = None
            for tc2 in range(2):
                for dc in range(NF):
                    last = te.matmul(
                        pbk(5, tc2 * 256, 1),
                        xn2s[:, dc * STG + tc2 * CH:dc * STG + tc2 * CH + 128],
                        sgwsb[:, dc:dc + 1],
                        start=(dc == 0), stop=(dc == NF - 1))
            return last
        P.op("tensor", fgc, reads=["sgw", XN2R[st]], writes=[PB[5]])

        def fgsig(s, st=st):
            s.activation(gcol.ap()[:, st * 2:st * 2 + 1], pbk(5, 0, 1),
                         AF.Sigmoid, bias=0.0, scale=1.0)
            return s.activation(gcol.ap()[:, st * 2 + 1:st * 2 + 2],
                                pbk(5, 256, 1), AF.Sigmoid, bias=0.0, scale=1.0)
        P.op("scalar", fgsig, reads=[PB[5]], writes=["gcol"])

        hb = eu_bf if st % 2 == 0 else xn_fm
        hres = ["euA", "euB"] if st % 2 == 0 else ["xfA", "xfB"]
        tb, tbr = (mixb, "mixb") if st % 2 == 0 else (og_fm, "og_fm")

        def fgu(te):
            last = None
            for f2 in range(16):
                outap = pbk(f2 // 2, (f2 % 2) * 256, 256)
                for dc in range(NF):
                    last = te.matmul(
                        outap,
                        sgusb[:, dc * 2048 + f2 * 128: dc * 2048 + f2 * 128 + 128],
                        xn2s[:, dc * STG:dc * STG + STG],
                        start=(f2 % 2 == 0 and dc == 0),
                        stop=(f2 % 2 == 1 and dc == NF - 1))
            return last
        P.op("tensor", fgu, reads=list(R_SGU) + [XN2R[st]], writes=PB[:])

        def fsg(s):
            last = None
            for f in range(NF):
                last = s.activation(sl(hb, f * STG, STG),
                                    pbk(f // 2, (f % 2) * 256, 256),
                                    AF.Silu, bias=0.0, scale=1.0)
            return last
        P.op("scalar", fsg, reads=PB[0:4], writes=hres)

        def fup(v):
            last = None
            for f in range(NF):
                last = v.tensor_mul(sl(tb, f * STG, STG), sl(hb, f * STG, STG),
                                    pbk(4 + f // 2, (f % 2) * 256, 256))
            return last
        P.op("vector", fup, reads=hres + PB[4:8], writes=[tbr])

        def fsdn(te):
            last = None
            for tc2 in range(2):
                for hlf in range(2):
                    outap = pbk(tc2 * 2 + hlf, 0, 512)
                    for fc in range(NF):
                        last = te.matmul(
                            outap, sl(tb, fc * STG + tc2 * CH, 128),
                            sdnsb[:, fc * 1024 + hlf * 512: fc * 1024 + hlf * 512 + 512],
                            start=(fc == 0), stop=(fc == NF - 1))
            return last
        P.op("tensor", fsdn, reads=[R_SDN, tbr], writes=PB[0:4])

        def fsadd(v, st=st):
            last = None
            for tc2 in range(2):
                for hlf in range(2):
                    c0 = (base_tc + tc2) * 1024 + hlf * 512
                    last = v.scalar_tensor_tensor(
                        sl(resid, c0, 512), pbk(tc2 * 2 + hlf, 0, 512),
                        gcol.ap()[:, st * 2 + tc2:st * 2 + tc2 + 1],
                        sl(resid, c0, 512), ALU.mult, ALU.add)
            return last
        P.op("vector", fsadd, reads=PB[0:4] + RC + ["gcol"], writes=RC)

    def phase_B_expert(e, b=0, h=0, nxt=None):
        off1, off2, r1, r2 = expert_slots(e)
        final = (e == E - 1)
        w1v = wview(off1, 8192)
        w2v = wview(off2, 8192)
        for st in range(NSTG):
            xn2s = xn2v(st)
            base_tc = st * 2
            RC = ["resid:%d" % (base_tc + tc) for tc in range(2)]
            hb = eu_bf if st % 2 == 0 else xn_fm
            hres = ["euA", "euB"] if st % 2 == 0 else ["xfA", "xfB"]

            for half in range(2):
                def fw1h(te, half=half, xn2s=xn2s, w1v=w1v):
                    last = None
                    for fc in range(half * 4, half * 4 + 4):
                        outap = pbk(4 + fc // 2, (fc % 2) * 256, 256)
                        for dc in range(NF):
                            last = te.matmul(
                                outap,
                                w1v[:, dc * 1024 + fc * 128: dc * 1024 + fc * 128 + 128],
                                xn2s[:, dc * STG:dc * STG + STG],
                                start=(fc % 2 == 0 and dc == 0),
                                stop=(fc % 2 == 1 and dc == NF - 1))
                    return last
                P.op("tensor", fw1h, reads=[r1, XN2R[st]],
                     writes=PB[4 + 2 * half:6 + 2 * half])

                def fsih(s, half=half, hb=hb):
                    last = None
                    for fc in range(half * 4, half * 4 + 4):
                        last = s.activation(sl(hb, fc * STG, STG),
                                            pbk(4 + fc // 2, (fc % 2) * 256, 256),
                                            AF.Silu, bias=0.0, scale=1.0)
                    return last
                P.op("scalar", fsih, reads=PB[4 + 2 * half:6 + 2 * half],
                     writes=[hres[half]])

            for half in range(2):
                def fw2p(te, half=half, w2v=w2v, hb=hb):
                    last = None
                    for tc2 in range(2):
                        for hlf in range(2):
                            outap = pbk(tc2 * 2 + hlf, 0, 512)
                            for fc in range(half * 4, half * 4 + 4):
                                last = te.matmul(
                                    outap, sl(hb, fc * STG + tc2 * CH, 128),
                                    w2v[:, fc * 1024 + hlf * 512: fc * 1024 + hlf * 512 + 512],
                                    start=(half == 0 and fc == 0),
                                    stop=(half == 1 and fc == NF - 1))
                    return last
                P.op("tensor", fw2p, reads=[r2, hres[half]], writes=PB[0:4])

            def fadd(v, base_tc=base_tc, e=e, st=st):
                last = None
                for tc2 in range(2):
                    for hlf in range(2):
                        c0 = (base_tc + tc2) * 1024 + hlf * 512
                        last = v.scalar_tensor_tensor(
                            sl(resid, c0, 512), pbk(tc2 * 2 + hlf, 0, 512),
                            cmb_all.ap()[:, st * 16 + tc2 * 8 + e:
                                         st * 16 + tc2 * 8 + e + 1],
                            sl(resid, c0, 512), ALU.mult, ALU.add)
                return last
            P.op("vector", fadd, reads=PB[0:4] + RC + ["cmb_all"], writes=RC)

            if final:
                t0 = h * half_tokens

                def fst(g, st=st, t0=t0, b=b):
                    st_t0 = t0 + st * STG
                    oa = AP(y_out.tensor, b * T * D + st_t0 * D,
                            [[1024, 128], [131072, 2], [1, 1024]])
                    ia = AP(resid, st * 2 * 1024,
                            [[resid.ap().ap[0][0], 128], [1024, 2], [1, 1024]])
                    return [g.dma_start(out=oa, in_=ia)]
                P.dma("gpsimd", fst, reads=RC,
                      writes=["y:%d:%d:%d" % (b, h, st)])
                if st == 0 and nxt is not None:
                    emit_front(nxt[0], nxt[1], 0)

    def phase_B(b, h, nxt=None):
        emit_ffn_shared_loads()
        emit_expert_load(0)
        for st in range(NSTG):
            phase_B_front(b, h, st)
        for st in range(NSTG):
            phase_B_stage(b, h, st)
        for st in range(NSTG):
            phase_B_router(st)
        for e in range(E):
            if e + 1 < E:
                emit_expert_load(e + 1)
            phase_B_expert(e, b=b, h=h, nxt=nxt)

    bh = [(b, h) for b in range(nseq) for h in range(NH)]
    for i, (b, h) in enumerate(bh):
        phase_A(b, h, first=(i == 0), p=i % 2)
        phase_B(b, h, nxt=(bh[i + 1] if i + 1 < len(bh) else None))

    P.finalize()
    return nc


def make_host_inputs(inp):
    tT = lambda a: np.ascontiguousarray(np.asarray(a).T)
    cN = np.ascontiguousarray
    iden = np.eye(128, dtype=BF)
    umask = np.triu(np.ones((128, 128), dtype=np.float32), 1)
    blko = np.zeros((128, 128), dtype=np.float32)
    blko[0:64, 0:64] = 1.0
    blko[64:128, 64:128] = 1.0
    hdo = np.zeros((128, 128), dtype=np.float32)
    for f in range(8):
        hdo[0:64, f * 16 + 2 * f] = 1.0
        hdo[64:128, f * 16 + 2 * f + 1] = 1.0
    cvec = np.zeros((128, 52), dtype=np.float32)
    cvec[:, 47] = -30.0
    cvec[:, 48] = 30.0
    for i, nm in enumerate(("tm_r", "tm_k", "tm_v", "tm_w")):
        cvec[:, i * 8:(i + 1) * 8] = np.asarray(inp["L0_attn_" + nm]).reshape(8, 128).T
    cvec[:, 32:40] = np.asarray(inp["L0_attn_u"]).reshape(-1).reshape(8, 128).T
    cvec[:, 40] = np.tile(np.asarray(inp["L0_attn_ln_w"]), 2)
    cvec[:, 41] = np.tile(np.asarray(inp["L0_attn_ln_b"]), 2)
    cvec[:, 42] = 1e-5
    cvec[:, 43] = 4096.0 * 1e-5
    cvec[:, 44] = 1.0
    cvec[:, 45] = -1.0
    cvec[:, 46] = -0.5
    shared = {
        "wrT": tT(inp["L0_attn_W_r"]), "wkT": tT(inp["L0_attn_W_k"]),
        "wvT": tT(inp["L0_attn_W_v"]), "wwT": tT(inp["L0_attn_W_w"]),
        "woT": tT(inp["L0_attn_W_o"]),
        "w1": cN(np.asarray(inp["L0_ffn_w1"])), "w2": cN(np.asarray(inp["L0_ffn_w2"])),
        "sgu": cN(np.asarray(inp["L0_ffn_shared_gate_up"])),
        "sdn": cN(np.asarray(inp["L0_ffn_shared_down"])),
        "rtr": cN(np.asarray(inp["L0_ffn_router_w"])),
        "sgw": cN(np.asarray(inp["L0_ffn_shared_gate_w"])),
        "idn": iden, "um8": umask.astype(np.uint8),
        "idf": np.eye(128, dtype=np.float32), "blk": blko.astype(BF), "hdo": hdo.astype(BF),
        "on1": np.ones((1, 128), dtype=BF), "cvc": cvec,
    }
    x = np.asarray(inp["L0_attn_x"])
    xs = [cN(x[c * 2:(c + 1) * 2]) for c in range(NCORE)]
    return shared, xs


def kernel(**inputs):
    shared, xs = make_host_inputs(inputs)
    nc = build_nc()
    in_maps = [dict(shared, x=xs[c]) for c in range(NCORE)]
    res = run_bass_kernel_spmd(nc, in_maps, core_ids=list(range(NCORE)))
    out = np.concatenate([res.results[c]["y"] for c in range(NCORE)], axis=0)
    return np.ascontiguousarray(out.astype(np.float32))


# version 24
# speedup vs baseline: 1.1693x; 1.0207x over previous
import numpy as np
import ml_dtypes
import concourse.bass as bass
import concourse.mybir as mybir
from concourse.bass_utils import run_bass_kernel_spmd
from concourse.ap import AP

F32 = mybir.dt.float32
BF16 = mybir.dt.bfloat16
AF = mybir.ActivationFunctionType
ALU = mybir.AluOpType
AXX = mybir.AxisListType.X

NCORE = 8
D = 1024
NF = 8
E = 8
EPS = 1e-5
STG = 256
CH = 128
BF = ml_dtypes.bfloat16

CONSTS = ("iden", "umask8", "idenf", "blko", "hdo", "ones1", "cvec")


class Prog:

    ENGS = ("sync", "gpsimd", "tensor", "vector", "scalar")
    NLANES = 12

    def __init__(self, nc, selfsync=False):
        self.nc = nc
        self.selfsync = selfsync
        if selfsync:
            self.ssem = {e: nc.alloc_semaphore(name="ss_" + e) for e in self.ENGS}
            self.sctr = {e: [0] for e in self.ENGS}
        self.q = {e: [] for e in self.ENGS}
        self.sem = {e: nc.alloc_semaphore(name="s_" + e) for e in self.ENGS}
        self.cnt = {e: 0 for e in self.ENGS}
        self.lane_pool = {"gpsimd": list(range(8)), "sync": list(range(8, 12))}
        self.lane_sem = [nc.alloc_semaphore(name="ln%d" % i)
                         for i in range(self.NLANES)]
        self.lane_cnt = [0] * self.NLANES
        self.lane_rr = {"gpsimd": 0, "sync": 0}
        self.vclock = {e: {} for e in self.ENGS}
        self.res = {}

    def _handle(self, sk):
        return self.sem[sk] if isinstance(sk, str) else self.lane_sem[sk]

    def _wrap(self, eng, engine):
        if not self.selfsync:
            return engine
        sem, ctr = self.ssem[eng], self.sctr[eng]

        class _Ser:
            def __getattr__(_s, nm):
                a = getattr(engine, nm)
                if not callable(a):
                    return a

                def call(*args, **kw):
                    r = a(*args, **kw)
                    if (hasattr(r, "then_inc") and nm != "wait_ge"
                            and not nm.startswith("dma_start")
                            and not nm.startswith("sem_")):
                        ctr[0] += 1
                        r.then_inc(sem, 1)
                        engine.wait_ge(sem, ctr[0])
                    return r
                return call
        return _Ser()

    def _dep_points(self, reads, writes):
        pts = []
        for r in reads:
            e = self.res.get(r)
            if e is None or e["w"] is None:
                raise KeyError(f"read of resource {r!r} with no prior writer")
            pts.append(e["w"])
        for w in writes:
            e = self.res.get(w)
            if e is not None:
                if e["w"] is not None:
                    pts.append(e["w"])
                pts.extend(e["r"])
        return pts

    def _plan(self, eng, pts):
        vc = self.vclock[eng]
        need = {}
        for sk, val, _ in pts:
            if need.get(sk, -1) < val:
                need[sk] = val
        waits = []
        for sk, val in sorted(need.items(), key=str):
            if sk == eng and val <= self.cnt[eng]:
                continue
            if vc.get(sk, 0) >= val:
                continue
            waits.append((self._handle(sk), val))
        for sk, val, pvc in pts:
            for k2, v2 in pvc.items():
                if vc.get(k2, 0) < v2:
                    vc[k2] = v2
            if vc.get(sk, 0) < val:
                vc[sk] = val
        return waits

    def _update_res(self, reads, writes, pt):
        for w in writes:
            self.res[w] = {"w": pt, "r": []}
        for r in reads:
            e = self.res.get(r)
            if e is None:
                e = self.res[r] = {"w": None, "r": []}
            e["r"].append(pt)

    def op(self, eng, fn, reads=(), writes=()):
        pts = self._dep_points(reads, writes)
        waits = self._plan(eng, pts)
        self.cnt[eng] += 1
        myval = self.cnt[eng]
        sem = self.sem[eng]

        def run(engine, waits=waits, fn=fn, eng=eng):
            for h, v in waits:
                engine.wait_ge(h, v)
            last = fn(self._wrap(eng, engine))
            assert last is not None, "op fn must return its last instruction"
            if self.selfsync:
                engine.sem_inc(sem, 1)
            else:
                last.then_inc(sem, 1)

        self.q[eng].append(run)
        pt = (eng, myval, dict(self.vclock[eng]))
        self._update_res(reads, writes, pt)

    def dma(self, eng, fn, n=1, reads=(), writes=()):
        pts = self._dep_points(reads, writes)
        pool = self.lane_pool[eng]
        lane = pool[self.lane_rr[eng]]
        self.lane_rr[eng] = (self.lane_rr[eng] + 1) % len(pool)
        if self.lane_cnt[lane] > 0:
            pts = pts + [(lane, self.lane_cnt[lane], {})]
        waits = self._plan(eng, pts)
        self.lane_cnt[lane] += 16 * n
        target = self.lane_cnt[lane]
        sem = self.lane_sem[lane]

        def run(engine, waits=waits, fn=fn, n=n, eng=eng):
            for h, v in waits:
                engine.wait_ge(h, v)
            insts = fn(self._wrap(eng, engine))
            assert len(insts) == n, f"expected {n} DMAs, got {len(insts)}"
            for i in insts:
                i.then_inc(sem, 16)

        self.q[eng].append(run)
        pt = (lane, target, dict(self.vclock[eng]))
        self._update_res(reads, writes, pt)

    def finalize(self):
        with self.nc.Block() as block:
            final = ([(self.sem[e], self.cnt[e]) for e in self.ENGS
                      if self.cnt[e] > 0] +
                     [(self.lane_sem[i], self.lane_cnt[i])
                      for i in range(self.NLANES) if self.lane_cnt[i] > 0])
            for name in self.ENGS:
                ops = self.q[name]

                def body(engine, ops=ops):
                    for op in ops:
                        op(engine)
                    for s, v in final:
                        engine.wait_ge(s, v)

                getattr(block, name)(body)


def build_nc(T=2048, nseq=2, half_tokens=1024, race=False, tap=False, tapset=None):
    NH = T // half_tokens
    NSTG = half_tokens // STG
    NTC = half_tokens // CH

    nc = bass.Bass("TRN2", target_bir_lowering=False,
                   detect_race_conditions=race)

    x_in = nc.dram_tensor("x", [nseq, T, D], F32, kind="ExternalInput").ap()
    y_out = nc.dram_tensor("y", [nseq, T, D], F32, kind="ExternalOutput").ap()
    d_wT = {nm: nc.dram_tensor("w%sT" % nm, [D, D], F32, kind="ExternalInput").ap()
            for nm in "rkvwo"}
    d_w1 = nc.dram_tensor("w1", [E, D, D], F32, kind="ExternalInput").ap()
    d_w2 = nc.dram_tensor("w2", [E, D, D], F32, kind="ExternalInput").ap()
    d_sgu = nc.dram_tensor("sgu", [D, 2 * D], F32, kind="ExternalInput").ap()
    d_sdn = nc.dram_tensor("sdn", [D, D], F32, kind="ExternalInput").ap()
    d_rtr = nc.dram_tensor("rtr", [D, E], F32, kind="ExternalInput").ap()
    d_sgw = nc.dram_tensor("sgw", [D, 1], F32, kind="ExternalInput").ap()
    d_iden = nc.dram_tensor("idn", [128, 128], BF16, kind="ExternalInput").ap()
    d_um8 = nc.dram_tensor("um8", [128, 128], mybir.dt.uint8, kind="ExternalInput").ap()
    d_idenf = nc.dram_tensor("idf", [128, 128], F32, kind="ExternalInput").ap()
    d_blko = nc.dram_tensor("blk", [128, 128], BF16, kind="ExternalInput").ap()
    d_hdo = nc.dram_tensor("hdo", [128, 128], BF16, kind="ExternalInput").ap()
    d_ones = nc.dram_tensor("on1", [1, 128], BF16, kind="ExternalInput").ap()
    d_cvec = nc.dram_tensor("cvc", [128, 52], F32, kind="ExternalInput").ap()

    A = lambda n, c, dt: nc.alloc_sbuf_tensor(n, [128, c], dt)
    resid = A("resid", NTC * 1024, F32)
    WPC = max(5 * 8192, 8 * 2048 + 3 * 8192 + 64 + 8)
    wpool = A("wpool", WPC, BF16)
    wps = wpool.ap().ap[0][0]

    def wview(off, cols):
        return AP(wpool, off, [[wps, 128], [1, cols]])

    wT = {nm: wview(i * 8192, 8192) for i, nm in enumerate("rkvwo")}
    OFF_SGU, OFF_W1, OFF_W2, OFF_SDN = 0, 8 * 2048, 8 * 2048 + 8192, 8 * 2048 + 2 * 8192
    OFF_RTR, OFF_SGW = 8 * 2048 + 3 * 8192, 8 * 2048 + 3 * 8192 + 64
    sgusb = wview(OFF_SGU, 8 * 2048)
    sdnsb = wview(OFF_SDN, 8192)
    rtrsb = wview(OFF_RTR, 64)
    sgwsb = wview(OFF_SGW, 8)
    R_SGU = ("wt_r", "wt_k")
    R_W1, R_W2, R_SDN = "wt_v", "wt_w", "wt_o"

    iden = A("iden", 128, BF16)
    umask8 = A("umask8", 128, mybir.dt.uint8)
    idenf = A("idenf", 128, F32)
    blko = A("blko", 128, BF16)
    hdo = A("hdo_sb", 128, BF16)
    ones1 = nc.alloc_sbuf_tensor("ones1", [1, 128], BF16)
    cvec = A("cvec", 52, F32)
    lnw64 = A("lnw64", 1, F32)

    xn_tok = A("xn_tok", 2048, BF16)
    xn_fm = A("xn_fm", 2048, BF16)
    xsh_fm = A("xsh_fm", 2048, BF16)
    mixb = A("mixb", 2048, BF16)
    tm1m = A("tm1m", 32, F32)
    r_bf = A("r_bf", 2048, BF16)
    k_bf = A("k_bf", 2048, BF16)
    eu_bf = A("eu_bf", 2048, BF16)
    sc_f32 = A("sc_f32", 2048, F32)
    sc2_f32 = A("sc2_f32", 2048, F32)
    chunk_f32 = A("chunk_f32", 2048, F32)
    expr_bf = A("expr_bf", 1024, BF16)
    rt_bf = A("rt_bf", 1024, BF16)
    kt_bf = A("kt_bf", 1024, BF16)
    kh_bf = A("kh_bf", 1024, BF16)
    khtok = A("khtok", 1024, BF16)
    t2_bf = A("t2_bf", 1024, BF16)
    vtok = A("vtok", 2048, BF16)
    mixtmp = A("mixtmp", 2048, BF16)
    vtil = A("vtil", 1024, BF16)
    mA_bf = A("mA_bf", 1024, BF16)
    mB_bf = A("mB_bf", 1024, BF16)
    S_bf = A("S_bf", 1024, BF16)
    eA_f32 = A("eA_f32", 48, F32)
    mcol_f32 = A("mcol_f32", 96, F32)
    D_sb = A("D_sb", 16, F32)
    o_fm = A("o_fm", 2048, BF16)
    og_fm = A("og_fm", 2048, BF16)
    prevc = A("prevc", 8, BF16)
    stat6 = A("stat6", 24, F32)
    mv = A("mv", 8, F32)
    o2_bf = expr_bf
    r1_bf = rt_bf
    b2_bf = kt_bf
    on_bf = kh_bf
    cmb_all = A("cmb_all", 64, BF16)
    gcol = A("gcol", 8, F32)
    lg_tok = A("lg_tok", 16, F32)
    esm = A("esm", 16, F32)
    top8 = A("top8", 16, F32)
    ssum = A("ssum", 4, F32)
    lgfm = nc.alloc_sbuf_tensor("lgfm", [8, 256], F32)

    pb = [nc.alloc_psum_tensor("pb%d" % i, [128, 512], F32) for i in range(8)]
    pbs = [p.ap().ap[0][0] for p in pb]
    PB = ["pb%d" % i for i in range(8)]

    def pbk(i, c0, n, parts=128):
        return AP(pb[i], c0, [[pbs[i], parts], [1, n]])

    def pbk_bf(i, c0, n, parts=128):
        return AP(pb[i], 0, [[pbs[i], parts], [1, 512]]).bitcast(BF16)[:, c0:c0 + n]

    tapd = {}
    tapn = [0]
    if tap:
        NCHUNK = nseq * T // CH
        _all = (("t_ofm", 1024, BF16), ("t_S", 1024, BF16),
                ("t_kt", 1024, BF16), ("t_mA", 1024, BF16),
                ("t_mB", 1024, BF16), ("t_og", 2048, BF16),
                ("t_var", 1024, F32))
        for nm, cols, dt in _all:
            if tapset is None or nm in tapset:
                tapd[nm] = nc.dram_tensor(nm, [NCHUNK, 128, cols], dt,
                                          kind="ExternalOutput").ap()

    P = Prog(nc, selfsync=race)
    pipe_sem = nc.alloc_semaphore(name="pipe")
    pipe_n = [0]

    def vguard(v, inst):
        if race:
            return
        pipe_n[0] += 1
        inst.then_inc(pipe_sem, 1)
        v.wait_ge(pipe_sem, pipe_n[0])

    def sl(t, c0, n):
        return t.ap()[:, c0:c0 + n]

    def ap3(t, blocks, bstep, inner, off=0):
        base = t.ap()
        return AP(t, off, [[base.ap[0][0], 128], [bstep, blocks], [1, inner]])

    def strided(t, off, step, count):
        base = t.ap()
        return AP(t, off, [[base.ap[0][0], 128], [step, count]])

    def fconsts(g):
        return [g.dma_start(out=iden.ap(), in_=d_iden),
                g.dma_start(out=umask8.ap(), in_=d_um8),
                g.dma_start(out=idenf.ap(), in_=d_idenf),
                g.dma_start(out=blko.ap(), in_=d_blko),
                g.dma_start(out=hdo.ap(), in_=d_hdo),
                g.dma_start(out=ones1.ap(), in_=d_ones),
                g.dma_start(out=cvec.ap(), in_=d_cvec)]
    P.dma("gpsimd", fconsts, n=7, writes=CONSTS)

    P.op("vector", lambda v: v.tensor_scalar_mul(lnw64.ap(), cvec.ap()[:, 40:41], 64.0),
         reads=["cvec"], writes=["lnw64"])
    P.op("vector", lambda v: v.tensor_scalar(tm1m.ap(), cvec.ap()[:, 0:32],
                                             -1.0, 1.0, ALU.mult, ALU.add),
         reads=["cvec"], writes=["tm1m"])

    def load_mat_1024(dst_off, src_ap, src_elem_off=0):
        def f(g):
            ia = AP(src_ap.tensor, src_elem_off, [[1024, 128], [131072, 8], [1, 1024]])
            oa = AP(wpool, dst_off, [[wps, 128], [1024, 8], [1, 1024]])
            return [g.dma_start(out=oa, in_=ia)]
        return f

    REGN = {0: "wt_r", 8192: "wt_k", 16384: "wt_v", 24576: "wt_w",
            32768: "wt_o"}

    def attn_offs(p):
        return {"r": 0 if p == 0 else 32768, "k": 8192, "v": 16384,
                "w": 24576, "o": 32768 if p == 0 else 0}

    def emit_attn_weight_loads(p):
        offs = attn_offs(p)
        for nm in "rkvwo":
            P.dma("gpsimd", load_mat_1024(offs[nm], d_wT[nm]),
                  writes=[REGN[offs[nm]]])

    def emit_ffn_shared_loads():
        def fsgu(g):
            ia = AP(d_sgu.tensor, 0, [[2048, 128], [262144, 8], [1, 2048]])
            oa = AP(wpool, OFF_SGU, [[wps, 128], [2048, 8], [1, 2048]])
            return [g.dma_start(out=oa, in_=ia)]
        P.dma("gpsimd", fsgu, writes=list(R_SGU))
        P.dma("gpsimd", load_mat_1024(OFF_SDN, d_sdn), writes=[R_SDN])

        def frtr(g):
            ia = AP(d_rtr.tensor, 0, [[8, 128], [1024, 8], [1, 8]])
            oa = AP(wpool, OFF_RTR, [[wps, 128], [8, 8], [1, 8]])
            return [g.dma_start(out=oa, in_=ia)]
        P.dma("gpsimd", frtr, writes=["rtr"])

        def fsgw(g):
            ia = AP(d_sgw.tensor, 0, [[1, 128], [128, 8], [1, 1]])
            oa = AP(wpool, OFF_SGW, [[wps, 128], [1, 8]])
            return [g.dma_start(out=oa, in_=ia, allow_slow_non_contiguous=True)]
        P.dma("gpsimd", fsgw, writes=["sgw"])

    def emit_ln_stats(src_ap, src_res, tag):
        s6 = stat6.ap()[:, tag * 12:tag * 12 + 12]
        m_ = lambda i: mv.ap()[:, tag * 4 + i:tag * 4 + i + 1]

        def fv(v):
            i1 = v.bn_stats(s6[:, 0:6], src_ap[:, 0:512])
            i2 = v.bn_stats(s6[:, 6:12], src_ap[:, 512:1024])
            if not race:
                i1.then_inc(pipe_sem, 1)
                i2.then_inc(pipe_sem, 1)
                pipe_n[0] += 2
            v.tensor_copy(m_(2), cvec.ap()[:, 40:41])
            if not race:
                v.wait_ge(pipe_sem, pipe_n[0])
            return v.bn_aggr(mv.ap()[:, tag * 4:tag * 4 + 2], s6)
        P.op("vector", fv, reads=[src_res, "cvec"],
             writes=["stat6:%d" % tag, "mv:%d" % tag])

        def fs(s):
            s.activation(m_(2), m_(1), AF.Ln, bias=cvec.ap()[:, 42:43], scale=1.0)
            return s.activation(m_(2), m_(2), AF.Exp, bias=0.0, scale=cvec.ap()[:, 46:47])
        P.op("scalar", fs, reads=["cvec"], writes=["mv:%d" % tag])

    def emit_ln_norm(src_ap, src_res, dst_ap, tag):
        m_ = lambda i: mv.ap()[:, tag * 4 + i:tag * 4 + i + 1]

        def fv(v):
            return v.tensor_scalar(dst_ap, src_ap, m_(0), m_(2),
                                   ALU.subtract, ALU.mult)
        P.op("vector", fv, reads=[src_res, "mv:%d" % tag], writes=["xn_tok"])

    def emit_ln(src_aps, src_ress):
        for tag in range(2):
            emit_ln_stats(src_aps[tag], src_ress[tag], tag)
        for tag in range(2):
            emit_ln_norm(src_aps[tag], src_ress[tag],
                         sl(xn_tok, tag * 1024, 1024), tag)

    def emit_xn_transposes():
        xst = xn_fm.ap().ap[0][0]
        tst = xn_tok.ap().ap[0][0]
        for tc in range(2):
            def ftr(sy, tc=tc):
                oa = AP(xn_fm, tc * CH, [[xst, 128], [STG, NF], [1, 128]])
                ia = AP(xn_tok, tc * 1024, [[tst, 128], [128, NF], [1, 128]])
                return [sy.dma_start(out=oa, in_=ia, transpose=True)]
            P.dma("sync", ftr, n=1, reads=["xn_tok"],
                  writes=["xfA", "xfB"])

    def emit_proj_fm(w_view, w_res, src_tile, src_res, banks, evac, n=STG):
        def f(te):
            last = None
            for fo in range(NF):
                outap = pbk(banks[fo // 2], (fo % 2) * n, n)
                for dc in range(NF):
                    last = te.matmul(
                        outap,
                        w_view[:, dc * 1024 + fo * 128: dc * 1024 + fo * 128 + 128],
                        sl(src_tile, dc * STG, n),
                        start=(dc == 0), stop=(dc == NF - 1))
            return last
        P.op("tensor", f, reads=[w_res, src_res],
             writes=[PB[b] for b in banks])
        evac(banks, n)

    def scan_banks(tc):
        pA = [0, 1, 2, 3] if tc % 2 == 0 else [4, 5, 6, 7]
        pB = [4, 5, 6, 7] if tc % 2 == 0 else [0, 1, 2, 3]
        return pA, pB, [PB[b_] for b_ in pA], [PB[b_] for b_ in pB]

    def scan_front(tc):
        mo = tc * 48
        eo = tc * 24
        RM, RE = "mcol:%d" % tc, "eA:%d" % tc
        mc = lambda a, b: mcol_f32.ap()[:, mo + a:mo + b]
        eA = eA_f32.ap()[:, eo:eo + 8]

        def fscan(v):
            last = None
            for f in range(NF):
                src = sl(sc_f32, f * STG + tc * CH, CH)
                last = v.tensor_tensor_scan(sl(chunk_f32, f * CH, CH), src, src,
                                            0.0, ALU.add, ALU.bypass)
            return last
        P.op("vector", fscan, reads=["sc_f32"], writes=["chunk_f32"])

        def fmid(v):
            v.tensor_copy(mc(0, 8), strided(chunk_f32, 63, CH, NF))
            v.tensor_scalar(mc(8, 16), strided(chunk_f32, 63, CH, NF),
                            -1.0, -30.0, ALU.mult, ALU.add)
            v.tensor_scalar_add(mc(32, 40), strided(chunk_f32, 63, CH, NF),
                                -30.0)
            return v.tensor_sub(eA, strided(chunk_f32, CH - 1, CH, NF),
                                strided(chunk_f32, 63, CH, NF))
        P.op("vector", fmid, reads=["chunk_f32"], writes=[RM, RE])

        def fmid2(s):
            eAk = eA_f32.ap()[:, eo + 8:eo + 16]
            eAm = eA_f32.ap()[:, eo + 16:eo + 24]
            s.activation(mc(16, 24), mc(0, 8), AF.Exp,
                         bias=cvec.ap()[:, 47:48], scale=1.0)
            s.activation(mc(24, 32), mc(0, 8), AF.Exp,
                         bias=cvec.ap()[:, 48:49], scale=cvec.ap()[:, 45:46])
            s.activation(eAk, eA, AF.Exp,
                         bias=cvec.ap()[:, 48:49], scale=cvec.ap()[:, 45:46])
            s.activation(eAm, eA, AF.Exp,
                         bias=cvec.ap()[:, 47:48], scale=cvec.ap()[:, 45:46])
            return s.activation(eA, eA, AF.Exp, bias=0.0,
                                scale=cvec.ap()[:, 45:46])
        P.op("scalar", fmid2, reads=["cvec"], writes=[RM, RE])

        def fexpr(s):
            last = None
            for f in range(NF):
                last = s.activation(sl(expr_bf, f * CH + 1, CH - 1),
                                    sl(chunk_f32, f * CH, CH - 1),
                                    AF.Exp, bias=mc(32 + f, 33 + f),
                                    scale=cvec.ap()[:, 45:46])
            return last
        P.op("scalar", fexpr, reads=["chunk_f32", RM, "cvec"], writes=["expr_bf"])
        P.op("vector", lambda v: v.tensor_copy(strided(expr_bf, 0, CH, NF),
                                               mc(16, 24)),
             reads=[RM], writes=["expr_bf"])

        def fek(s):
            last = None
            for f in range(NF):
                last = s.activation(sl(chunk_f32, 1024 + f * CH, CH),
                                    sl(chunk_f32, f * CH, CH),
                                    AF.Exp, bias=mc(8 + f, 9 + f), scale=1.0)
            return last
        P.op("scalar", fek, reads=[RM], writes=["chunk_f32"])

    def scan_main(tc):
        pA, pB, PA, PBk = scan_banks(tc)
        mo = tc * 48
        eo = tc * 24
        RM, RE = "mcol:%d" % tc, "eA:%d" % tc
        mc = lambda a, b: mcol_f32.ap()[:, mo + a:mo + b]
        eA = eA_f32.ap()[:, eo:eo + 8]

        def frt(v):
            rsl = ap3(r_bf, NF, STG, CH, off=tc * CH)
            v.tensor_mul(ap3(rt_bf, NF, CH, CH), rsl, ap3(expr_bf, NF, CH, CH))
            v.tensor_mul(ap3(kt_bf, NF, CH, CH), ap3(k_bf, NF, STG, CH, off=tc * CH),
                         ap3(chunk_f32, NF, CH, CH, off=1024))
            return v.tensor_mul(ap3(t2_bf, NF, CH, CH), rsl,
                                ap3(eu_bf, NF, STG, CH, off=tc * CH))
        P.op("vector", frt,
             reads=["r_bf", "expr_bf", "k_bf", "chunk_f32", "euA", "euB"],
             writes=["rt_bf", "kt_bf", "t2_bf"])

        def fkh(v):
            last = None
            for f in range(NF):
                last = v.tensor_scalar_mul(sl(kh_bf, f * CH, CH),
                                           sl(kt_bf, f * CH, CH),
                                           eA_f32.ap()[:, eo + 8 + f:eo + 9 + f])
            return last
        P.op("vector", fkh, reads=["kt_bf", RE], writes=["kh_bf"])

        for rnd in range(2):
            def fkhT(te, rnd=rnd):
                last = None
                for j in range(4):
                    last = te.transpose(
                        AP(pb[pB[j]], 0, [[pbs[pB[j]], 128], [1, 512]])
                        .bitcast(BF16)[:, 0:128],
                        sl(kh_bf, (rnd * 4 + j) * CH, CH), iden.ap())
                return last
            P.op("tensor", fkhT, reads=["kh_bf", "iden"], writes=PBk)

            def fkhE(v, rnd=rnd):
                last = None
                for j in range(4):
                    last = v.tensor_copy(
                        sl(khtok, (rnd * 4 + j) * 128, 128),
                        AP(pb[pB[j]], 0, [[pbs[pB[j]], 128], [1, 512]])
                        .bitcast(BF16)[:, 0:128])
                return last
            P.op("vector", fkhE, reads=PBk, writes=["khtok"])

        def fd(te):
            last = None
            for f in range(NF):
                last = te.matmul(pbk(pB[0], 0, 16), sl(t2_bf, f * CH, CH),
                                 hdo.ap()[:, f * 16:f * 16 + 16],
                                 start=(f == 0), stop=(f == NF - 1))
            return last
        P.op("tensor", fd, reads=["t2_bf", "hdo"], writes=[PBk[0]])
        P.op("vector", lambda v: v.tensor_copy(D_sb.ap(), pbk(pB[0], 0, 16)),
             reads=[PBk[0]], writes=["D_sb"])

        sprime = kh_bf
        sprime2 = t2_bf

        def fsprime(v):
            last = None
            for f in range(NF):
                last = v.tensor_scalar_mul(
                    sl(sprime, f * 128, 128), sl(S_bf, f * 128, 128),
                    mc(24 + f, 25 + f))
            return last
        P.op("vector", fsprime, reads=["S_bf", RM], writes=["kh_bf"])

        def fsprime2(v):
            last = None
            for f in range(NF):
                last = v.tensor_scalar_mul(
                    sl(sprime2, f * 128, 128), sl(sprime, f * 128, 128),
                    eA_f32.ap()[:, eo + 16 + f:eo + 17 + f])
            return last
        P.op("vector", fsprime2, reads=["kh_bf", RE], writes=["t2_bf"])

        def fpall(te):
            last = None
            for f in range(NF):
                te.matmul(pbk(pA[0] + f // 4, (f % 4) * 128, 128),
                          kt_bf.ap()[0:64, f * CH:f * CH + CH],
                          rt_bf.ap()[0:64, f * CH:f * CH + CH],
                          start=True, stop=True)
                last = te.matmul(pbk(pA[2] + f // 4, (f % 4) * 128, 128),
                                 kt_bf.ap()[64:128, f * CH:f * CH + CH],
                                 rt_bf.ap()[64:128, f * CH:f * CH + CH],
                                 start=True, stop=True)
            return last
        P.op("tensor", fpall, reads=["kt_bf", "rt_bf"], writes=PA)

        def fmaskall(v):
            last = None
            for f in range(NF):
                v.select(sl(mA_bf, f * 128, 128), umask8.ap(),
                         pbk(pA[0] + f // 4, (f % 4) * 128, 128), umask8.ap())
                last = v.select(sl(mB_bf, f * 128, 128), umask8.ap(),
                                pbk(pA[2] + f // 4, (f % 4) * 128, 128), umask8.ap())
            E60 = 1.1420073898156842e+26
            vguard(v, last)
            v.tensor_scalar_mul(mA_bf.ap(), mA_bf.ap(), E60)
            return v.tensor_scalar_mul(mB_bf.ap(), mB_bf.ap(), E60)
        P.op("vector", fmaskall, reads=["umask8"] + PA,
             writes=["mA", "mB"])

        def fvtil(s):
            last = None
            for f in range(NF):
                s.activation(sl(vtil, f * 128, 64),
                             sl(vtok, tc * 1024 + f * 128, 64),
                             AF.Copy, bias=0.0,
                             scale=D_sb.ap()[:, 2 * f:2 * f + 1])
                last = s.activation(sl(vtil, f * 128 + 64, 64),
                                    sl(vtok, tc * 1024 + f * 128 + 64, 64),
                                    AF.Copy, bias=0.0,
                                    scale=D_sb.ap()[:, 2 * f + 1:2 * f + 2])
            return last
        P.op("scalar", fvtil, reads=["vtok", "D_sb"], writes=["vtil"])

        def foall(te):
            last = None
            for f in range(NF):
                b_ = pB[0] + f // 4
                c0 = (f % 4) * 128
                oo = pbk(b_, c0, 128)
                ooA = AP(pb[b_], c0, [[pbs[b_], 64], [1, 128]])
                ooB = AP(pb[b_], 64 * pbs[b_] + c0, [[pbs[b_], 64], [1, 128]])
                te.matmul(oo, sl(sprime, f * 128, 128),
                          rt_bf.ap()[:, f * CH:f * CH + CH], start=True, stop=False)
                te.matmul(ooA, sl(vtok, tc * 1024 + f * 128, 64),
                          sl(mA_bf, f * 128, 128), start=False, stop=False)
                te.matmul(ooB, sl(vtok, tc * 1024 + f * 128 + 64, 64),
                          sl(mB_bf, f * 128, 128), start=False, stop=False)
                te.matmul(oo, sl(vtil, f * 128, 128), iden.ap(),
                          start=False, stop=True)
                last = te.matmul(pbk(pB[2] + f // 4, (f % 4) * 128, 128),
                                 sl(khtok, f * 128, 128),
                                 sl(vtok, tc * 1024 + f * 128, 128),
                                 start=True, stop=True)
            return last
        P.op("tensor", foall,
             reads=["kh_bf", "rt_bf", "vtil", "iden", "mA",
                    "mB", "khtok", "vtok"],
             writes=PBk)

        def fseC(s):
            last = None
            for f in range(NF):
                last = s.activation(sl(o_fm, f * STG + tc * CH, CH),
                                    pbk(pB[0] + f // 4, (f % 4) * 128, 128),
                                    AF.Copy, bias=0.0, scale=1.0)
            return last
        P.op("scalar", fseC, reads=PBk[0:2], writes=["o_fm:%d" % tc])

        def fseall(v):
            sst = S_bf.ap().ap[0][0]
            s2t = t2_bf.ap().ap[0][0]
            last = None
            for half in range(2):
                for bi in range(2):
                    b_ = pB[2] + bi
                    f0 = bi * 4
                    co = half * 64
                    po = half * 64 * pbs[b_] + co
                    last = v.tensor_add(
                        AP(S_bf, half * 64 * sst + f0 * 128 + co,
                           [[sst, 64], [128, 4], [1, 64]]),
                        AP(t2_bf, half * 64 * s2t + f0 * 128 + co,
                           [[s2t, 64], [128, 4], [1, 64]]),
                        AP(pb[b_], po, [[pbs[b_], 64], [128, 4], [1, 64]]))
            return last
        P.op("vector", fseall, reads=PBk[2:4] + ["t2_bf"],
             writes=["S_bf"])

    def scan_gn(tc, alt=False):
        pA, pB, PA, PBk = scan_banks(tc)
        gns = mixb.ap().bitcast(F32)[:, 0:1024]
        if alt:
            r1v, b2v = rt_bf.ap(), kt_bf.ap()
            RR = ["rt_bf", "kt_bf"]
        else:
            scb = sc_f32.ap().bitcast(BF16)
            r1v, b2v = scb[:, 0:1024], scb[:, 1024:2048]
            RR = ["sc_f32"]
        RO = "o_fm:%d" % tc

        def fo2(s):
            return s.activation(ap3(mB_bf, NF, CH, CH),
                                ap3(o_fm, NF, STG, CH, off=tc * CH),
                                AF.Square, bias=0.0, scale=1.0)
        P.op("scalar", fo2, reads=[RO], writes=["mB"])

        def fgst(te):
            last = None
            for f in range(NF):
                bank = pA[0] + f // 4
                last = te.matmul(pbk(bank, (f % 4) * 128, 128), blko.ap(),
                                 sl(o_fm, f * STG + tc * CH, CH),
                                 start=(f % 4 == 0), stop=(f % 4 == 3))
            for f in range(NF):
                bank = pA[2] + f // 4
                last = te.matmul(pbk(bank, (f % 4) * 128, 128), blko.ap(),
                                 sl(mB_bf, f * CH, CH),
                                 start=(f % 4 == 0), stop=(f % 4 == 3))
            return last
        P.op("tensor", fgst, reads=["blko", RO, "mB"], writes=PA)

        def fgev(v):
            v.tensor_copy(sl(sc2_f32, 0, 512), pbk(pA[0], 0, 512))
            v.tensor_copy(sl(sc2_f32, 512, 512), pbk(pA[1], 0, 512))
            v.tensor_copy(sl(sc2_f32, 1024, 512), pbk(pA[2], 0, 512))
            return v.tensor_copy(sl(sc2_f32, 1536, 512), pbk(pA[3], 0, 512))
        P.op("vector", fgev, reads=PA, writes=["sc2_f32"])

        def fgsq(v):
            v.tensor_mul(gns, sl(sc2_f32, 0, 1024), sl(sc2_f32, 0, 1024))
            v.scalar_tensor_tensor(sl(sc2_f32, 1024, 1024),
                                   sl(sc2_f32, 1024, 1024), 64.0, gns,
                                   ALU.mult, ALU.subtract)
            return v.tensor_single_scalar(sl(sc2_f32, 1024, 1024),
                                          sl(sc2_f32, 1024, 1024), 0.0, ALU.max)
        P.op("vector", fgsq, reads=["mixb"], writes=["sc2_f32", "mixb"])

        def fgln(s):
            s.activation(sl(sc2_f32, 1024, 1024), sl(sc2_f32, 1024, 1024), AF.Ln,
                         bias=cvec.ap()[:, 43:44], scale=1.0)
            return s.activation(sl(sc2_f32, 1024, 1024), sl(sc2_f32, 1024, 1024),
                                AF.Exp, bias=0.0, scale=cvec.ap()[:, 46:47])
        P.op("scalar", fgln, reads=["cvec"], writes=["sc2_f32"])

        def fr1(v):
            v.tensor_scalar_mul(r1v, sl(sc2_f32, 1024, 1024),
                                lnw64.ap()[:, 0:1])
            v.tensor_mul(gns, sl(sc2_f32, 0, 1024), r1v)
            return v.tensor_scalar(b2v, gns, -1.0 / 64.0,
                                   cvec.ap()[:, 41:42], ALU.mult, ALU.add)
        P.op("vector", fr1, reads=["lnw64", "sc2_f32", "cvec"],
             writes=RR + ["mixb"])

        def fon(v):
            og = ap3(og_fm, NF, STG, CH, off=tc * CH)
            v.tensor_mul(og, ap3(o_fm, NF, STG, CH, off=tc * CH), r1v)
            v.tensor_add(og, og, b2v)
            return v.tensor_mul(og, og,
                                ap3(r_bf, NF, STG, CH, off=tc * CH))
        P.op("vector", fon, reads=[RO, "r_bf"] + RR,
             writes=["og_fm"])

        if tap:
            ci = tapn[0]
            tapn[0] += 1

            _tt = [(nm, t_, cols, rr) for nm, t_, cols, rr in
                   (("t_ofm", o_fm, 1024, "o_fm:%d" % tc),
                    ("t_S", S_bf, 1024, "S_bf"),
                    ("t_kt", kt_bf, 1024, "kt_bf"),
                    ("t_mA", mA_bf, 1024, "mA"),
                    ("t_mB", mB_bf, 1024, "mB"),
                    ("t_og", og_fm, 2048, "og_fm"),
                    ("t_var", sc2_f32, 1024, "sc2_f32")) if nm in tapd]

            def ftap(g, tc=tc, ci=ci, _tt=_tt):
                outs = []
                for nm, t_, cols, _ in _tt:
                    nel = 128 * cols
                    ia = (t_.ap()[:, 0:cols] if nm != "t_var"
                          else t_.ap()[:, 1024:2048])
                    if nm == "t_ofm":
                        ia = ap3(t_, NF, STG, CH, off=tc * CH)
                    oa = AP(tapd[nm].tensor, ci * nel,
                            [[cols, 128], [1, cols]])
                    outs.append(g.dma_start(out=oa, in_=ia))
                return outs
            P.dma("gpsimd", ftap, n=len(_tt),
                  reads=[rr for _, _, _, rr in _tt],
                  writes=["tap%d" % ci])
    def emit_front(b, h, st):
        t0 = h * half_tokens
        st_t0 = t0 + st * STG
        base_tc = st * 2
        RC = ["resid:%d" % (base_tc + tc) for tc in range(2)]

        def fx(g, st_t0=st_t0, base_tc=base_tc, b=b):
            ia = AP(x_in.tensor, b * T * D + st_t0 * D,
                    [[1024, 128], [131072, 2], [1, 1024]])
            oa = AP(resid, base_tc * 1024,
                    [[resid.ap().ap[0][0], 128], [1024, 2], [1, 1024]])
            return [g.dma_start(out=oa, in_=ia)]
        P.dma("gpsimd", fx, writes=RC)

        emit_ln([sl(resid, (base_tc + tc) * 1024, 1024) for tc in range(2)], RC)
        emit_xn_transposes()
        if h == 0 and st == 0:
            P.op("vector", lambda v: v.memset(prevc.ap(), 0.0), writes=["prevc"])

        def fsh2(v):
            v.tensor_copy(ap3(xsh_fm, NF, STG, STG - 1, off=1),
                          ap3(xn_fm, NF, STG, STG - 1, off=0))
            v.tensor_copy(strided(xsh_fm, 0, STG, NF), prevc.ap())
            v.tensor_copy(prevc.ap(), strided(xn_fm, STG - 1, STG, NF))
            return v.tensor_sub(mixtmp.ap(), xn_fm.ap(), xsh_fm.ap())
        P.op("vector", fsh2, reads=["xfA", "xfB"],
             writes=["xsh_fm", "prevc", "mixtmp"])

    def phase_A(b, h, first=False, p=0):
        t0 = h * half_tokens
        offs = attn_offs(p)
        wtv = {nm: wview(offs[nm], 8192) for nm in "rkvwo"}
        rres = {nm: REGN[offs[nm]] for nm in "rkvwo"}
        emit_attn_weight_loads(p)
        if h == 0:
            P.op("vector", lambda v: v.memset(S_bf.ap(), 0.0), writes=["S_bf"])

        if first:
            emit_front(b, h, 0)
        for st in range(NSTG):
            base_tc = st * 2
            RC = ["resid:%d" % (base_tc + tc) for tc in range(2)]

            cvst = cvec.ap().ap[0][0]

            def emit_mix(coff, dst, dstres):
                def fmix(v):
                    last = None
                    for f in range(NF):
                        last = v.scalar_tensor_tensor(
                            sl(dst, f * STG, STG), sl(mixtmp, f * STG, STG),
                            cvec.ap()[:, coff + f:coff + f + 1],
                            sl(xsh_fm, f * STG, STG), ALU.mult, ALU.add)
                    return last
                P.op("vector", fmix, reads=["mixtmp", "cvec", "xsh_fm"],
                     writes=[dstres])

            def evac_r(banks, n):
                def fs(s):
                    last = None
                    for fo in range(NF):
                        last = s.activation(sl(r_bf, fo * STG, n),
                                            pbk(banks[fo // 2], (fo % 2) * n, n),
                                            AF.Sigmoid, bias=0.0, scale=1.0)
                    return last
                P.op("scalar", fs, reads=[PB[b_] for b_ in banks], writes=["r_bf"])
            emit_mix(0, mixb, "mixb")
            emit_proj_fm(wtv["r"], rres["r"], mixb, "mixb", [0, 1, 2, 3], evac_r)

            def evac_k(banks, n):
                def fv(v):
                    last = None
                    for fo in range(NF):
                        last = v.tensor_copy(sl(k_bf, fo * STG, n),
                                             pbk(banks[fo // 2], (fo % 2) * n, n))
                    return last
                P.op("vector", fv, reads=[PB[b_] for b_ in banks], writes=["k_bf"])

                def fs(s):
                    last = None
                    for fo in range(NF):
                        last = s.activation(sl(eu_bf, fo * STG, n),
                                            pbk(banks[fo // 2], (fo % 2) * n, n),
                                            AF.Exp,
                                            bias=cvec.ap()[:, 32 + fo:32 + fo + 1],
                                            scale=1.0)
                    return last
                P.op("scalar", fs, reads=[PB[b_] for b_ in banks] + ["cvec"],
                     writes=["euA", "euB"])
            emit_mix(8, og_fm, "og_fm")
            emit_proj_fm(wtv["k"], rres["k"], og_fm, "og_fm", [4, 5, 6, 7], evac_k)

            def evac_w(banks, n):
                def fs(s):
                    last = None
                    for fo in range(NF):
                        last = s.activation(sl(sc_f32, fo * n, n),
                                            pbk(banks[fo // 2], (fo % 2) * n, n),
                                            AF.Exp, bias=0.0, scale=cvec.ap()[:, 45:46])
                    return last
                P.op("scalar", fs, reads=[PB[b_] for b_ in banks] + ["cvec"],
                     writes=["sc_f32"])
                P.op("scalar", lambda s: s.activation(
                    sc_f32.ap(), sc_f32.ap(), AF.Ln, bias=cvec.ap()[:, 44:45],
                    scale=1.0), reads=["cvec"], writes=["sc_f32"])
                P.op("vector", lambda v: v.tensor_scalar_add(
                    sc_f32.ap(), sc_f32.ap(), 1e-4), writes=["sc_f32"])
            emit_mix(24, mixb, "mixb")
            emit_proj_fm(wtv["w"], rres["w"], mixb, "mixb", [0, 1, 2, 3], evac_w)

            emit_mix(16, og_fm, "og_fm")

            def fvproj(te):
                last = None
                for tc in range(2):
                    for hlf in range(2):
                        outap = pbk(4 + tc * 2 + hlf, 0, 512)
                        for dc in range(NF):
                            last = te.matmul(
                                outap, sl(og_fm, dc * STG + tc * CH, 128),
                                wtv["v"][:, dc * 1024 + hlf * 512: dc * 1024 + hlf * 512 + 512],
                                start=(dc == 0), stop=(dc == NF - 1))
                return last
            P.op("tensor", fvproj, reads=["og_fm", rres["v"]],
                 writes=[PB[4], PB[5], PB[6], PB[7]])

            def fvevac(s):
                last = None
                for tc in range(2):
                    for hlf in range(2):
                        last = s.activation(sl(vtok, tc * 1024 + hlf * 512, 512),
                                            pbk(4 + tc * 2 + hlf, 0, 512),
                                            AF.Copy, bias=0.0, scale=1.0)
                return last
            P.op("scalar", fvevac, reads=[PB[4], PB[5], PB[6], PB[7]],
                 writes=["vtok"])

            scan_front(0)
            scan_main(0)
            scan_front(1)
            scan_gn(0)
            scan_main(1)
            if st + 1 < NSTG:
                emit_front(b, h, st + 1)
            scan_gn(1, alt=(st == NSTG - 1))

            def foproj(te):
                last = None
                for tc in range(2):
                    for hlf in range(2):
                        outap = pbk(tc * 2 + hlf, 0, 512)
                        for dc in range(NF):
                            last = te.matmul(
                                outap,
                                sl(og_fm, dc * STG + tc * CH, 128),
                                wtv["o"][:, dc * 1024 + hlf * 512: dc * 1024 + hlf * 512 + 512],
                                start=(dc == 0), stop=(dc == NF - 1))
                return last
            P.op("tensor", foproj, reads=[rres["o"], "og_fm"], writes=PB[0:4])

            def fradd(v, base_tc=base_tc):
                last = None
                for tc in range(2):
                    for hlf in range(2):
                        c0 = (base_tc + tc) * 1024 + hlf * 512
                        last = v.tensor_add(sl(resid, c0, 512), sl(resid, c0, 512),
                                            pbk(tc * 2 + hlf, 0, 512))
                return last
            P.op("vector", fradd, reads=PB[0:4] + RC, writes=RC)

    XN2H = [(sc_f32, 0), (sc_f32, 2048), (chunk_f32, 0), (chunk_f32, 2048)]
    XN2R = ["sc_f32", "sc_f32", "chunk_f32", "chunk_f32"]

    def xn2v(st):
        t, c0 = XN2H[st]
        a = t.ap() if t.ap().dtype == BF16 else t.ap().bitcast(BF16)
        return a[:, c0:c0 + 2048]

    def expert_slots(e):
        if e % 2 == 0:
            return OFF_W1, OFF_W2, R_W1, R_W2
        return 0, 8192, "wt_r", "wt_k"

    def emit_expert_load(e):
        off1, off2, r1, r2 = expert_slots(e)
        P.dma("gpsimd", load_mat_1024(off1, d_w1, src_elem_off=e * D * D),
              writes=[r1])
        P.dma("gpsimd", load_mat_1024(off2, d_w2, src_elem_off=e * D * D),
              writes=[r2])

    def phase_B_router(st):
        xn2s = xn2v(st)

        def frt(te):
            last = None
            for tc2 in range(2):
                for dc in range(NF):
                    last = te.matmul(
                        pbk(1 + tc2, 0, 8),
                        xn2s[:, dc * STG + tc2 * CH:dc * STG + tc2 * CH + 128],
                        rtrsb[:, dc * 8:dc * 8 + 8],
                        start=(dc == 0), stop=(dc == NF - 1))
            return last
        P.op("tensor", frt, reads=["rtr", XN2R[st]], writes=[PB[1], PB[2]])

        def frte(v):
            v.tensor_copy(lg_tok.ap()[:, 0:8], pbk(1, 0, 8))
            return v.tensor_copy(lg_tok.ap()[:, 8:16], pbk(2, 0, 8))
        P.op("vector", frte, reads=[PB[1], PB[2]], writes=["lg_tok"])

        def fmax(v):
            v.reduce_max(ssum.ap()[:, 0:1], lg_tok.ap()[:, 0:8], AXX, negate=True)
            return v.reduce_max(ssum.ap()[:, 2:3], lg_tok.ap()[:, 8:16], AXX,
                                negate=True)
        P.op("vector", fmax, reads=["lg_tok"], writes=["ssum"])

        def fex(s):
            s.activation(esm.ap()[:, 0:8], lg_tok.ap()[:, 0:8], AF.Exp,
                         bias=ssum.ap()[:, 0:1], scale=1.0,
                         accum_out=ssum.ap()[:, 1:2])
            return s.activation(esm.ap()[:, 8:16], lg_tok.ap()[:, 8:16], AF.Exp,
                                bias=ssum.ap()[:, 2:3], scale=1.0,
                                accum_out=ssum.ap()[:, 3:4])
        P.op("scalar", fex, reads=["lg_tok"], writes=["esm", "ssum"])

        def ftp(v, st=st):
            ea = esm.ap()[:, 0:8]
            eb = esm.ap()[:, 8:16]
            ta = top8.ap()[:, 0:8]
            tb = top8.ap()[:, 8:16]
            ga = lg_tok.ap()[:, 0:8]
            gb = lg_tok.ap()[:, 8:16]
            v.max(ta, ea)
            vguard(v, v.max(tb, eb))
            v.reciprocal(ssum.ap()[:, 0:1], ssum.ap()[:, 1:2])
            v.reciprocal(ssum.ap()[:, 2:3], ssum.ap()[:, 3:4])
            v.tensor_single_scalar(ga, ea, top8.ap()[:, 1:2], ALU.is_ge)
            vguard(v, v.tensor_single_scalar(gb, eb, top8.ap()[:, 9:10],
                                             ALU.is_ge))
            v.tensor_mul(ga, ga, ea)
            vguard(v, v.tensor_mul(gb, gb, eb))
            v.tensor_scalar_mul(cmb_all.ap()[:, st * 16:st * 16 + 8], ga,
                                ssum.ap()[:, 0:1])
            return v.tensor_scalar_mul(cmb_all.ap()[:, st * 16 + 8:st * 16 + 16],
                                       gb, ssum.ap()[:, 2:3])
        P.op("vector", ftp, reads=["esm"],
             writes=["lg_tok", "top8", "ssum", "cmb_all"])

    def phase_B_front(b, h, st):
        base_tc = st * 2
        RC = ["resid:%d" % (base_tc + tc) for tc in range(2)]
        emit_ln([sl(resid, (base_tc + tc) * 1024, 1024) for tc in range(2)], RC)
        host, c0 = XN2H[st]
        hst = host.ap().ap[0][0]
        tst = xn_tok.ap().ap[0][0]
        for tc in range(2):
            def ftr(sy, tc=tc):
                oa = AP(host, c0 // 2 + tc * 64,
                        [[hst, 128], [128, NF], [1, 64]]).bitcast(BF16)
                ia = AP(xn_tok, tc * 1024, [[tst, 128], [128, NF], [1, 128]])
                return [sy.dma_start(out=oa, in_=ia, transpose=True)]
            P.dma("sync", ftr, n=1, reads=["xn_tok"], writes=[XN2R[st]])

    def phase_B_stage(b, h, st):
        base_tc = st * 2
        RC = ["resid:%d" % (base_tc + tc) for tc in range(2)]
        xn2s = xn2v(st)

        def fgc(te):
            last = None
            for tc2 in range(2):
                for dc in range(NF):
                    last = te.matmul(
                        pbk(5, tc2 * 256, 1),
                        xn2s[:, dc * STG + tc2 * CH:dc * STG + tc2 * CH + 128],
                        sgwsb[:, dc:dc + 1],
                        start=(dc == 0), stop=(dc == NF - 1))
            return last
        P.op("tensor", fgc, reads=["sgw", XN2R[st]], writes=[PB[5]])

        def fgsig(s, st=st):
            s.activation(gcol.ap()[:, st * 2:st * 2 + 1], pbk(5, 0, 1),
                         AF.Sigmoid, bias=0.0, scale=1.0)
            return s.activation(gcol.ap()[:, st * 2 + 1:st * 2 + 2],
                                pbk(5, 256, 1), AF.Sigmoid, bias=0.0, scale=1.0)
        P.op("scalar", fgsig, reads=[PB[5]], writes=["gcol"])

        hb = eu_bf if st % 2 == 0 else xn_fm
        hres = ["euA", "euB"] if st % 2 == 0 else ["xfA", "xfB"]
        tb, tbr = (mixb, "mixb") if st % 2 == 0 else (og_fm, "og_fm")

        for half in range(2):
            def fguh(te, half=half, xn2s=xn2s):
                last = None
                for f2 in range(half * 8, half * 8 + 8):
                    outap = pbk(f2 // 2, (f2 % 2) * 256, 256)
                    for dc in range(NF):
                        last = te.matmul(
                            outap,
                            sgusb[:, dc * 2048 + f2 * 128: dc * 2048 + f2 * 128 + 128],
                            xn2s[:, dc * STG:dc * STG + STG],
                            start=(f2 % 2 == 0 and dc == 0),
                            stop=(f2 % 2 == 1 and dc == NF - 1))
                return last
            P.op("tensor", fguh, reads=list(R_SGU) + [XN2R[st]],
                 writes=PB[half * 4:half * 4 + 4])

        def fsg(s):
            last = None
            for f in range(NF):
                last = s.activation(sl(hb, f * STG, STG),
                                    pbk(f // 2, (f % 2) * 256, 256),
                                    AF.Silu, bias=0.0, scale=1.0)
            return last
        P.op("scalar", fsg, reads=PB[0:4], writes=hres)

        def fup(v):
            last = None
            for f in range(NF):
                last = v.tensor_mul(sl(tb, f * STG, STG), sl(hb, f * STG, STG),
                                    pbk(4 + f // 2, (f % 2) * 256, 256))
            return last
        P.op("vector", fup, reads=hres + PB[4:8], writes=[tbr])

        def fsdn(te):
            last = None
            for tc2 in range(2):
                for hlf in range(2):
                    outap = pbk(tc2 * 2 + hlf, 0, 512)
                    for fc in range(NF):
                        last = te.matmul(
                            outap, sl(tb, fc * STG + tc2 * CH, 128),
                            sdnsb[:, fc * 1024 + hlf * 512: fc * 1024 + hlf * 512 + 512],
                            start=(fc == 0), stop=(fc == NF - 1))
            return last
        P.op("tensor", fsdn, reads=[R_SDN, tbr], writes=PB[0:4])

        def fsadd(v, st=st):
            last = None
            for tc2 in range(2):
                for hlf in range(2):
                    c0 = (base_tc + tc2) * 1024 + hlf * 512
                    last = v.scalar_tensor_tensor(
                        sl(resid, c0, 512), pbk(tc2 * 2 + hlf, 0, 512),
                        gcol.ap()[:, st * 2 + tc2:st * 2 + tc2 + 1],
                        sl(resid, c0, 512), ALU.mult, ALU.add)
            return last
        P.op("vector", fsadd, reads=PB[0:4] + RC + ["gcol"], writes=RC)

    def phase_B_expert(e, b=0, h=0, nxt=None):
        off1, off2, r1, r2 = expert_slots(e)
        final = (e == E - 1)
        w1v = wview(off1, 8192)
        w2v = wview(off2, 8192)
        for st in range(NSTG):
            xn2s = xn2v(st)
            base_tc = st * 2
            RC = ["resid:%d" % (base_tc + tc) for tc in range(2)]
            hb = eu_bf if st % 2 == 0 else xn_fm
            hres = ["euA", "euB"] if st % 2 == 0 else ["xfA", "xfB"]

            for half in range(2):
                def fw1h(te, half=half, xn2s=xn2s, w1v=w1v):
                    last = None
                    for fc in range(half * 4, half * 4 + 4):
                        outap = pbk(4 + fc // 2, (fc % 2) * 256, 256)
                        for dc in range(NF):
                            last = te.matmul(
                                outap,
                                w1v[:, dc * 1024 + fc * 128: dc * 1024 + fc * 128 + 128],
                                xn2s[:, dc * STG:dc * STG + STG],
                                start=(fc % 2 == 0 and dc == 0),
                                stop=(fc % 2 == 1 and dc == NF - 1))
                    return last
                P.op("tensor", fw1h, reads=[r1, XN2R[st]],
                     writes=PB[4 + 2 * half:6 + 2 * half])

                def fsih(s, half=half, hb=hb):
                    last = None
                    for fc in range(half * 4, half * 4 + 4):
                        last = s.activation(sl(hb, fc * STG, STG),
                                            pbk(4 + fc // 2, (fc % 2) * 256, 256),
                                            AF.Silu, bias=0.0, scale=1.0)
                    return last
                P.op("scalar", fsih, reads=PB[4 + 2 * half:6 + 2 * half],
                     writes=[hres[half]])

            for half in range(2):
                def fw2p(te, half=half, w2v=w2v, hb=hb):
                    last = None
                    for tc2 in range(2):
                        for hlf in range(2):
                            outap = pbk(tc2 * 2 + hlf, 0, 512)
                            for fc in range(half * 4, half * 4 + 4):
                                last = te.matmul(
                                    outap, sl(hb, fc * STG + tc2 * CH, 128),
                                    w2v[:, fc * 1024 + hlf * 512: fc * 1024 + hlf * 512 + 512],
                                    start=(half == 0 and fc == 0),
                                    stop=(half == 1 and fc == NF - 1))
                    return last
                P.op("tensor", fw2p, reads=[r2, hres[half]], writes=PB[0:4])

            def fadd(v, base_tc=base_tc, e=e, st=st):
                last = None
                for tc2 in range(2):
                    for hlf in range(2):
                        c0 = (base_tc + tc2) * 1024 + hlf * 512
                        last = v.scalar_tensor_tensor(
                            sl(resid, c0, 512), pbk(tc2 * 2 + hlf, 0, 512),
                            cmb_all.ap()[:, st * 16 + tc2 * 8 + e:
                                         st * 16 + tc2 * 8 + e + 1],
                            sl(resid, c0, 512), ALU.mult, ALU.add)
                return last
            P.op("vector", fadd, reads=PB[0:4] + RC + ["cmb_all"], writes=RC)

            if final:
                t0 = h * half_tokens

                def fst(g, st=st, t0=t0, b=b):
                    st_t0 = t0 + st * STG
                    oa = AP(y_out.tensor, b * T * D + st_t0 * D,
                            [[1024, 128], [131072, 2], [1, 1024]])
                    ia = AP(resid, st * 2 * 1024,
                            [[resid.ap().ap[0][0], 128], [1024, 2], [1, 1024]])
                    return [g.dma_start(out=oa, in_=ia)]
                P.dma("gpsimd", fst, reads=RC,
                      writes=["y:%d:%d:%d" % (b, h, st)])
                if st == 0 and nxt is not None:
                    emit_front(nxt[0], nxt[1], 0)

    def phase_B(b, h, nxt=None):
        emit_ffn_shared_loads()
        emit_expert_load(0)
        for st in range(NSTG):
            phase_B_front(b, h, st)
        for st in range(NSTG):
            phase_B_stage(b, h, st)
        for st in range(NSTG):
            phase_B_router(st)
        for e in range(E):
            if e + 1 < E:
                emit_expert_load(e + 1)
            phase_B_expert(e, b=b, h=h, nxt=nxt)

    bh = [(b, h) for b in range(nseq) for h in range(NH)]
    for i, (b, h) in enumerate(bh):
        phase_A(b, h, first=(i == 0), p=i % 2)
        phase_B(b, h, nxt=(bh[i + 1] if i + 1 < len(bh) else None))

    P.finalize()
    return nc


def make_host_inputs(inp):
    tT = lambda a: np.ascontiguousarray(np.asarray(a).T)
    cN = np.ascontiguousarray
    iden = np.eye(128, dtype=BF)
    umask = np.triu(np.ones((128, 128), dtype=np.float32), 1)
    blko = np.zeros((128, 128), dtype=np.float32)
    blko[0:64, 0:64] = 1.0
    blko[64:128, 64:128] = 1.0
    hdo = np.zeros((128, 128), dtype=np.float32)
    for f in range(8):
        hdo[0:64, f * 16 + 2 * f] = 1.0
        hdo[64:128, f * 16 + 2 * f + 1] = 1.0
    cvec = np.zeros((128, 52), dtype=np.float32)
    cvec[:, 47] = -30.0
    cvec[:, 48] = 30.0
    for i, nm in enumerate(("tm_r", "tm_k", "tm_v", "tm_w")):
        cvec[:, i * 8:(i + 1) * 8] = np.asarray(inp["L0_attn_" + nm]).reshape(8, 128).T
    cvec[:, 32:40] = np.asarray(inp["L0_attn_u"]).reshape(-1).reshape(8, 128).T
    cvec[:, 40] = np.tile(np.asarray(inp["L0_attn_ln_w"]), 2)
    cvec[:, 41] = np.tile(np.asarray(inp["L0_attn_ln_b"]), 2)
    cvec[:, 42] = 1e-5
    cvec[:, 43] = 4096.0 * 1e-5
    cvec[:, 44] = 1.0
    cvec[:, 45] = -1.0
    cvec[:, 46] = -0.5
    shared = {
        "wrT": tT(inp["L0_attn_W_r"]), "wkT": tT(inp["L0_attn_W_k"]),
        "wvT": tT(inp["L0_attn_W_v"]), "wwT": tT(inp["L0_attn_W_w"]),
        "woT": tT(inp["L0_attn_W_o"]),
        "w1": cN(np.asarray(inp["L0_ffn_w1"])), "w2": cN(np.asarray(inp["L0_ffn_w2"])),
        "sgu": cN(np.asarray(inp["L0_ffn_shared_gate_up"])),
        "sdn": cN(np.asarray(inp["L0_ffn_shared_down"])),
        "rtr": cN(np.asarray(inp["L0_ffn_router_w"])),
        "sgw": cN(np.asarray(inp["L0_ffn_shared_gate_w"])),
        "idn": iden, "um8": umask.astype(np.uint8),
        "idf": np.eye(128, dtype=np.float32), "blk": blko.astype(BF), "hdo": hdo.astype(BF),
        "on1": np.ones((1, 128), dtype=BF), "cvc": cvec,
    }
    x = np.asarray(inp["L0_attn_x"])
    xs = [cN(x[c * 2:(c + 1) * 2]) for c in range(NCORE)]
    return shared, xs


def kernel(**inputs):
    shared, xs = make_host_inputs(inputs)
    nc = build_nc()
    in_maps = [dict(shared, x=xs[c]) for c in range(NCORE)]
    res = run_bass_kernel_spmd(nc, in_maps, core_ids=list(range(NCORE)))
    out = np.concatenate([res.results[c]["y"] for c in range(NCORE)], axis=0)
    return np.ascontiguousarray(out.astype(np.float32))


# version 25 (final)
# speedup vs baseline: 1.1860x; 1.0142x over previous
import numpy as np
import ml_dtypes
import concourse.bass as bass
import concourse.mybir as mybir
from concourse.bass_utils import run_bass_kernel_spmd
from concourse.ap import AP

F32 = mybir.dt.float32
BF16 = mybir.dt.bfloat16
AF = mybir.ActivationFunctionType
ALU = mybir.AluOpType
AXX = mybir.AxisListType.X

NCORE = 8
D = 1024
NF = 8
E = 8
EPS = 1e-5
STG = 256
CH = 128
BF = ml_dtypes.bfloat16

CONSTS = ("iden", "umask8", "idenf", "blko", "hdo", "ones1", "cvec")


class Prog:

    ENGS = ("sync", "gpsimd", "tensor", "vector", "scalar")
    NLANES = 12

    def __init__(self, nc, selfsync=False):
        self.nc = nc
        self.selfsync = selfsync
        if selfsync:
            self.ssem = {e: nc.alloc_semaphore(name="ss_" + e) for e in self.ENGS}
            self.sctr = {e: [0] for e in self.ENGS}
        self.q = {e: [] for e in self.ENGS}
        self.sem = {e: nc.alloc_semaphore(name="s_" + e) for e in self.ENGS}
        self.cnt = {e: 0 for e in self.ENGS}
        self.lane_pool = {"gpsimd": list(range(8)), "sync": list(range(8, 12))}
        self.lane_sem = [nc.alloc_semaphore(name="ln%d" % i)
                         for i in range(self.NLANES)]
        self.lane_cnt = [0] * self.NLANES
        self.lane_rr = {"gpsimd": 0, "sync": 0}
        self.vclock = {e: {} for e in self.ENGS}
        self.res = {}

    def _handle(self, sk):
        return self.sem[sk] if isinstance(sk, str) else self.lane_sem[sk]

    def _wrap(self, eng, engine):
        if not self.selfsync:
            return engine
        sem, ctr = self.ssem[eng], self.sctr[eng]

        class _Ser:
            def __getattr__(_s, nm):
                a = getattr(engine, nm)
                if not callable(a):
                    return a

                def call(*args, **kw):
                    r = a(*args, **kw)
                    if (hasattr(r, "then_inc") and nm != "wait_ge"
                            and not nm.startswith("dma_start")
                            and not nm.startswith("sem_")):
                        ctr[0] += 1
                        r.then_inc(sem, 1)
                        engine.wait_ge(sem, ctr[0])
                    return r
                return call
        return _Ser()

    def _dep_points(self, reads, writes):
        pts = []
        for r in reads:
            e = self.res.get(r)
            if e is None or e["w"] is None:
                raise KeyError(f"read of resource {r!r} with no prior writer")
            pts.append(e["w"])
        for w in writes:
            e = self.res.get(w)
            if e is not None:
                if e["w"] is not None:
                    pts.append(e["w"])
                pts.extend(e["r"])
        return pts

    def _plan(self, eng, pts):
        vc = self.vclock[eng]
        need = {}
        for sk, val, _ in pts:
            if need.get(sk, -1) < val:
                need[sk] = val
        waits = []
        for sk, val in sorted(need.items(), key=str):
            if sk == eng and val <= self.cnt[eng]:
                continue
            if vc.get(sk, 0) >= val:
                continue
            waits.append((self._handle(sk), val))
        for sk, val, pvc in pts:
            for k2, v2 in pvc.items():
                if vc.get(k2, 0) < v2:
                    vc[k2] = v2
            if vc.get(sk, 0) < val:
                vc[sk] = val
        return waits

    def _update_res(self, reads, writes, pt):
        for w in writes:
            self.res[w] = {"w": pt, "r": []}
        for r in reads:
            e = self.res.get(r)
            if e is None:
                e = self.res[r] = {"w": None, "r": []}
            e["r"].append(pt)

    def op(self, eng, fn, reads=(), writes=()):
        pts = self._dep_points(reads, writes)
        waits = self._plan(eng, pts)
        self.cnt[eng] += 1
        myval = self.cnt[eng]
        sem = self.sem[eng]

        def run(engine, waits=waits, fn=fn, eng=eng):
            for h, v in waits:
                engine.wait_ge(h, v)
            last = fn(self._wrap(eng, engine))
            assert last is not None, "op fn must return its last instruction"
            if self.selfsync:
                engine.sem_inc(sem, 1)
            else:
                last.then_inc(sem, 1)

        self.q[eng].append(run)
        pt = (eng, myval, dict(self.vclock[eng]))
        self._update_res(reads, writes, pt)

    def dma(self, eng, fn, n=1, reads=(), writes=()):
        pts = self._dep_points(reads, writes)
        pool = self.lane_pool[eng]
        lane = pool[self.lane_rr[eng]]
        self.lane_rr[eng] = (self.lane_rr[eng] + 1) % len(pool)
        if self.lane_cnt[lane] > 0:
            pts = pts + [(lane, self.lane_cnt[lane], {})]
        waits = self._plan(eng, pts)
        self.lane_cnt[lane] += 16 * n
        target = self.lane_cnt[lane]
        sem = self.lane_sem[lane]

        def run(engine, waits=waits, fn=fn, n=n, eng=eng):
            for h, v in waits:
                engine.wait_ge(h, v)
            insts = fn(self._wrap(eng, engine))
            assert len(insts) == n, f"expected {n} DMAs, got {len(insts)}"
            for i in insts:
                i.then_inc(sem, 16)

        self.q[eng].append(run)
        pt = (lane, target, dict(self.vclock[eng]))
        self._update_res(reads, writes, pt)

    def finalize(self):
        with self.nc.Block() as block:
            final = ([(self.sem[e], self.cnt[e]) for e in self.ENGS
                      if self.cnt[e] > 0] +
                     [(self.lane_sem[i], self.lane_cnt[i])
                      for i in range(self.NLANES) if self.lane_cnt[i] > 0])
            for name in self.ENGS:
                ops = self.q[name]

                def body(engine, ops=ops):
                    for op in ops:
                        op(engine)
                    for s, v in final:
                        engine.wait_ge(s, v)

                getattr(block, name)(body)


def build_nc(T=2048, nseq=2, half_tokens=1024, race=False, tap=False, tapset=None):
    NH = T // half_tokens
    NSTG = half_tokens // STG
    NTC = half_tokens // CH

    nc = bass.Bass("TRN2", target_bir_lowering=False,
                   detect_race_conditions=race)

    x_in = nc.dram_tensor("x", [nseq, T, D], F32, kind="ExternalInput").ap()
    y_out = nc.dram_tensor("y", [nseq, T, D], F32, kind="ExternalOutput").ap()
    d_wT = {nm: nc.dram_tensor("w%sT" % nm, [D, D], F32, kind="ExternalInput").ap()
            for nm in "rkvwo"}
    d_w1 = nc.dram_tensor("w1", [E, D, D], F32, kind="ExternalInput").ap()
    d_w2 = nc.dram_tensor("w2", [E, D, D], F32, kind="ExternalInput").ap()
    d_sgu = nc.dram_tensor("sgu", [D, 2 * D], F32, kind="ExternalInput").ap()
    d_sdn = nc.dram_tensor("sdn", [D, D], F32, kind="ExternalInput").ap()
    d_rtr = nc.dram_tensor("rtr", [D, E], F32, kind="ExternalInput").ap()
    d_sgw = nc.dram_tensor("sgw", [D, 1], F32, kind="ExternalInput").ap()
    d_iden = nc.dram_tensor("idn", [128, 128], BF16, kind="ExternalInput").ap()
    d_um8 = nc.dram_tensor("um8", [128, 128], mybir.dt.uint8, kind="ExternalInput").ap()
    d_idenf = nc.dram_tensor("idf", [128, 128], F32, kind="ExternalInput").ap()
    d_blko = nc.dram_tensor("blk", [128, 128], BF16, kind="ExternalInput").ap()
    d_hdo = nc.dram_tensor("hdo", [128, 128], BF16, kind="ExternalInput").ap()
    d_ones = nc.dram_tensor("on1", [1, 128], BF16, kind="ExternalInput").ap()
    d_cvec = nc.dram_tensor("cvc", [128, 52], F32, kind="ExternalInput").ap()

    A = lambda n, c, dt: nc.alloc_sbuf_tensor(n, [128, c], dt)
    resid = A("resid", NTC * 1024, F32)
    WPC = max(5 * 8192, 8 * 2048 + 3 * 8192 + 64 + 8)
    wpool = A("wpool", WPC, BF16)
    wps = wpool.ap().ap[0][0]

    def wview(off, cols):
        return AP(wpool, off, [[wps, 128], [1, cols]])

    wT = {nm: wview(i * 8192, 8192) for i, nm in enumerate("rkvwo")}
    OFF_SGU, OFF_W1, OFF_W2, OFF_SDN = 0, 8 * 2048, 8 * 2048 + 8192, 8 * 2048 + 2 * 8192
    OFF_RTR, OFF_SGW = 8 * 2048 + 3 * 8192, 8 * 2048 + 3 * 8192 + 64
    sgusb = wview(OFF_SGU, 8 * 2048)
    sdnsb = wview(OFF_SDN, 8192)
    rtrsb = wview(OFF_RTR, 64)
    sgwsb = wview(OFF_SGW, 8)
    R_SGU = ("wt_r", "wt_k")
    R_W1, R_W2, R_SDN = "wt_v", "wt_w", "wt_o"

    iden = A("iden", 128, BF16)
    umask8 = A("umask8", 128, mybir.dt.uint8)
    idenf = A("idenf", 128, F32)
    blko = A("blko", 128, BF16)
    hdo = A("hdo_sb", 128, BF16)
    ones1 = nc.alloc_sbuf_tensor("ones1", [1, 128], BF16)
    cvec = A("cvec", 52, F32)
    lnw64 = A("lnw64", 1, F32)

    xn_tok = A("xn_tok", 2048, BF16)
    xn_fm = A("xn_fm", 2048, BF16)
    xsh_fm = A("xsh_fm", 2048, BF16)
    mixb = A("mixb", 2048, BF16)
    tm1m = A("tm1m", 32, F32)
    r_bf = A("r_bf", 2048, BF16)
    k_bf = A("k_bf", 2048, BF16)
    eu_bf = A("eu_bf", 2048, BF16)
    sc_f32 = A("sc_f32", 2048, F32)
    sc2_f32 = A("sc2_f32", 2048, F32)
    chunk_f32 = A("chunk_f32", 2048, F32)
    expr_bf = A("expr_bf", 1024, BF16)
    rt_bf = A("rt_bf", 1024, BF16)
    kt_bf = A("kt_bf", 1024, BF16)
    kh_bf = A("kh_bf", 1024, BF16)
    khtok = A("khtok", 1024, BF16)
    t2_bf = A("t2_bf", 1024, BF16)
    vtok = A("vtok", 2048, BF16)
    mixtmp = A("mixtmp", 2048, BF16)
    vtil = A("vtil", 1024, BF16)
    mA_bf = A("mA_bf", 1024, BF16)
    mB_bf = A("mB_bf", 1024, BF16)
    S_bf = A("S_bf", 1024, BF16)
    eA_f32 = A("eA_f32", 48, F32)
    mcol_f32 = A("mcol_f32", 96, F32)
    D_sb = A("D_sb", 16, F32)
    o_fm = A("o_fm", 2048, BF16)
    og_fm = A("og_fm", 2048, BF16)
    prevc = A("prevc", 8, BF16)
    stat6 = A("stat6", 24, F32)
    mv = A("mv", 8, F32)
    o2_bf = expr_bf
    r1_bf = rt_bf
    b2_bf = kt_bf
    on_bf = kh_bf
    cmb_all = A("cmb_all", 64, BF16)
    gcol = A("gcol", 8, F32)
    lg_tok = A("lg_tok", 16, F32)
    esm = A("esm", 16, F32)
    top8 = A("top8", 16, F32)
    ssum = A("ssum", 4, F32)
    lgfm = nc.alloc_sbuf_tensor("lgfm", [8, 256], F32)

    pb = [nc.alloc_psum_tensor("pb%d" % i, [128, 512], F32) for i in range(8)]
    pbs = [p.ap().ap[0][0] for p in pb]
    PB = ["pb%d" % i for i in range(8)]

    def pbk(i, c0, n, parts=128):
        return AP(pb[i], c0, [[pbs[i], parts], [1, n]])

    def pbk_bf(i, c0, n, parts=128):
        return AP(pb[i], 0, [[pbs[i], parts], [1, 512]]).bitcast(BF16)[:, c0:c0 + n]

    tapd = {}
    tapn = [0]
    if tap:
        NCHUNK = nseq * T // CH
        _all = (("t_ofm", 1024, BF16), ("t_S", 1024, BF16),
                ("t_kt", 1024, BF16), ("t_mA", 1024, BF16),
                ("t_mB", 1024, BF16), ("t_og", 2048, BF16),
                ("t_var", 1024, F32))
        for nm, cols, dt in _all:
            if tapset is None or nm in tapset:
                tapd[nm] = nc.dram_tensor(nm, [NCHUNK, 128, cols], dt,
                                          kind="ExternalOutput").ap()

    P = Prog(nc, selfsync=race)
    pipe_sem = nc.alloc_semaphore(name="pipe")
    pipe_n = [0]

    def vguard(v, inst):
        if race:
            return
        pipe_n[0] += 1
        inst.then_inc(pipe_sem, 1)
        v.wait_ge(pipe_sem, pipe_n[0])

    def sl(t, c0, n):
        return t.ap()[:, c0:c0 + n]

    def ap3(t, blocks, bstep, inner, off=0):
        base = t.ap()
        return AP(t, off, [[base.ap[0][0], 128], [bstep, blocks], [1, inner]])

    def strided(t, off, step, count):
        base = t.ap()
        return AP(t, off, [[base.ap[0][0], 128], [step, count]])

    def fconsts(g):
        return [g.dma_start(out=iden.ap(), in_=d_iden),
                g.dma_start(out=umask8.ap(), in_=d_um8),
                g.dma_start(out=idenf.ap(), in_=d_idenf),
                g.dma_start(out=blko.ap(), in_=d_blko),
                g.dma_start(out=hdo.ap(), in_=d_hdo),
                g.dma_start(out=ones1.ap(), in_=d_ones),
                g.dma_start(out=cvec.ap(), in_=d_cvec)]
    P.dma("gpsimd", fconsts, n=7, writes=CONSTS)

    P.op("vector", lambda v: v.tensor_scalar_mul(lnw64.ap(), cvec.ap()[:, 40:41], 64.0),
         reads=["cvec"], writes=["lnw64"])
    P.op("vector", lambda v: v.tensor_scalar(tm1m.ap(), cvec.ap()[:, 0:32],
                                             -1.0, 1.0, ALU.mult, ALU.add),
         reads=["cvec"], writes=["tm1m"])

    def load_mat_1024(dst_off, src_ap, src_elem_off=0):
        def f(g):
            ia = AP(src_ap.tensor, src_elem_off, [[1024, 128], [131072, 8], [1, 1024]])
            oa = AP(wpool, dst_off, [[wps, 128], [1024, 8], [1, 1024]])
            return [g.dma_start(out=oa, in_=ia)]
        return f

    REGN = {0: "wt_r", 8192: "wt_k", 16384: "wt_v", 24576: "wt_w",
            32768: "wt_o"}

    def attn_offs(p):
        return {"r": 0 if p == 0 else 32768, "k": 8192, "v": 16384,
                "w": 24576, "o": 32768 if p == 0 else 0}

    def emit_attn_weight_loads(p):
        offs = attn_offs(p)
        for nm in "rkvwo":
            P.dma("gpsimd", load_mat_1024(offs[nm], d_wT[nm]),
                  writes=[REGN[offs[nm]]])

    def emit_ffn_shared_loads():
        def fsgu(g):
            ia = AP(d_sgu.tensor, 0, [[2048, 128], [262144, 8], [1, 2048]])
            oa = AP(wpool, OFF_SGU, [[wps, 128], [2048, 8], [1, 2048]])
            return [g.dma_start(out=oa, in_=ia)]
        P.dma("gpsimd", fsgu, writes=list(R_SGU))
        P.dma("gpsimd", load_mat_1024(OFF_SDN, d_sdn), writes=[R_SDN])

        def frtr(g):
            ia = AP(d_rtr.tensor, 0, [[8, 128], [1024, 8], [1, 8]])
            oa = AP(wpool, OFF_RTR, [[wps, 128], [8, 8], [1, 8]])
            return [g.dma_start(out=oa, in_=ia)]
        P.dma("gpsimd", frtr, writes=["rtr"])

        def fsgw(g):
            ia = AP(d_sgw.tensor, 0, [[1, 128], [128, 8], [1, 1]])
            oa = AP(wpool, OFF_SGW, [[wps, 128], [1, 8]])
            return [g.dma_start(out=oa, in_=ia, allow_slow_non_contiguous=True)]
        P.dma("gpsimd", fsgw, writes=["sgw"])

    def emit_ln_stats(src_ap, src_res, tag):
        s6 = stat6.ap()[:, tag * 12:tag * 12 + 12]
        m_ = lambda i: mv.ap()[:, tag * 4 + i:tag * 4 + i + 1]

        def fv(v):
            i1 = v.bn_stats(s6[:, 0:6], src_ap[:, 0:512])
            i2 = v.bn_stats(s6[:, 6:12], src_ap[:, 512:1024])
            if not race:
                i1.then_inc(pipe_sem, 1)
                i2.then_inc(pipe_sem, 1)
                pipe_n[0] += 2
            v.tensor_copy(m_(2), cvec.ap()[:, 40:41])
            if not race:
                v.wait_ge(pipe_sem, pipe_n[0])
            return v.bn_aggr(mv.ap()[:, tag * 4:tag * 4 + 2], s6)
        P.op("vector", fv, reads=[src_res, "cvec"],
             writes=["stat6:%d" % tag, "mv:%d" % tag])

        def fs(s):
            s.activation(m_(2), m_(1), AF.Ln, bias=cvec.ap()[:, 42:43], scale=1.0)
            return s.activation(m_(2), m_(2), AF.Exp, bias=0.0, scale=cvec.ap()[:, 46:47])
        P.op("scalar", fs, reads=["cvec"], writes=["mv:%d" % tag])

    def emit_ln_norm(src_ap, src_res, dst_ap, tag):
        m_ = lambda i: mv.ap()[:, tag * 4 + i:tag * 4 + i + 1]

        def fv(v):
            return v.tensor_scalar(dst_ap, src_ap, m_(0), m_(2),
                                   ALU.subtract, ALU.mult)
        P.op("vector", fv, reads=[src_res, "mv:%d" % tag], writes=["xn_tok"])

    def emit_ln(src_aps, src_ress):
        for tag in range(2):
            emit_ln_stats(src_aps[tag], src_ress[tag], tag)
        for tag in range(2):
            emit_ln_norm(src_aps[tag], src_ress[tag],
                         sl(xn_tok, tag * 1024, 1024), tag)

    def emit_xn_transposes():
        xst = xn_fm.ap().ap[0][0]
        tst = xn_tok.ap().ap[0][0]
        for tc in range(2):
            def ftr(sy, tc=tc):
                oa = AP(xn_fm, tc * CH, [[xst, 128], [STG, NF], [1, 128]])
                ia = AP(xn_tok, tc * 1024, [[tst, 128], [128, NF], [1, 128]])
                return [sy.dma_start(out=oa, in_=ia, transpose=True)]
            P.dma("sync", ftr, n=1, reads=["xn_tok"],
                  writes=["xfA", "xfB"])

    def emit_proj_fm(w_view, w_res, src_tile, src_res, banks, evac, n=STG):
        def f(te):
            last = None
            for fo in range(NF):
                outap = pbk(banks[fo // 2], (fo % 2) * n, n)
                for dc in range(NF):
                    last = te.matmul(
                        outap,
                        w_view[:, dc * 1024 + fo * 128: dc * 1024 + fo * 128 + 128],
                        sl(src_tile, dc * STG, n),
                        start=(dc == 0), stop=(dc == NF - 1))
            return last
        P.op("tensor", f, reads=[w_res, src_res],
             writes=[PB[b] for b in banks])
        evac(banks, n)

    def scan_banks(tc):
        pA = [0, 1, 2, 3] if tc % 2 == 0 else [4, 5, 6, 7]
        pB = [4, 5, 6, 7] if tc % 2 == 0 else [0, 1, 2, 3]
        return pA, pB, [PB[b_] for b_ in pA], [PB[b_] for b_ in pB]

    def scan_front(tc):
        mo = tc * 48
        eo = tc * 24
        RM, RE = "mcol:%d" % tc, "eA:%d" % tc
        mc = lambda a, b: mcol_f32.ap()[:, mo + a:mo + b]
        eA = eA_f32.ap()[:, eo:eo + 8]

        def fscan(v):
            last = None
            for f in range(NF):
                src = sl(sc_f32, f * STG + tc * CH, CH)
                last = v.tensor_tensor_scan(sl(chunk_f32, f * CH, CH), src, src,
                                            0.0, ALU.add, ALU.bypass)
            return last
        P.op("vector", fscan, reads=["sc_f32"], writes=["chunk_f32"])

        def fmid(v):
            v.tensor_copy(mc(0, 8), strided(chunk_f32, 63, CH, NF))
            v.tensor_scalar(mc(8, 16), strided(chunk_f32, 63, CH, NF),
                            -1.0, -30.0, ALU.mult, ALU.add)
            v.tensor_scalar_add(mc(32, 40), strided(chunk_f32, 63, CH, NF),
                                -30.0)
            return v.tensor_sub(eA, strided(chunk_f32, CH - 1, CH, NF),
                                strided(chunk_f32, 63, CH, NF))
        P.op("vector", fmid, reads=["chunk_f32"], writes=[RM, RE])

        def fmid2(s):
            eAk = eA_f32.ap()[:, eo + 8:eo + 16]
            eAm = eA_f32.ap()[:, eo + 16:eo + 24]
            s.activation(mc(16, 24), mc(0, 8), AF.Exp,
                         bias=cvec.ap()[:, 47:48], scale=1.0)
            s.activation(mc(24, 32), mc(0, 8), AF.Exp,
                         bias=cvec.ap()[:, 48:49], scale=cvec.ap()[:, 45:46])
            s.activation(eAk, eA, AF.Exp,
                         bias=cvec.ap()[:, 48:49], scale=cvec.ap()[:, 45:46])
            s.activation(eAm, eA, AF.Exp,
                         bias=cvec.ap()[:, 47:48], scale=cvec.ap()[:, 45:46])
            return s.activation(eA, eA, AF.Exp, bias=0.0,
                                scale=cvec.ap()[:, 45:46])
        P.op("scalar", fmid2, reads=["cvec"], writes=[RM, RE])

        def fexpr(s):
            last = None
            for f in range(NF):
                last = s.activation(sl(expr_bf, f * CH + 1, CH - 1),
                                    sl(chunk_f32, f * CH, CH - 1),
                                    AF.Exp, bias=mc(32 + f, 33 + f),
                                    scale=cvec.ap()[:, 45:46])
            return last
        P.op("scalar", fexpr, reads=["chunk_f32", RM, "cvec"], writes=["expr_bf"])
        P.op("vector", lambda v: v.tensor_copy(strided(expr_bf, 0, CH, NF),
                                               mc(16, 24)),
             reads=[RM], writes=["expr_bf"])

        def fek(s):
            last = None
            for f in range(NF):
                last = s.activation(sl(chunk_f32, 1024 + f * CH, CH),
                                    sl(chunk_f32, f * CH, CH),
                                    AF.Exp, bias=mc(8 + f, 9 + f), scale=1.0)
            return last
        P.op("scalar", fek, reads=[RM], writes=["chunk_f32"])

    def scan_main(tc):
        pA, pB, PA, PBk = scan_banks(tc)
        mo = tc * 48
        eo = tc * 24
        RM, RE = "mcol:%d" % tc, "eA:%d" % tc
        mc = lambda a, b: mcol_f32.ap()[:, mo + a:mo + b]
        eA = eA_f32.ap()[:, eo:eo + 8]

        def frt(v):
            rsl = ap3(r_bf, NF, STG, CH, off=tc * CH)
            v.tensor_mul(ap3(rt_bf, NF, CH, CH), rsl, ap3(expr_bf, NF, CH, CH))
            v.tensor_mul(ap3(kt_bf, NF, CH, CH), ap3(k_bf, NF, STG, CH, off=tc * CH),
                         ap3(chunk_f32, NF, CH, CH, off=1024))
            return v.tensor_mul(ap3(t2_bf, NF, CH, CH), rsl,
                                ap3(eu_bf, NF, STG, CH, off=tc * CH))
        P.op("vector", frt,
             reads=["r_bf", "expr_bf", "k_bf", "chunk_f32", "euA", "euB"],
             writes=["rt_bf", "kt_bf", "t2_bf"])

        def fkh(v):
            last = None
            for f in range(NF):
                last = v.tensor_scalar_mul(sl(kh_bf, f * CH, CH),
                                           sl(kt_bf, f * CH, CH),
                                           eA_f32.ap()[:, eo + 8 + f:eo + 9 + f])
            return last
        P.op("vector", fkh, reads=["kt_bf", RE], writes=["kh_bf"])

        for rnd in range(2):
            kb = pB if rnd == 0 else pA
            kbr = PBk if rnd == 0 else PA

            def fkhT(te, rnd=rnd, kb=kb):
                last = None
                for j in range(4):
                    last = te.transpose(
                        AP(pb[kb[j]], 0, [[pbs[kb[j]], 128], [1, 512]])
                        .bitcast(BF16)[:, 0:128],
                        sl(kh_bf, (rnd * 4 + j) * CH, CH), iden.ap())
                return last
            P.op("tensor", fkhT, reads=["kh_bf", "iden"], writes=kbr)

            def fkhE(v, rnd=rnd, kb=kb):
                last = None
                for j in range(4):
                    last = v.tensor_copy(
                        sl(khtok, (rnd * 4 + j) * 128, 128),
                        AP(pb[kb[j]], 0, [[pbs[kb[j]], 128], [1, 512]])
                        .bitcast(BF16)[:, 0:128])
                return last
            P.op("vector", fkhE, reads=kbr, writes=["khtok"])

        def fd(te):
            last = None
            for f in range(NF):
                last = te.matmul(pbk(pB[0], 0, 16), sl(t2_bf, f * CH, CH),
                                 hdo.ap()[:, f * 16:f * 16 + 16],
                                 start=(f == 0), stop=(f == NF - 1))
            return last
        P.op("tensor", fd, reads=["t2_bf", "hdo"], writes=[PBk[0]])
        P.op("vector", lambda v: v.tensor_copy(D_sb.ap(), pbk(pB[0], 0, 16)),
             reads=[PBk[0]], writes=["D_sb"])

        sprime = kh_bf
        sprime2 = t2_bf

        def fsprime(v):
            last = None
            for f in range(NF):
                last = v.tensor_scalar_mul(
                    sl(sprime, f * 128, 128), sl(S_bf, f * 128, 128),
                    mc(24 + f, 25 + f))
            return last
        P.op("vector", fsprime, reads=["S_bf", RM], writes=["kh_bf"])

        def fsprime2(v):
            last = None
            for f in range(NF):
                last = v.tensor_scalar_mul(
                    sl(sprime2, f * 128, 128), sl(sprime, f * 128, 128),
                    eA_f32.ap()[:, eo + 16 + f:eo + 17 + f])
            return last
        P.op("vector", fsprime2, reads=["kh_bf", RE], writes=["t2_bf"])

        def fpall(te):
            last = None
            for f in range(NF):
                te.matmul(pbk(pA[0] + f // 4, (f % 4) * 128, 128),
                          kt_bf.ap()[0:64, f * CH:f * CH + CH],
                          rt_bf.ap()[0:64, f * CH:f * CH + CH],
                          start=True, stop=True)
                last = te.matmul(pbk(pA[2] + f // 4, (f % 4) * 128, 128),
                                 kt_bf.ap()[64:128, f * CH:f * CH + CH],
                                 rt_bf.ap()[64:128, f * CH:f * CH + CH],
                                 start=True, stop=True)
            return last
        P.op("tensor", fpall, reads=["kt_bf", "rt_bf"], writes=PA)

        def fmaskall(v):
            last = None
            for f in range(NF):
                v.select(sl(mA_bf, f * 128, 128), umask8.ap(),
                         pbk(pA[0] + f // 4, (f % 4) * 128, 128), umask8.ap())
                last = v.select(sl(mB_bf, f * 128, 128), umask8.ap(),
                                pbk(pA[2] + f // 4, (f % 4) * 128, 128), umask8.ap())
            E60 = 1.1420073898156842e+26
            vguard(v, last)
            v.tensor_scalar_mul(mA_bf.ap(), mA_bf.ap(), E60)
            return v.tensor_scalar_mul(mB_bf.ap(), mB_bf.ap(), E60)
        P.op("vector", fmaskall, reads=["umask8"] + PA,
             writes=["mA", "mB"])

        def fvtil(s):
            last = None
            for f in range(NF):
                s.activation(sl(vtil, f * 128, 64),
                             sl(vtok, tc * 1024 + f * 128, 64),
                             AF.Copy, bias=0.0,
                             scale=D_sb.ap()[:, 2 * f:2 * f + 1])
                last = s.activation(sl(vtil, f * 128 + 64, 64),
                                    sl(vtok, tc * 1024 + f * 128 + 64, 64),
                                    AF.Copy, bias=0.0,
                                    scale=D_sb.ap()[:, 2 * f + 1:2 * f + 2])
            return last
        P.op("scalar", fvtil, reads=["vtok", "D_sb"], writes=["vtil"])

        def foall(te):
            last = None
            for f in range(NF):
                b_ = pB[0] + f // 4
                c0 = (f % 4) * 128
                oo = pbk(b_, c0, 128)
                ooA = AP(pb[b_], c0, [[pbs[b_], 64], [1, 128]])
                ooB = AP(pb[b_], 64 * pbs[b_] + c0, [[pbs[b_], 64], [1, 128]])
                te.matmul(oo, sl(sprime, f * 128, 128),
                          rt_bf.ap()[:, f * CH:f * CH + CH], start=True, stop=False)
                te.matmul(ooA, sl(vtok, tc * 1024 + f * 128, 64),
                          sl(mA_bf, f * 128, 128), start=False, stop=False)
                te.matmul(ooB, sl(vtok, tc * 1024 + f * 128 + 64, 64),
                          sl(mB_bf, f * 128, 128), start=False, stop=False)
                te.matmul(oo, sl(vtil, f * 128, 128), iden.ap(),
                          start=False, stop=True)
                last = te.matmul(pbk(pB[2] + f // 4, (f % 4) * 128, 128),
                                 sl(khtok, f * 128, 128),
                                 sl(vtok, tc * 1024 + f * 128, 128),
                                 start=True, stop=True)
            return last
        P.op("tensor", foall,
             reads=["kh_bf", "rt_bf", "vtil", "iden", "mA",
                    "mB", "khtok", "vtok"],
             writes=PBk)

        def fseC(s):
            last = None
            for f in range(NF):
                last = s.activation(sl(o_fm, f * STG + tc * CH, CH),
                                    pbk(pB[0] + f // 4, (f % 4) * 128, 128),
                                    AF.Copy, bias=0.0, scale=1.0)
            return last
        P.op("scalar", fseC, reads=PBk[0:2], writes=["o_fm:%d" % tc])

        def fseall(v):
            sst = S_bf.ap().ap[0][0]
            s2t = t2_bf.ap().ap[0][0]
            last = None
            for half in range(2):
                for bi in range(2):
                    b_ = pB[2] + bi
                    f0 = bi * 4
                    co = half * 64
                    po = half * 64 * pbs[b_] + co
                    last = v.tensor_add(
                        AP(S_bf, half * 64 * sst + f0 * 128 + co,
                           [[sst, 64], [128, 4], [1, 64]]),
                        AP(t2_bf, half * 64 * s2t + f0 * 128 + co,
                           [[s2t, 64], [128, 4], [1, 64]]),
                        AP(pb[b_], po, [[pbs[b_], 64], [128, 4], [1, 64]]))
            return last
        P.op("vector", fseall, reads=PBk[2:4] + ["t2_bf"],
             writes=["S_bf"])

    def scan_gn(tc, alt=False):
        pA, pB, PA, PBk = scan_banks(tc)
        gns = mixb.ap().bitcast(F32)[:, 0:1024]
        if alt:
            r1v, b2v = rt_bf.ap(), kt_bf.ap()
            RR = ["rt_bf", "kt_bf"]
        else:
            scb = sc_f32.ap().bitcast(BF16)
            r1v, b2v = scb[:, 0:1024], scb[:, 1024:2048]
            RR = ["sc_f32"]
        RO = "o_fm:%d" % tc

        def fo2(s):
            return s.activation(ap3(mB_bf, NF, CH, CH),
                                ap3(o_fm, NF, STG, CH, off=tc * CH),
                                AF.Square, bias=0.0, scale=1.0)
        P.op("scalar", fo2, reads=[RO], writes=["mB"])

        def fgst(te):
            last = None
            for f in range(NF):
                bank = pA[0] + f // 4
                last = te.matmul(pbk(bank, (f % 4) * 128, 128), blko.ap(),
                                 sl(o_fm, f * STG + tc * CH, CH),
                                 start=(f % 4 == 0), stop=(f % 4 == 3))
            for f in range(NF):
                bank = pA[2] + f // 4
                last = te.matmul(pbk(bank, (f % 4) * 128, 128), blko.ap(),
                                 sl(mB_bf, f * CH, CH),
                                 start=(f % 4 == 0), stop=(f % 4 == 3))
            return last
        P.op("tensor", fgst, reads=["blko", RO, "mB"], writes=PA)

        def fgev(v):
            v.tensor_copy(sl(sc2_f32, 0, 512), pbk(pA[0], 0, 512))
            v.tensor_copy(sl(sc2_f32, 512, 512), pbk(pA[1], 0, 512))
            v.tensor_copy(sl(sc2_f32, 1024, 512), pbk(pA[2], 0, 512))
            return v.tensor_copy(sl(sc2_f32, 1536, 512), pbk(pA[3], 0, 512))
        P.op("vector", fgev, reads=PA, writes=["sc2_f32"])

        def fgsq(v):
            v.tensor_mul(gns, sl(sc2_f32, 0, 1024), sl(sc2_f32, 0, 1024))
            v.scalar_tensor_tensor(sl(sc2_f32, 1024, 1024),
                                   sl(sc2_f32, 1024, 1024), 64.0, gns,
                                   ALU.mult, ALU.subtract)
            return v.tensor_single_scalar(sl(sc2_f32, 1024, 1024),
                                          sl(sc2_f32, 1024, 1024), 0.0, ALU.max)
        P.op("vector", fgsq, reads=["mixb"], writes=["sc2_f32", "mixb"])

        def fgln(s):
            s.activation(sl(sc2_f32, 1024, 1024), sl(sc2_f32, 1024, 1024), AF.Ln,
                         bias=cvec.ap()[:, 43:44], scale=1.0)
            return s.activation(sl(sc2_f32, 1024, 1024), sl(sc2_f32, 1024, 1024),
                                AF.Exp, bias=0.0, scale=cvec.ap()[:, 46:47])
        P.op("scalar", fgln, reads=["cvec"], writes=["sc2_f32"])

        def fr1(v):
            v.tensor_scalar_mul(r1v, sl(sc2_f32, 1024, 1024),
                                lnw64.ap()[:, 0:1])
            v.tensor_mul(gns, sl(sc2_f32, 0, 1024), r1v)
            return v.tensor_scalar(b2v, gns, -1.0 / 64.0,
                                   cvec.ap()[:, 41:42], ALU.mult, ALU.add)
        P.op("vector", fr1, reads=["lnw64", "sc2_f32", "cvec"],
             writes=RR + ["mixb"])

        def fon(v):
            og = ap3(og_fm, NF, STG, CH, off=tc * CH)
            v.tensor_mul(og, ap3(o_fm, NF, STG, CH, off=tc * CH), r1v)
            v.tensor_add(og, og, b2v)
            return v.tensor_mul(og, og,
                                ap3(r_bf, NF, STG, CH, off=tc * CH))
        P.op("vector", fon, reads=[RO, "r_bf"] + RR,
             writes=["og_fm"])

        if tap:
            ci = tapn[0]
            tapn[0] += 1

            _tt = [(nm, t_, cols, rr) for nm, t_, cols, rr in
                   (("t_ofm", o_fm, 1024, "o_fm:%d" % tc),
                    ("t_S", S_bf, 1024, "S_bf"),
                    ("t_kt", kt_bf, 1024, "kt_bf"),
                    ("t_mA", mA_bf, 1024, "mA"),
                    ("t_mB", mB_bf, 1024, "mB"),
                    ("t_og", og_fm, 2048, "og_fm"),
                    ("t_var", sc2_f32, 1024, "sc2_f32")) if nm in tapd]

            def ftap(g, tc=tc, ci=ci, _tt=_tt):
                outs = []
                for nm, t_, cols, _ in _tt:
                    nel = 128 * cols
                    ia = (t_.ap()[:, 0:cols] if nm != "t_var"
                          else t_.ap()[:, 1024:2048])
                    if nm == "t_ofm":
                        ia = ap3(t_, NF, STG, CH, off=tc * CH)
                    oa = AP(tapd[nm].tensor, ci * nel,
                            [[cols, 128], [1, cols]])
                    outs.append(g.dma_start(out=oa, in_=ia))
                return outs
            P.dma("gpsimd", ftap, n=len(_tt),
                  reads=[rr for _, _, _, rr in _tt],
                  writes=["tap%d" % ci])
    def emit_front(b, h, st):
        t0 = h * half_tokens
        st_t0 = t0 + st * STG
        base_tc = st * 2
        RC = ["resid:%d" % (base_tc + tc) for tc in range(2)]

        def fx(g, st_t0=st_t0, base_tc=base_tc, b=b):
            ia = AP(x_in.tensor, b * T * D + st_t0 * D,
                    [[1024, 128], [131072, 2], [1, 1024]])
            oa = AP(resid, base_tc * 1024,
                    [[resid.ap().ap[0][0], 128], [1024, 2], [1, 1024]])
            return [g.dma_start(out=oa, in_=ia)]
        P.dma("gpsimd", fx, writes=RC)

        emit_ln([sl(resid, (base_tc + tc) * 1024, 1024) for tc in range(2)], RC)
        emit_xn_transposes()
        if h == 0 and st == 0:
            P.op("vector", lambda v: v.memset(prevc.ap(), 0.0), writes=["prevc"])

        def fsh2(v):
            v.tensor_copy(ap3(xsh_fm, NF, STG, STG - 1, off=1),
                          ap3(xn_fm, NF, STG, STG - 1, off=0))
            v.tensor_copy(strided(xsh_fm, 0, STG, NF), prevc.ap())
            v.tensor_copy(prevc.ap(), strided(xn_fm, STG - 1, STG, NF))
            return v.tensor_sub(mixtmp.ap(), xn_fm.ap(), xsh_fm.ap())
        P.op("vector", fsh2, reads=["xfA", "xfB"],
             writes=["xsh_fm", "prevc", "mixtmp"])

    def phase_A(b, h, first=False, p=0):
        t0 = h * half_tokens
        offs = attn_offs(p)
        wtv = {nm: wview(offs[nm], 8192) for nm in "rkvwo"}
        rres = {nm: REGN[offs[nm]] for nm in "rkvwo"}
        emit_attn_weight_loads(p)
        if h == 0:
            P.op("vector", lambda v: v.memset(S_bf.ap(), 0.0), writes=["S_bf"])

        if first:
            emit_front(b, h, 0)
        for st in range(NSTG):
            base_tc = st * 2
            RC = ["resid:%d" % (base_tc + tc) for tc in range(2)]

            cvst = cvec.ap().ap[0][0]

            def emit_mix(coff, dst, dstres):
                def fmix(v):
                    last = None
                    for f in range(NF):
                        last = v.scalar_tensor_tensor(
                            sl(dst, f * STG, STG), sl(mixtmp, f * STG, STG),
                            cvec.ap()[:, coff + f:coff + f + 1],
                            sl(xsh_fm, f * STG, STG), ALU.mult, ALU.add)
                    return last
                P.op("vector", fmix, reads=["mixtmp", "cvec", "xsh_fm"],
                     writes=[dstres])

            def evac_r(banks, n):
                def fs(s):
                    last = None
                    for fo in range(NF):
                        last = s.activation(sl(r_bf, fo * STG, n),
                                            pbk(banks[fo // 2], (fo % 2) * n, n),
                                            AF.Sigmoid, bias=0.0, scale=1.0)
                    return last
                P.op("scalar", fs, reads=[PB[b_] for b_ in banks], writes=["r_bf"])
            emit_mix(0, mixb, "mixb")
            emit_proj_fm(wtv["r"], rres["r"], mixb, "mixb", [0, 1, 2, 3], evac_r)

            def evac_k(banks, n):
                def fv(v):
                    last = None
                    for fo in range(NF):
                        last = v.tensor_copy(sl(k_bf, fo * STG, n),
                                             pbk(banks[fo // 2], (fo % 2) * n, n))
                    return last
                P.op("vector", fv, reads=[PB[b_] for b_ in banks], writes=["k_bf"])

                def fs(s):
                    last = None
                    for fo in range(NF):
                        last = s.activation(sl(eu_bf, fo * STG, n),
                                            pbk(banks[fo // 2], (fo % 2) * n, n),
                                            AF.Exp,
                                            bias=cvec.ap()[:, 32 + fo:32 + fo + 1],
                                            scale=1.0)
                    return last
                P.op("scalar", fs, reads=[PB[b_] for b_ in banks] + ["cvec"],
                     writes=["euA", "euB"])
            emit_mix(8, og_fm, "og_fm")
            emit_proj_fm(wtv["k"], rres["k"], og_fm, "og_fm", [4, 5, 6, 7], evac_k)

            def evac_w(banks, n):
                def fs(s):
                    last = None
                    for fo in range(NF):
                        last = s.activation(sl(sc_f32, fo * n, n),
                                            pbk(banks[fo // 2], (fo % 2) * n, n),
                                            AF.Exp, bias=0.0, scale=cvec.ap()[:, 45:46])
                    return last
                P.op("scalar", fs, reads=[PB[b_] for b_ in banks] + ["cvec"],
                     writes=["sc_f32"])
                P.op("scalar", lambda s: s.activation(
                    sc_f32.ap(), sc_f32.ap(), AF.Ln, bias=cvec.ap()[:, 44:45],
                    scale=1.0), reads=["cvec"], writes=["sc_f32"])
                P.op("vector", lambda v: v.tensor_scalar_add(
                    sc_f32.ap(), sc_f32.ap(), 1e-4), writes=["sc_f32"])
            emit_mix(24, mixb, "mixb")
            emit_proj_fm(wtv["w"], rres["w"], mixb, "mixb", [0, 1, 2, 3], evac_w)

            emit_mix(16, og_fm, "og_fm")

            def fvproj(te):
                last = None
                for tc in range(2):
                    for hlf in range(2):
                        outap = pbk(4 + tc * 2 + hlf, 0, 512)
                        for dc in range(NF):
                            last = te.matmul(
                                outap, sl(og_fm, dc * STG + tc * CH, 128),
                                wtv["v"][:, dc * 1024 + hlf * 512: dc * 1024 + hlf * 512 + 512],
                                start=(dc == 0), stop=(dc == NF - 1))
                return last
            P.op("tensor", fvproj, reads=["og_fm", rres["v"]],
                 writes=[PB[4], PB[5], PB[6], PB[7]])

            def fvevac(s):
                last = None
                for tc in range(2):
                    for hlf in range(2):
                        last = s.activation(sl(vtok, tc * 1024 + hlf * 512, 512),
                                            pbk(4 + tc * 2 + hlf, 0, 512),
                                            AF.Copy, bias=0.0, scale=1.0)
                return last
            P.op("scalar", fvevac, reads=[PB[4], PB[5], PB[6], PB[7]],
                 writes=["vtok"])

            scan_front(0)
            scan_main(0)
            scan_front(1)
            scan_gn(0)
            scan_main(1)
            if st + 1 < NSTG:
                emit_front(b, h, st + 1)
            scan_gn(1, alt=(st == NSTG - 1))

            def foproj(te):
                last = None
                for tc in range(2):
                    for hlf in range(2):
                        outap = pbk(tc * 2 + hlf, 0, 512)
                        for dc in range(NF):
                            last = te.matmul(
                                outap,
                                sl(og_fm, dc * STG + tc * CH, 128),
                                wtv["o"][:, dc * 1024 + hlf * 512: dc * 1024 + hlf * 512 + 512],
                                start=(dc == 0), stop=(dc == NF - 1))
                return last
            P.op("tensor", foproj, reads=[rres["o"], "og_fm"], writes=PB[0:4])

            def fradd(v, base_tc=base_tc):
                last = None
                for tc in range(2):
                    for hlf in range(2):
                        c0 = (base_tc + tc) * 1024 + hlf * 512
                        last = v.tensor_add(sl(resid, c0, 512), sl(resid, c0, 512),
                                            pbk(tc * 2 + hlf, 0, 512))
                return last
            P.op("vector", fradd, reads=PB[0:4] + RC, writes=RC)

    XN2H = [(sc_f32, 0), (sc_f32, 2048), (chunk_f32, 0), (chunk_f32, 2048)]
    XN2R = ["sc_f32", "sc_f32", "chunk_f32", "chunk_f32"]

    def xn2v(st):
        t, c0 = XN2H[st]
        a = t.ap() if t.ap().dtype == BF16 else t.ap().bitcast(BF16)
        return a[:, c0:c0 + 2048]

    def expert_slots(e):
        if e % 2 == 0:
            return OFF_W1, OFF_W2, R_W1, R_W2
        return 0, 8192, "wt_r", "wt_k"

    def emit_expert_load(e):
        off1, off2, r1, r2 = expert_slots(e)
        P.dma("gpsimd", load_mat_1024(off1, d_w1, src_elem_off=e * D * D),
              writes=[r1])
        P.dma("gpsimd", load_mat_1024(off2, d_w2, src_elem_off=e * D * D),
              writes=[r2])

    def phase_B_router(st):
        xn2s = xn2v(st)

        def frt(te):
            last = None
            for tc2 in range(2):
                for dc in range(NF):
                    last = te.matmul(
                        pbk(1 + tc2, 0, 8),
                        xn2s[:, dc * STG + tc2 * CH:dc * STG + tc2 * CH + 128],
                        rtrsb[:, dc * 8:dc * 8 + 8],
                        start=(dc == 0), stop=(dc == NF - 1))
            return last
        P.op("tensor", frt, reads=["rtr", XN2R[st]], writes=[PB[1], PB[2]])

        def frte(v):
            v.tensor_copy(lg_tok.ap()[:, 0:8], pbk(1, 0, 8))
            return v.tensor_copy(lg_tok.ap()[:, 8:16], pbk(2, 0, 8))
        P.op("vector", frte, reads=[PB[1], PB[2]], writes=["lg_tok"])

        def fmax(v):
            v.reduce_max(ssum.ap()[:, 0:1], lg_tok.ap()[:, 0:8], AXX, negate=True)
            return v.reduce_max(ssum.ap()[:, 2:3], lg_tok.ap()[:, 8:16], AXX,
                                negate=True)
        P.op("vector", fmax, reads=["lg_tok"], writes=["ssum"])

        def fex(s):
            s.activation(esm.ap()[:, 0:8], lg_tok.ap()[:, 0:8], AF.Exp,
                         bias=ssum.ap()[:, 0:1], scale=1.0,
                         accum_out=ssum.ap()[:, 1:2])
            return s.activation(esm.ap()[:, 8:16], lg_tok.ap()[:, 8:16], AF.Exp,
                                bias=ssum.ap()[:, 2:3], scale=1.0,
                                accum_out=ssum.ap()[:, 3:4])
        P.op("scalar", fex, reads=["lg_tok"], writes=["esm", "ssum"])

        def ftp(v, st=st):
            ea = esm.ap()[:, 0:8]
            eb = esm.ap()[:, 8:16]
            ta = top8.ap()[:, 0:8]
            tb = top8.ap()[:, 8:16]
            ga = lg_tok.ap()[:, 0:8]
            gb = lg_tok.ap()[:, 8:16]
            v.max(ta, ea)
            vguard(v, v.max(tb, eb))
            v.reciprocal(ssum.ap()[:, 0:1], ssum.ap()[:, 1:2])
            v.reciprocal(ssum.ap()[:, 2:3], ssum.ap()[:, 3:4])
            v.tensor_single_scalar(ga, ea, top8.ap()[:, 1:2], ALU.is_ge)
            vguard(v, v.tensor_single_scalar(gb, eb, top8.ap()[:, 9:10],
                                             ALU.is_ge))
            v.tensor_mul(ga, ga, ea)
            vguard(v, v.tensor_mul(gb, gb, eb))
            v.tensor_scalar_mul(cmb_all.ap()[:, st * 16:st * 16 + 8], ga,
                                ssum.ap()[:, 0:1])
            return v.tensor_scalar_mul(cmb_all.ap()[:, st * 16 + 8:st * 16 + 16],
                                       gb, ssum.ap()[:, 2:3])
        P.op("vector", ftp, reads=["esm"],
             writes=["lg_tok", "top8", "ssum", "cmb_all"])

    def phase_B_front(b, h, st):
        base_tc = st * 2
        RC = ["resid:%d" % (base_tc + tc) for tc in range(2)]
        emit_ln([sl(resid, (base_tc + tc) * 1024, 1024) for tc in range(2)], RC)
        host, c0 = XN2H[st]
        hst = host.ap().ap[0][0]
        tst = xn_tok.ap().ap[0][0]
        for tc in range(2):
            def ftr(sy, tc=tc):
                oa = AP(host, c0 // 2 + tc * 64,
                        [[hst, 128], [128, NF], [1, 64]]).bitcast(BF16)
                ia = AP(xn_tok, tc * 1024, [[tst, 128], [128, NF], [1, 128]])
                return [sy.dma_start(out=oa, in_=ia, transpose=True)]
            P.dma("sync", ftr, n=1, reads=["xn_tok"], writes=[XN2R[st]])

    def phase_B_stage(b, h, st):
        base_tc = st * 2
        RC = ["resid:%d" % (base_tc + tc) for tc in range(2)]
        xn2s = xn2v(st)

        def fgc(te):
            last = None
            for tc2 in range(2):
                for dc in range(NF):
                    last = te.matmul(
                        pbk(5, tc2 * 256, 1),
                        xn2s[:, dc * STG + tc2 * CH:dc * STG + tc2 * CH + 128],
                        sgwsb[:, dc:dc + 1],
                        start=(dc == 0), stop=(dc == NF - 1))
            return last
        P.op("tensor", fgc, reads=["sgw", XN2R[st]], writes=[PB[5]])

        def fgsig(s, st=st):
            s.activation(gcol.ap()[:, st * 2:st * 2 + 1], pbk(5, 0, 1),
                         AF.Sigmoid, bias=0.0, scale=1.0)
            return s.activation(gcol.ap()[:, st * 2 + 1:st * 2 + 2],
                                pbk(5, 256, 1), AF.Sigmoid, bias=0.0, scale=1.0)
        P.op("scalar", fgsig, reads=[PB[5]], writes=["gcol"])

        hb = eu_bf if st % 2 == 0 else xn_fm
        hres = ["euA", "euB"] if st % 2 == 0 else ["xfA", "xfB"]
        tb, tbr = (mixb, "mixb") if st % 2 == 0 else (og_fm, "og_fm")

        for half in range(2):
            def fguh(te, half=half, xn2s=xn2s):
                last = None
                for f2 in range(half * 8, half * 8 + 8):
                    outap = pbk(f2 // 2, (f2 % 2) * 256, 256)
                    for dc in range(NF):
                        last = te.matmul(
                            outap,
                            sgusb[:, dc * 2048 + f2 * 128: dc * 2048 + f2 * 128 + 128],
                            xn2s[:, dc * STG:dc * STG + STG],
                            start=(f2 % 2 == 0 and dc == 0),
                            stop=(f2 % 2 == 1 and dc == NF - 1))
                return last
            P.op("tensor", fguh, reads=list(R_SGU) + [XN2R[st]],
                 writes=PB[half * 4:half * 4 + 4])

        def fsg(s):
            last = None
            for f in range(NF):
                last = s.activation(sl(hb, f * STG, STG),
                                    pbk(f // 2, (f % 2) * 256, 256),
                                    AF.Silu, bias=0.0, scale=1.0)
            return last
        P.op("scalar", fsg, reads=PB[0:4], writes=hres)

        def fup(v):
            last = None
            for f in range(NF):
                last = v.tensor_mul(sl(tb, f * STG, STG), sl(hb, f * STG, STG),
                                    pbk(4 + f // 2, (f % 2) * 256, 256))
            return last
        P.op("vector", fup, reads=hres + PB[4:8], writes=[tbr])

        def fsdn(te):
            last = None
            for tc2 in range(2):
                for hlf in range(2):
                    outap = pbk(tc2 * 2 + hlf, 0, 512)
                    for fc in range(NF):
                        last = te.matmul(
                            outap, sl(tb, fc * STG + tc2 * CH, 128),
                            sdnsb[:, fc * 1024 + hlf * 512: fc * 1024 + hlf * 512 + 512],
                            start=(fc == 0), stop=(fc == NF - 1))
            return last
        P.op("tensor", fsdn, reads=[R_SDN, tbr], writes=PB[0:4])

        def fsadd(v, st=st):
            last = None
            for tc2 in range(2):
                for hlf in range(2):
                    c0 = (base_tc + tc2) * 1024 + hlf * 512
                    last = v.scalar_tensor_tensor(
                        sl(resid, c0, 512), pbk(tc2 * 2 + hlf, 0, 512),
                        gcol.ap()[:, st * 2 + tc2:st * 2 + tc2 + 1],
                        sl(resid, c0, 512), ALU.mult, ALU.add)
            return last
        P.op("vector", fsadd, reads=PB[0:4] + RC + ["gcol"], writes=RC)

    def phase_B_expert(e, b=0, h=0, nxt=None):
        off1, off2, r1, r2 = expert_slots(e)
        final = (e == E - 1)
        w1v = wview(off1, 8192)
        w2v = wview(off2, 8192)
        for st in range(NSTG):
            xn2s = xn2v(st)
            base_tc = st * 2
            RC = ["resid:%d" % (base_tc + tc) for tc in range(2)]
            hb = eu_bf if st % 2 == 0 else xn_fm
            hres = ["euA", "euB"] if st % 2 == 0 else ["xfA", "xfB"]

            for half in range(2):
                def fw1h(te, half=half, xn2s=xn2s, w1v=w1v):
                    last = None
                    for fc in range(half * 4, half * 4 + 4):
                        outap = pbk(4 + fc // 2, (fc % 2) * 256, 256)
                        for dc in range(NF):
                            last = te.matmul(
                                outap,
                                w1v[:, dc * 1024 + fc * 128: dc * 1024 + fc * 128 + 128],
                                xn2s[:, dc * STG:dc * STG + STG],
                                start=(fc % 2 == 0 and dc == 0),
                                stop=(fc % 2 == 1 and dc == NF - 1))
                    return last
                P.op("tensor", fw1h, reads=[r1, XN2R[st]],
                     writes=PB[4 + 2 * half:6 + 2 * half])

                def fsih(s, half=half, hb=hb):
                    last = None
                    for fc in range(half * 4, half * 4 + 4):
                        last = s.activation(sl(hb, fc * STG, STG),
                                            pbk(4 + fc // 2, (fc % 2) * 256, 256),
                                            AF.Silu, bias=0.0, scale=1.0)
                    return last
                P.op("scalar", fsih, reads=PB[4 + 2 * half:6 + 2 * half],
                     writes=[hres[half]])

            for half in range(2):
                def fw2p(te, half=half, w2v=w2v, hb=hb):
                    last = None
                    for tc2 in range(2):
                        for hlf in range(2):
                            outap = pbk(tc2 * 2 + hlf, 0, 512)
                            for fc in range(half * 4, half * 4 + 4):
                                last = te.matmul(
                                    outap, sl(hb, fc * STG + tc2 * CH, 128),
                                    w2v[:, fc * 1024 + hlf * 512: fc * 1024 + hlf * 512 + 512],
                                    start=(half == 0 and fc == 0),
                                    stop=(half == 1 and fc == NF - 1))
                    return last
                P.op("tensor", fw2p, reads=[r2, hres[half]], writes=PB[0:4])

            def fadd(v, base_tc=base_tc, e=e, st=st):
                last = None
                for tc2 in range(2):
                    for hlf in range(2):
                        c0 = (base_tc + tc2) * 1024 + hlf * 512
                        last = v.scalar_tensor_tensor(
                            sl(resid, c0, 512), pbk(tc2 * 2 + hlf, 0, 512),
                            cmb_all.ap()[:, st * 16 + tc2 * 8 + e:
                                         st * 16 + tc2 * 8 + e + 1],
                            sl(resid, c0, 512), ALU.mult, ALU.add)
                return last
            P.op("vector", fadd, reads=PB[0:4] + RC + ["cmb_all"], writes=RC)

            if final:
                t0 = h * half_tokens

                def fst(g, st=st, t0=t0, b=b):
                    st_t0 = t0 + st * STG
                    oa = AP(y_out.tensor, b * T * D + st_t0 * D,
                            [[1024, 128], [131072, 2], [1, 1024]])
                    ia = AP(resid, st * 2 * 1024,
                            [[resid.ap().ap[0][0], 128], [1024, 2], [1, 1024]])
                    return [g.dma_start(out=oa, in_=ia)]
                P.dma("gpsimd", fst, reads=RC,
                      writes=["y:%d:%d:%d" % (b, h, st)])
                if st == 0 and nxt is not None:
                    emit_front(nxt[0], nxt[1], 0)

    def phase_B(b, h, nxt=None):
        emit_ffn_shared_loads()
        emit_expert_load(0)
        for st in range(NSTG):
            phase_B_front(b, h, st)
        for st in range(NSTG):
            phase_B_stage(b, h, st)
        for st in range(NSTG):
            phase_B_router(st)
        for e in range(E):
            if e + 1 < E:
                emit_expert_load(e + 1)
            phase_B_expert(e, b=b, h=h, nxt=nxt)

    bh = [(b, h) for b in range(nseq) for h in range(NH)]
    for i, (b, h) in enumerate(bh):
        phase_A(b, h, first=(i == 0), p=i % 2)
        phase_B(b, h, nxt=(bh[i + 1] if i + 1 < len(bh) else None))

    P.finalize()
    return nc


def make_host_inputs(inp):
    tT = lambda a: np.ascontiguousarray(np.asarray(a).T)
    cN = np.ascontiguousarray
    iden = np.eye(128, dtype=BF)
    umask = np.triu(np.ones((128, 128), dtype=np.float32), 1)
    blko = np.zeros((128, 128), dtype=np.float32)
    blko[0:64, 0:64] = 1.0
    blko[64:128, 64:128] = 1.0
    hdo = np.zeros((128, 128), dtype=np.float32)
    for f in range(8):
        hdo[0:64, f * 16 + 2 * f] = 1.0
        hdo[64:128, f * 16 + 2 * f + 1] = 1.0
    cvec = np.zeros((128, 52), dtype=np.float32)
    cvec[:, 47] = -30.0
    cvec[:, 48] = 30.0
    for i, nm in enumerate(("tm_r", "tm_k", "tm_v", "tm_w")):
        cvec[:, i * 8:(i + 1) * 8] = np.asarray(inp["L0_attn_" + nm]).reshape(8, 128).T
    cvec[:, 32:40] = np.asarray(inp["L0_attn_u"]).reshape(-1).reshape(8, 128).T
    cvec[:, 40] = np.tile(np.asarray(inp["L0_attn_ln_w"]), 2)
    cvec[:, 41] = np.tile(np.asarray(inp["L0_attn_ln_b"]), 2)
    cvec[:, 42] = 1e-5
    cvec[:, 43] = 4096.0 * 1e-5
    cvec[:, 44] = 1.0
    cvec[:, 45] = -1.0
    cvec[:, 46] = -0.5
    shared = {
        "wrT": tT(inp["L0_attn_W_r"]), "wkT": tT(inp["L0_attn_W_k"]),
        "wvT": tT(inp["L0_attn_W_v"]), "wwT": tT(inp["L0_attn_W_w"]),
        "woT": tT(inp["L0_attn_W_o"]),
        "w1": cN(np.asarray(inp["L0_ffn_w1"])), "w2": cN(np.asarray(inp["L0_ffn_w2"])),
        "sgu": cN(np.asarray(inp["L0_ffn_shared_gate_up"])),
        "sdn": cN(np.asarray(inp["L0_ffn_shared_down"])),
        "rtr": cN(np.asarray(inp["L0_ffn_router_w"])),
        "sgw": cN(np.asarray(inp["L0_ffn_shared_gate_w"])),
        "idn": iden, "um8": umask.astype(np.uint8),
        "idf": np.eye(128, dtype=np.float32), "blk": blko.astype(BF), "hdo": hdo.astype(BF),
        "on1": np.ones((1, 128), dtype=BF), "cvc": cvec,
    }
    x = np.asarray(inp["L0_attn_x"])
    xs = [cN(x[c * 2:(c + 1) * 2]) for c in range(NCORE)]
    return shared, xs


def kernel(**inputs):
    shared, xs = make_host_inputs(inputs)
    nc = build_nc()
    in_maps = [dict(shared, x=xs[c]) for c in range(NCORE)]
    res = run_bass_kernel_spmd(nc, in_maps, core_ids=list(range(NCORE)))
    out = np.concatenate([res.results[c]["y"] for c in range(NCORE)], axis=0)
    return np.ascontiguousarray(out.astype(np.float32))
